# version 23 (final)
# baseline (speedup 1.0000x reference)
import numpy as np
import concourse.bass as bass
import concourse.mybir as mybir

F32 = mybir.dt.float32
F32R = mybir.dt.float32r
BF16 = mybir.dt.bfloat16
AF = mybir.ActivationFunctionType
ALU = mybir.AluOpType

H = 16
N = 64
C = 1024
L = 64
NCC = 8


class Sched:

    ENGINES = ("tensor", "vector", "scalar", "gpsimd", "sync")

    def __init__(self, nc):
        self.nc = nc
        self.q = {e: [] for e in self.ENGINES}
        self.psem = {e: nc.alloc_semaphore(name=f"ph_{e}") for e in self.ENGINES}
        self.pcount = {e: 0 for e in self.ENGINES}
        self.dirty = set()
        self.dsem = {}
        self.dtot = {}
        self.dpend = {}
        self.seen = {e: {} for e in self.ENGINES}

    def dma_sem(self, name):
        if name not in self.dsem:
            self.dsem[name] = self.nc.alloc_semaphore(name=f"dma_{name}")
            self.dtot[name] = 0
        return self.dsem[name]

    def on(self, eng, fn):
        self.q[eng].append(("op", fn))
        self.dirty.add(eng)

    def sep(self, e, n=3):
        for k in range(n):
            i = self._sepi = (getattr(self, "_sepi", 0) + 1) % 4
            _cp(e, self.dummy[0:1, i : i + 1], self.dummy[0:1, 4 + i : 5 + i])

    def dma(self, eng, semname, out, in_, defer=False, **kw):
        sem = self.dma_sem(semname)
        if defer:
            self.dpend[semname] = self.dpend.get(semname, 0) + 16
        else:
            self.dtot[semname] += 16

        def fn(e, out=out, in_=in_, sem=sem, kw=kw):
            e.dma_start(out, in_, **kw).then_inc(sem, 16)

        self.q[eng].append(("op", fn))
        self.dirty.add(eng)

    def dma_commit(self, semname):
        self.dtot[semname] += self.dpend.pop(semname, 0)

    def barrier(self, waiters=None):
        if waiters is None:
            waiters = self.ENGINES
        for e in sorted(self.dirty):
            self.pcount[e] += 1
            self.q[e].append(("inc", self.psem[e], 1))
        self.dirty = set()
        for e in waiters:
            seen = self.seen[e]
            for p in self.ENGINES:
                if p == e:
                    continue
                if self.pcount[p] > seen.get(p, 0):
                    self.q[e].append(("wait", self.psem[p], self.pcount[p]))
                    seen[p] = self.pcount[p]
            for name in sorted(self.dtot):
                key = "dma_" + name
                if self.dtot[name] > seen.get(key, 0):
                    self.q[e].append(("wait", self.dsem[name], self.dtot[name]))
                    seen[key] = self.dtot[name]

    def finalize(self, block):
        for ename in self.ENGINES:
            ops = self.q[ename]
            if not ops:
                continue

            def make(ops=ops):
                def run(e):
                    rec = _EngRec(e)
                    for item in ops:
                        if item[0] == "op":
                            item[1](rec)
                        elif item[0] == "inc":
                            _, sem, val = item
                            e.drain().then_inc(sem, val)
                            rec.last = None
                        else:
                            _, sem, val = item
                            e.wait_ge(sem, val)
                            rec.last = None

                return run

            getattr(block, ename)(make())


class _EngRec:

    def __init__(self, eng):
        self._eng = eng
        self.last = None

    def __getattr__(self, name):
        attr = getattr(self._eng, name)
        if not callable(attr):
            return attr

        def wrap(*a, **kw):
            res = attr(*a, **kw)
            if isinstance(res, bass.BassInstruction) and not name.startswith("dma"):
                self.last = res
            return res

        return wrap


def make_const_arrays():
    su = np.triu(np.ones((64, 64), np.float32), 1)
    ud = np.triu(np.ones((64, 64), np.float32), 0)
    sl = np.tril(np.ones((64, 64), np.float32), -1)
    i64 = np.eye(64, dtype=np.float32)
    mask_g1 = np.concatenate([su, ud, su, ud], axis=1)
    consts = {
        "mask_g1": np.concatenate([mask_g1, mask_g1], axis=0),
        "mask_sl": np.concatenate([sl, sl], axis=0),
        "i64b": np.concatenate([i64, i64], axis=0).astype(np.float32),
        "zeros1": np.zeros((128, 1), np.float32),
    }
    return consts


class ScanBufs:
    def __init__(self, nc, Bt):
        CB = Bt // L
        a = nc.alloc_sbuf_tensor
        self.ar = a("sc_ar", [128, NCC * 2 * Bt], BF16)
        self.bk = a("sc_bk", [128, NCC * 2 * Bt], BF16)
        self.vf = a("sc_vf", [128, NCC * Bt], BF16)
        self.tm = a("sc_tm", [128, 192], BF16)
        self.X = a("sc_X", [128, 128], BF16)
        self.g1 = a("sc_g1", [128, 256], BF16)
        self.P = a("sc_P", [128, 64 + 5 * 128], BF16)
        self.HT = a("sc_HT", [128, 64], BF16)
        self.Qf = a("sc_Qf", [128, 64], BF16)
        self.M = a("sc_M", [128, NCC * 128], BF16)
        self.slog = a("sc_slog", [128, NCC * Bt], F32)
        self.ep = a("sc_ep", [128, NCC * Bt], BF16)
        self.em = a("sc_em", [128, NCC * Bt], BF16)
        self.eprev = a("sc_eprev", [128, NCC * Bt], BF16)
        self.carry = a("sc_carry", [128, NCC], F32)
        self.carry2 = a("sc_carry2", [128, NCC], F32)
        self.offs = a("sc_offs", [128, NCC * CB], F32)
        self.dl = a("sc_dl", [128, NCC * CB], F32)


class ConstSB:

    def __init__(self, nc, s):
        a = nc.alloc_sbuf_tensor
        self.mask_g1 = a("cs_maskg1", [128, 256], F32)
        self.mask_sl = a("cs_masksl", [128, 64], F32)
        self.i64b = a("cs_i64b", [128, 64], BF16)
        self.zeros1 = a("cs_zeros1", [128, 1], F32)

    def declare_inputs(self, nc):
        self.d_mask_g1 = nc.dram_tensor("c_mask_g1", [128, 256], F32,
                                        kind="ExternalInput").ap()
        self.d_mask_sl = nc.dram_tensor("c_mask_sl", [128, 64], F32,
                                        kind="ExternalInput").ap()
        self.d_i64b = nc.dram_tensor("c_i64b_f", [128, 64], F32,
                                     kind="ExternalInput").ap()
        self.d_zeros1 = nc.dram_tensor("c_zeros1", [128, 1], F32,
                                       kind="ExternalInput").ap()

    def emit_load(self, s):
        s.dma("sync", "cload", self.mask_g1[:], self.d_mask_g1)
        s.dma("sync", "cload", self.mask_sl[:], self.d_mask_sl)
        s.dma("gpsimd", "cloadg", self.i64b[:], self.d_i64b)
        s.dma("sync", "cload", self.zeros1[:], self.d_zeros1)




def emit_scan_prep(s, bufs, psum, consts, Bt, src_sb, exp_scale,
                   r_src, k_src, v_src, a_src, b_src, first_block,
                   dl_t=None, dl_cb=0, nch_tot=None, blk_par=0):
    CB = Bt // L
    carry_in = bufs.carry if blk_par == 0 else bufs.carry2
    carry_out = bufs.carry2 if blk_par == 0 else bufs.carry
    if first_block:
        s.on("vector", lambda e: e.memset(carry_in[:], 0.0))
        s.barrier(waiters=("vector",))

    def cums(e):
        for j in range(NCC):
            e.tensor_tensor_scan(
                bufs.slog[:, j * Bt : (j + 1) * Bt],
                src_sb[:, j * Bt : (j + 1) * Bt],
                consts.zeros1[:].broadcast_to([128, Bt]),
                carry_in[:, j : j + 1],
                ALU.add,
                ALU.add,
            )

    s.on("vector", cums)
    s.barrier(waiters=("vector",))

    def mkoffs(e):
        for j in range(NCC):
            e.tensor_copy(bufs.offs[:, j * CB : j * CB + 1],
                          carry_in[:, j : j + 1])
            if CB > 1:
                sl3 = bufs.slog[:, j * Bt : (j + 1) * Bt].rearrange(
                    "p (c l) -> p c l", l=L
                )
                e.tensor_copy(
                    bufs.offs[:, j * CB + 1 : (j + 1) * CB].unsqueeze(-1),
                    sl3[:, 0 : CB - 1, L - 1 : L],
                )
    def carryupd(e):
        for j in range(NCC):
            e.tensor_copy(carry_out[:, j : j + 1],
                          bufs.slog[:, (j + 1) * Bt - 1 : (j + 1) * Bt])

    s.on("vector", mkoffs)
    s.on("vector", carryupd)
    s.barrier(waiters=("vector", "scalar"))

    def mkrel(e):
        for j in range(NCC):
            ob = bufs.offs[:, j * CB : (j + 1) * CB].unsqueeze(-1)
            e.tensor_sub(
                bufs.slog[:, j * Bt : (j + 1) * Bt].rearrange(
                    "p (c l) -> p c l", l=L
                ),
                bufs.slog[:, j * Bt : (j + 1) * Bt].rearrange(
                    "p (c l) -> p c l", l=L
                ),
                ob.broadcast_to([128, CB, L]),
            )

    s.on("vector", mkrel)
    s.barrier(waiters=("vector", "scalar"))

    if dl_t is None:
        dl_t, dl_cb, nch_tot = bufs.dl, 0, CB
    rel = bufs.slog

    def mkexp(e):
        sc = exp_scale
        for j in range(NCC):
            sl = slice(j * Bt, (j + 1) * Bt)
            e.activation(bufs.ep[:, sl], rel[:, sl], AF.Exp, scale=sc)
            e.activation(bufs.em[:, sl], rel[:, sl], AF.Exp, scale=-sc)
            r3 = rel[:, sl].rearrange("p (c l) -> p c l", l=L)
            p3 = bufs.eprev[:, sl].rearrange("p (c l) -> p c l", l=L)
            e.activation(p3[:, :, 1:L], r3[:, :, 0 : L - 1], AF.Exp, scale=sc)
        for j in range(NCC):
            r3 = rel[:, j * Bt : (j + 1) * Bt].rearrange(
                "p (c l) -> p c l", l=L
            )
            e.activation(
                dl_t[:, j * nch_tot + dl_cb : j * nch_tot + dl_cb + CB
                     ].unsqueeze(-1),
                r3[:, :, L - 1 : L],
                AF.Exp,
                scale=sc,
            )

    def epfix(e):
        for j in range(NCC):
            p3 = bufs.eprev[:, j * Bt : (j + 1) * Bt].rearrange(
                "p (c l) -> p c l", l=L
            )
            e.memset(p3[:, :, 0:1], 1.0)

    s.on("scalar", mkexp)
    s.on("vector", epfix)
    s.barrier(waiters=("vector", "gpsimd"))

    def mkhats_v(e):
        for j in range(NCC):
            sl = slice(j * Bt, (j + 1) * Bt)
            e.tensor_mul(bufs.ar[:, j * 2 * Bt + Bt : (j + 1) * 2 * Bt],
                         r_src[:, sl], bufs.ep[:, sl])
            e.tensor_mul(bufs.ar[:, j * 2 * Bt : j * 2 * Bt + Bt],
                         a_src[:, sl], bufs.eprev[:, sl])
            if v_src is not bufs.vf:
                e.tensor_copy(bufs.vf[:, sl], v_src[:, sl])

    def mkhats_g(e):
        for j in range(NCC):
            sl = slice(j * Bt, (j + 1) * Bt)
            e.tensor_mul(bufs.bk[:, j * 2 * Bt : j * 2 * Bt + Bt],
                         b_src[:, sl], bufs.em[:, sl])
            e.tensor_mul(bufs.bk[:, j * 2 * Bt + Bt : (j + 1) * 2 * Bt],
                         k_src[:, sl], bufs.em[:, sl])

    s.on("vector", mkhats_v)
    s.on("gpsimd", mkhats_g)
    s.barrier()


def emit_scan_block(s, bufs, psum, consts, Bt, y_out_sb, first_block,
                    y_base=0, y_stride=None, dl_t=None, dl_cb=0, nch_tot=None,
                    qf_t=None, ht_t=None):
    CB = Bt // L
    if y_stride is None:
        y_stride = Bt
    if dl_t is None:
        dl_t, dl_cb, nch_tot = bufs.dl, 0, CB
    if qf_t is None:
        qf_t, ht_t = bufs.qf_blk if hasattr(bufs, "qf_blk") else None, None
    if first_block:
        def minit(e):
            e.memset(
                bufs.M[:].rearrange("p (j u) -> p j u", u=128)[:, :, 0:64], 0.0
            )
            for j in range(NCC):
                e.tensor_copy(bufs.M[:, j * 128 + 64 : j * 128 + 128],
                              consts.i64b[:])

        s.on("vector", minit)
        s.barrier(waiters=("tensor",))

    for j in range(NCC):
        for c in range(CB):
            emit_chunk(s, bufs, psum, consts, Bt, j, c,
                       dl_t, dl_cb, nch_tot, qf_t, ht_t)
        yb0 = j * y_stride + y_base

        def yev(e, yb0=yb0):
            e.tensor_copy(y_out_sb[0:64, yb0 : yb0 + Bt], psum[2][0:64, 0:Bt])
            e.tensor_copy(y_out_sb[64:128, yb0 : yb0 + Bt],
                          psum[6][64:128, 0:Bt])

        s.on("vector", yev)
        s.barrier(waiters=("tensor",))


def emit_chunk(s, bufs, psum, consts, Bt, j, c,
               dl_t, dl_cb, nch_tot, qf_t, ht_t):
    CB = Bt // L
    cb = c * L
    if qf_t is None:
        qf_t, qf_col = bufs.Qf, 0
        ht_col = 0
        ht_t = bufs.HT
    else:
        qf_col = (j * CB + c) * 64
        ht_col = (j * CB + c) * 64

    def hrows(hh):
        return slice(64 * hh, 64 * hh + 64)

    def fmt(t, hh, base):
        return t[hrows(hh), base + cb : base + cb + L]

    a_t = lambda hh: fmt(bufs.ar, hh, j * 2 * Bt)
    r_t = lambda hh: fmt(bufs.ar, hh, j * 2 * Bt + Bt)
    b_t = lambda hh: fmt(bufs.bk, hh, j * 2 * Bt)
    k_t = lambda hh: fmt(bufs.bk, hh, j * 2 * Bt + Bt)
    v_t = lambda hh: fmt(bufs.vf, hh, j * Bt)

    def arpair(hh):
        r3 = bufs.ar[hrows(hh), :].rearrange("p (u t) -> p u t", t=Bt)
        return r3[:, 2 * j : 2 * j + 2, cb : cb + L]

    tp = lambda hh: (64 * hh, 64 * hh)
    I = lambda hh: consts.i64b[hrows(hh), :]

    btm = lambda hh: bufs.tm[hrows(hh), 0:64]
    ktm = lambda hh: bufs.tm[hrows(hh), 64:128]
    vtm = lambda hh: bufs.tm[hrows(hh), 128:192]
    aU = lambda hh: bufs.X[hrows(hh), 0:64]
    xSA = lambda hh: bufs.X[hrows(hh), 64:128]
    GT1 = lambda hh: bufs.g1[hrows(hh), 0:64]
    MRBT = lambda hh: bufs.g1[hrows(hh), 64:128]
    MAKT = lambda hh: bufs.g1[hrows(hh), 128:192]
    MRKT = lambda hh: bufs.g1[hrows(hh), 192:256]

    def Gp(lv, hh):
        if lv == 0:
            return bufs.P[hrows(hh), 0:64]
        return bufs.P[hrows(hh), 64 + (lv - 1) * 128 : 64 + (lv - 1) * 128 + 64]

    def GpT(lv, hh):
        if lv == 0:
            return GT1(hh)
        return bufs.P[hrows(hh), 64 + (lv - 1) * 128 + 64 : 64 + lv * 128]

    def tmm(e):
        for hh in range(2):
            for i, src in enumerate((a_t(hh), b_t(hh), k_t(hh), v_t(hh))):
                e.transpose(
                    psum[0][hrows(hh), i * 64 : i * 64 + 64],
                    src,
                    I(hh),
                    tile_position=tp(hh),
                )

    s.on("tensor", tmm)
    s.barrier(waiters=("vector",))

    def tmev(e):
        e.tensor_copy(bufs.X[:, 0:64], psum[0][:, 0:64])
        e.tensor_copy(bufs.tm[:, 0:192], psum[0][:, 64:256])

    s.on("vector", tmev)
    s.barrier(waiters=("tensor",))

    def gmm(e):
        for hh in range(2):
            e.matmul(psum[1][hrows(hh), 0:128], b_t(hh), arpair(hh),
                     start=True, stop=True, tile_position=tp(hh))
            e.matmul(psum[1][hrows(hh), 128:256], k_t(hh), arpair(hh),
                     start=True, stop=True, tile_position=tp(hh))
            e.matmul(psum[3][hrows(hh), 256:320], a_t(hh), b_t(hh),
                     start=True, stop=True, tile_position=tp(hh))

    s.on("tensor", gmm)
    s.barrier(waiters=("vector",))

    def gev(e):
        e.tensor_mul(bufs.g1[:], psum[1][:, 0:256], consts.mask_g1[:])
        e.tensor_mul(bufs.P[:, 0:64], psum[3][:, 256:320], consts.mask_sl[:])

    s.on("vector", gev)
    s.barrier(waiters=("tensor",))

    for lv in range(1, 6):
        def sqmm(e, lv=lv):
            for hh in range(2):
                if lv < 5:
                    e.matmul(psum[3][hrows(hh), 0:64],
                             GpT(lv - 1, hh), Gp(lv - 1, hh),
                             start=True, stop=True, tile_position=tp(hh))
                e.matmul(psum[3][hrows(hh), 64:128],
                         Gp(lv - 1, hh), GpT(lv - 1, hh),
                         start=True, stop=True, tile_position=tp(hh))

        s.on("tensor", sqmm)
        s.barrier(waiters=("vector",))

        def sqev(e, lv=lv):
            if lv < 5:
                e.tensor_copy(bufs.P[:, 64 + (lv - 1) * 128 : 64 + lv * 128],
                              psum[3][:, 0:128])
            else:
                e.tensor_copy(
                    bufs.P[:, 64 + (lv - 1) * 128 + 64 : 64 + lv * 128],
                    psum[3][:, 64:128],
                )

        s.on("vector", sqev)
        s.barrier(waiters=("tensor",))

    def xloc(e):
        for hh in range(2):
            e.matmul(psum[4][hrows(hh), 64:128], MAKT(hh), vtm(hh),
                     start=True, stop=True, tile_position=tp(hh))

    s.on("tensor", xloc)
    s.barrier(waiters=("vector",))

    s.on("vector", lambda e: e.tensor_copy(bufs.X[:, 64:128],
                                           psum[4][:, 64:128]))
    s.barrier(waiters=("tensor",))

    for lv in range(6):
        def chmm(e, lv=lv):
            for hh in range(2):
                e.matmul(psum[4][hrows(hh), 0:128], GpT(lv, hh),
                         bufs.X[hrows(hh), 0:128],
                         start=True, stop=False, tile_position=tp(hh))
                e.matmul(psum[4][hrows(hh), 0:128], I(hh),
                         bufs.X[hrows(hh), 0:128],
                         start=False, stop=True, tile_position=tp(hh))

        s.on("tensor", chmm)
        s.barrier(waiters=("vector",))

        s.on("vector", lambda e: e.tensor_copy(bufs.X[:, 0:128],
                                               psum[4][:, 0:128]))
        s.barrier(waiters=("tensor",))

    def fmm(e):
        for hh in range(2):
            e.matmul(psum[5][hrows(hh), 0:64], aU(hh), btm(hh),
                     start=True, stop=True, tile_position=tp(hh))
            e.matmul(psum[5][hrows(hh), 64:128], aU(hh), MRBT(hh),
                     start=True, stop=True, tile_position=tp(hh))
            yb = psum[2] if hh == 0 else psum[6]
            e.matmul(yb[hrows(hh), cb : cb + 64], xSA(hh), MRBT(hh),
                     start=True, stop=False, tile_position=tp(hh))
            e.matmul(yb[hrows(hh), cb : cb + 64], vtm(hh), MRKT(hh),
                     start=False, stop=False, tile_position=tp(hh))

    s.on("tensor", fmm)
    s.barrier(waiters=("vector",))

    def fev(e):
        e.tensor_copy(ht_t[:, ht_col : ht_col + 64], psum[5][:, 0:64])
        e.tensor_add(qf_t[:, qf_col : qf_col + 64], psum[5][:, 64:128],
                     bufs.ar[:, j * 2 * Bt + Bt + cb : j * 2 * Bt + Bt + cb + 64])

    s.on("vector", fev)
    s.barrier(waiters=("tensor",))

    Mj = lambda hh: bufs.M[hrows(hh), j * 128 : j * 128 + 128]
    Zj = lambda hh: bufs.M[hrows(hh), j * 128 : j * 128 + 64]

    def smm(e):
        for hh in range(2):
            e.matmul(psum[7][hrows(hh), 0:128],
                     ht_t[hrows(hh), ht_col : ht_col + 64], Mj(hh),
                     start=True, stop=False, tile_position=tp(hh))
            e.matmul(psum[7][hrows(hh), 0:64], btm(hh), xSA(hh),
                     start=False, stop=False, tile_position=tp(hh))
            e.matmul(psum[7][hrows(hh), 0:64], ktm(hh), vtm(hh),
                     start=False, stop=False, tile_position=tp(hh))
            e.matmul(psum[7][hrows(hh), 0:128], I(hh), Mj(hh),
                     start=False, stop=True, tile_position=tp(hh))
        for hh in range(2):
            yb = psum[2] if hh == 0 else psum[6]
            e.matmul(yb[hrows(hh), cb : cb + 64], Zj(hh),
                     qf_t[hrows(hh), qf_col : qf_col + 64],
                     start=False, stop=True, tile_position=tp(hh))

    s.on("tensor", smm)
    s.barrier(waiters=("vector",))

    def sev(e):
        e.tensor_scalar(
            bufs.M[:, j * 128 : j * 128 + 128],
            psum[7][:, 0:128],
            dl_t[:, j * nch_tot + dl_cb + c : j * nch_tot + dl_cb + c + 1],
            None,
            ALU.mult,
        )

    s.on("vector", sev)
    s.barrier(waiters=("tensor",))


def _cp(e, dst, src):
    if getattr(e, "_kind", None) == "scalar":
        e.activation(dst, src, AF.Copy)
        return
    try:
        e.tensor_copy(dst, src)
    except AttributeError:
        e.activation(dst, src, AF.Copy)


def emit_scan_block_v2(s, bufs, psum, consts, Bt, y_out_sb, first_block,
                       y_base=0, y_stride=None, dl_t=None, dl_cb=0,
                       nch_tot=None, qf_t=None, ht_t=None):
    CB = Bt // L
    if y_stride is None:
        y_stride = Bt
    if first_block:
        def minit(e):
            e.memset(
                bufs.M[:].rearrange("p (j u) -> p j u", u=128)[:, :, 0:64], 0.0
            )
            for j in range(NCC):
                e.tensor_copy(bufs.M[:, j * 128 + 64 : j * 128 + 128],
                              consts.i64b[:])

        s.on("vector", minit)
        s.barrier(waiters=("tensor",))

    builders = [emit_chunk_v2(s, bufs, psum, consts, Bt, c, y_out_sb, y_base,
                              y_stride, dl_t, dl_cb, nch_tot, qf_t, ht_t)
                for c in range(CB)]
    NSTEP = len(builders[0])
    starts = []
    for i in range(2 * CB):
        starts.append(i if i < 2 else starts[i - 2] + 20)
    total = starts[-1] + 2 * NSTEP
    for p in range(total):
        trec, vrec, srec = _Rec("tensor"), _Rec("vector"), _Rec("scalar")
        for i in range(2 * CB):
            c, h = i // 2, i % 2
            ladder = builders[c]
            rel = p - starts[i]
            if rel < 0 or rel >= 2 * NSTEP:
                continue
            k, kind = rel // 2, rel % 2
            if kind == 0:
                if ladder[k][0] is not None:
                    ladder[k][0](trec, h)
            else:
                if ladder[k][1] is not None:
                    ladder[k][1](vrec, srec, h)
        any_ = False
        for eng, rec in (("tensor", trec), ("vector", vrec), ("scalar", srec)):
            if rec.ops:
                s.on(eng, _replay(rec.ops))
                any_ = True
        if any_:
            s.barrier(waiters=("tensor", "vector", "scalar"))

def emit_chunk_v2(s, bufs, psum, consts, Bt, c, y_out_sb, y_base, y_stride,
                  dl_t, dl_cb, nch_tot, qf_t, ht_t):
    CB = Bt // L
    cb = c * L
    hr = lambda hh: slice(64 * hh, 64 * hh + 64)
    tp = lambda hh: (64 * hh, 64 * hh)
    I = lambda hh: consts.i64b[hr(hh), :]
    a_t = lambda j, hh: bufs.ar[hr(hh), j * 2 * Bt + cb : j * 2 * Bt + cb + L]
    b_t = lambda j, hh: bufs.bk[hr(hh), j * 2 * Bt + cb : j * 2 * Bt + cb + L]
    k_t = lambda j, hh: bufs.bk[hr(hh),
                                j * 2 * Bt + Bt + cb : j * 2 * Bt + Bt + cb + L]
    v_t = lambda j, hh: bufs.vf[hr(hh), j * Bt + cb : j * Bt + cb + L]

    def arpair(j, hh):
        r3 = bufs.ar[hr(hh), :].rearrange("p (u t) -> p u t", t=Bt)
        return r3[:, 2 * j : 2 * j + 2, cb : cb + L]

    aU = lambda j, hh: bufs.Xb[hr(hh), j * 128 : j * 128 + 64]
    xSA = lambda j, hh: bufs.Xb[hr(hh), j * 128 + 64 : j * 128 + 128]
    X2 = lambda j, hh: bufs.Xb[hr(hh), j * 128 : j * 128 + 128]
    btm = lambda j, hh: bufs.TMb[hr(hh), j * 192 : j * 192 + 64]
    ktm = lambda j, hh: bufs.TMb[hr(hh), j * 192 + 64 : j * 192 + 128]
    vtm = lambda j, hh: bufs.TMb[hr(hh), j * 192 + 128 : j * 192 + 192]
    GT1 = lambda j, hh: bufs.g1b[hr(hh), j * 256 : j * 256 + 64]
    MRBT = lambda j, hh: bufs.g1b[hr(hh), j * 256 + 64 : j * 256 + 128]
    MAKT = lambda j, hh: bufs.g1b[hr(hh), j * 256 + 128 : j * 256 + 192]
    MRKT = lambda j, hh: bufs.g1b[hr(hh), j * 256 + 192 : j * 256 + 256]

    def Gp(lv, j, hh):
        if lv == 0:
            return bufs.Gb0[hr(hh), j * 64 : j * 64 + 64]
        o = ((lv - 1) * 8 + j) * 64
        return bufs.Gb14[hr(hh), o : o + 64]

    def GpT(lv, j, hh):
        if lv == 0:
            return GT1(j, hh)
        if lv == 1:
            return bufs.GTb1[hr(hh), j * 64 : j * 64 + 64]
        o = ((lv - 2) * 8 + j) * 64
        return bufs.GTb25[hr(hh), o : o + 64]

    import os as _os0
    BK_OLD = bool(_os0.environ.get("SCAN_BANKS_OLD"))

    def F(h, i):
        if BK_OLD:
            return psum[(2 + h) if i == 0 else (4 + h)]
        return psum[2 + 2 * h + i]

    def six(h):
        if BK_OLD:
            return psum[6][:, 4 * h * 64 : 4 * h * 64 + 256]
        return psum[6 + h][:, 0:256]

    def sixh(h, hh):
        if BK_OLD:
            return psum[6][hr(hh), h * 256 : h * 256 + 256]
        return psum[6 + h][hr(hh), 0:256]

    def sevh(h, hh):
        return psum[6 + h][hr(hh), 256:512]

    def js(h):
        return range(4 * h, 4 * h + 4)

    def pe_tmgr(e, h):
        pbh = psum[h]
        for j in js(h):
            for hh in range(2):
                for i, srct in enumerate((a_t(j, hh), b_t(j, hh),
                                          k_t(j, hh), v_t(j, hh))):
                    e.transpose(pbh[hr(hh), (j % 4) * 256 + i * 64 :
                                (j % 4) * 256 + i * 64 + 64],
                                srct, I(hh), tile_position=tp(hh))
        for j in js(h):
            bk_ = F(h, (j % 4) // 2)
            co = (j % 2) * 256
            for hh in range(2):
                e.matmul(bk_[hr(hh), co : co + 128], b_t(j, hh),
                         arpair(j, hh), start=True, stop=True,
                         tile_position=tp(hh))
                e.matmul(bk_[hr(hh), co + 128 : co + 256], k_t(j, hh),
                         arpair(j, hh), start=True, stop=True,
                         tile_position=tp(hh))
                gdst = (psum[6][hr(hh), j * 64 : j * 64 + 64] if BK_OLD else
                        sixh(h, hh)[:, (j % 4) * 64 : (j % 4) * 64 + 64])
                e.matmul(gdst, a_t(j, hh), b_t(j, hh),
                         start=True, stop=True, tile_position=tp(hh))

    def ev_tmgr(eV, eS, h):
        pbh = psum[h]
        src = pbh[:, 0:1024].rearrange("p (j u) -> p j u", u=256)
        dU = bufs.Xb[:, h * 512 : h * 512 + 512].rearrange(
            "p (j u) -> p j u", u=128)
        dT = bufs.TMb[:, h * 768 : h * 768 + 768].rearrange(
            "p (j u) -> p j u", u=192)
        _cp(eS, dU[:, :, 0:64], src[:, :, 0:64])
        _cp(eS, dT[:, :, 0:192], src[:, :, 64:256])
        eV.tensor_mul(bufs.g1b[:, h * 1024 : h * 1024 + 512],
                      F(h, 0)[:, 0:512], consts.mask_g2[:])
        eV.tensor_mul(bufs.g1b[:, h * 1024 + 512 : h * 1024 + 1024],
                      F(h, 1)[:, 0:512], consts.mask_g2[:])
        eV.tensor_mul(bufs.Gb0[:, h * 256 : h * 256 + 256], six(h),
                      consts.mask_sl8[:, 0:256])

    def pe_sq(e, h, lv):
        for j in js(h):
            bk_ = psum[2 + j // 4] if BK_OLD else F(h, 0)
            co = (j % 4) * 128
            for hh in range(2):
                if lv < 5:
                    e.matmul(bk_[hr(hh), co : co + 64],
                             GpT(lv - 1, j, hh), Gp(lv - 1, j, hh),
                             start=True, stop=True, tile_position=tp(hh))
                e.matmul(bk_[hr(hh), co + 64 : co + 128],
                         Gp(lv - 1, j, hh), GpT(lv - 1, j, hh),
                         start=True, stop=True, tile_position=tp(hh))

    def ev_sq(eV, eS, h, lv):
        sbank = psum[2 + h] if BK_OLD else F(h, 0)
        src = sbank[:, 0:512].rearrange("p (j u) -> p j u", u=128)
        if lv < 5:
            dG = (bufs.Gb14[:, ((lv - 1) * 8 + 4 * h) * 64 :
                            ((lv - 1) * 8 + 4 * h + 4) * 64]
                  .rearrange("p (j u) -> p j u", u=64))
            _cp(eV, dG[:, :, :], src[:, :, 0:64])
        if lv == 1:
            dT = bufs.GTb1[:, h * 256 : h * 256 + 256].rearrange(
                "p (j u) -> p j u", u=64)
        else:
            dT = (bufs.GTb25[:, ((lv - 2) * 8 + 4 * h) * 64 :
                             ((lv - 2) * 8 + 4 * h + 4) * 64]
                  .rearrange("p (j u) -> p j u", u=64))
        _cp(eV, dT[:, :, :], src[:, :, 64:128])

    def pe_xloc(e, h):
        for j in js(h):
            for hh in range(2):
                xdst = (psum[6][hr(hh), j * 64 : j * 64 + 64] if BK_OLD else
                        sixh(h, hh)[:, (j % 4) * 64 : (j % 4) * 64 + 64])
                e.matmul(xdst, MAKT(j, hh), vtm(j, hh),
                         start=True, stop=True, tile_position=tp(hh))

    def ev_xloc(eV, eS, h):
        dSA = bufs.Xb[:, h * 512 : h * 512 + 512].rearrange(
            "p (j u) -> p j u", u=128)
        _cp(eV, dSA[:, :, 64:128],
            six(h).rearrange("p (j u) -> p j u", u=64))

    def pe_chain(e, h, lv):
        for j in js(h):
            cbank = psum[4 + j // 4] if BK_OLD else F(h, 1)
            for hh in range(2):
                e.matmul(cbank[hr(hh), (j % 4) * 128 : (j % 4) * 128 + 128],
                         GpT(lv, j, hh), X2(j, hh),
                         start=True, stop=True, tile_position=tp(hh))

    def ev_chain(eV, eS, h, lv):
        cbank = psum[4 + h] if BK_OLD else F(h, 1)
        eV.tensor_add(bufs.Xb[:, h * 512 : h * 512 + 512],
                      cbank[:, 0:512],
                      bufs.Xb[:, h * 512 : h * 512 + 512])

    import os as _os
    FMM_OLD = bool(_os.environ.get("SCAN_FMM_OLD"))
    SMM_OLD = bool(_os.environ.get("SCAN_SMM_OLD"))

    def pe_fmm(e, h):
        for j in js(h):
            for hh in range(2):
                if FMM_OLD:
                    e.matmul(psum[6][hr(hh), j * 64 : j * 64 + 64],
                             aU(j, hh), btm(j, hh),
                             start=True, stop=True, tile_position=tp(hh))
                    e.matmul(psum[7][hr(hh), j * 64 : j * 64 + 64],
                             aU(j, hh), MRBT(j, hh),
                             start=True, stop=True, tile_position=tp(hh))
                    continue
                e.matmul(sixh(h, hh)[:, (j % 4) * 64 : (j % 4) * 64 + 64],
                         aU(j, hh), btm(j, hh),
                         start=True, stop=True, tile_position=tp(hh))
                e.matmul(sevh(h, hh)[:, (j % 4) * 64 : (j % 4) * 64 + 64],
                         aU(j, hh), MRBT(j, hh),
                         start=True, stop=True, tile_position=tp(hh))

    def ev_fmm(eV, eS, h):
        dst = ht_t[:].rearrange("p (j u) -> p j u", u=CB * 64)
        srcht = (psum[6][:, 4 * h * 64 : 4 * h * 64 + 256] if FMM_OLD
                 else six(h))
        _cp(eV, dst[:, 4 * h : 4 * h + 4, c * 64 : c * 64 + 64],
            srcht.rearrange("p (j u) -> p j u", u=64))
        dq = qf_t[:].rearrange("p (j u) -> p j u", u=CB * 64)
        rsl = bufs.ar[:].rearrange("p (u t) -> p u t", t=Bt)
        srcqf = (psum[7][:, 4 * h * 64 : 4 * h * 64 + 256] if FMM_OLD
                 else psum[6 + h][:, 256:512])
        eV.tensor_add(dq[:, 4 * h : 4 * h + 4, c * 64 : c * 64 + 64],
                      srcqf.rearrange("p (j u) -> p j u", u=64),
                      rsl[:, 8 * h + 1 : 8 * h + 8 : 2, cb : cb + 64])

    def pe_smm(e, h):
        for j in js(h):
            bk_ = psum[2 + j // 4] if BK_OLD else F(h, 0)
            co = (j % 4) * 128
            Mj = lambda hh: bufs.M[hr(hh), j * 128 : j * 128 + 128]
            Zj = lambda hh: bufs.M[hr(hh), j * 128 : j * 128 + 64]
            htj = lambda hh: ht_t[hr(hh), (j * CB + c) * 64 :
                                  (j * CB + c) * 64 + 64]
            qfj = lambda hh: qf_t[hr(hh), (j * CB + c) * 64 :
                                  (j * CB + c) * 64 + 64]
            for hh in range(2):
                e.matmul(bk_[hr(hh), co : co + 128], htj(hh), Mj(hh),
                         start=True, stop=False, tile_position=tp(hh))
                e.matmul(bk_[hr(hh), co : co + 64], btm(j, hh), xSA(j, hh),
                         start=False, stop=False, tile_position=tp(hh))
                e.matmul(bk_[hr(hh), co : co + 64], ktm(j, hh), vtm(j, hh),
                         start=False, stop=True, tile_position=tp(hh))
            for hh in range(2):
                if SMM_OLD:
                    yb = psum[4] if hh == 0 else psum[5]
                    yo = j * 64
                else:
                    yb = F(h, 1)
                    yo = (j % 4) * 64 + hh * 256
                e.matmul(yb[hr(hh), yo : yo + 64], xSA(j, hh), MRBT(j, hh),
                         start=True, stop=False, tile_position=tp(hh))
                e.matmul(yb[hr(hh), yo : yo + 64], vtm(j, hh), MRKT(j, hh),
                         start=False, stop=False, tile_position=tp(hh))
                e.matmul(yb[hr(hh), yo : yo + 64], Zj(hh), qfj(hh),
                         start=False, stop=True, tile_position=tp(hh))

    def ev_smm(eV, eS, h):
        mbank = psum[2 + h] if BK_OLD else F(h, 0)
        eV.tensor_add(bufs.mt[:, h * 512 : h * 512 + 512],
                      mbank[:, 0:512], bufs.M[:, h * 512 : h * 512 + 512])
        dst = y_out_sb.rearrange("p (j t) -> p j t", t=y_stride)
        if SMM_OLD:
            _cp(eS, dst[0:64, 4 * h : 4 * h + 4,
                        y_base + cb : y_base + cb + 64],
                psum[4][0:64, 4 * h * 64 : 4 * h * 64 + 256].rearrange(
                    "p (j u) -> p j u", u=64))
            _cp(eS, dst[64:128, 4 * h : 4 * h + 4,
                        y_base + cb : y_base + cb + 64],
                psum[5][64:128, 4 * h * 64 : 4 * h * 64 + 256].rearrange(
                    "p (j u) -> p j u", u=64))
        else:
            yb = F(h, 1)
            _cp(eV, dst[0:64, 4 * h : 4 * h + 4,
                        y_base + cb : y_base + cb + 64],
                yb[0:64, 0:256].rearrange("p (j u) -> p j u", u=64))
            _cp(eV, dst[64:128, 4 * h : 4 * h + 4,
                        y_base + cb : y_base + cb + 64],
                yb[64:128, 256:512].rearrange("p (j u) -> p j u", u=64))

    def ev_mdl(eV, eS, h):
        for j in js(h):
            eS.activation(
                bufs.M[:, j * 128 : j * 128 + 128],
                bufs.mt[:, j * 128 : j * 128 + 128], AF.Copy,
                scale=dl_t[:, j * nch_tot + dl_cb + c :
                           j * nch_tot + dl_cb + c + 1])

    ladder = [(pe_tmgr, ev_tmgr),
              (lambda e, h: (pe_sq(e, h, 1), pe_xloc(e, h)),
               lambda v, sc, h: (ev_sq(v, sc, h, 1), ev_xloc(v, sc, h))),
              (lambda e, h: (pe_sq(e, h, 2), pe_chain(e, h, 0)),
               lambda v, sc, h: (ev_sq(v, sc, h, 2), ev_chain(v, sc, h, 0))),
              (lambda e, h: (pe_sq(e, h, 3), pe_chain(e, h, 1)),
               lambda v, sc, h: (ev_sq(v, sc, h, 3), ev_chain(v, sc, h, 1))),
              (lambda e, h: (pe_sq(e, h, 4), pe_chain(e, h, 2)),
               lambda v, sc, h: (ev_sq(v, sc, h, 4), ev_chain(v, sc, h, 2))),
              (lambda e, h: (pe_sq(e, h, 5), pe_chain(e, h, 3)),
               lambda v, sc, h: (ev_sq(v, sc, h, 5), ev_chain(v, sc, h, 3))),
              (lambda e, h: pe_chain(e, h, 4),
               lambda v, sc, h: ev_chain(v, sc, h, 4)),
              (lambda e, h: pe_chain(e, h, 5),
               lambda v, sc, h: ev_chain(v, sc, h, 5)),
              (pe_fmm, ev_fmm),
              (pe_smm, ev_smm),
              (None, ev_mdl)]

    return ladder


class _Rec:

    def __init__(self, kind):
        self.__dict__["_kind"] = kind
        self.__dict__["ops"] = []

    def __getattr__(self, name):
        def call(*a, **kw):
            self.ops.append((name, a, kw))
        return call


def _replay(ops):
    def f(e):
        for name, a, kw in ops:
            getattr(e, name)(*a, **kw)
    return f


def emit_scan_block_v2_old(s, bufs, psum, consts, Bt, y_out_sb, first_block,
                       y_base=0, y_stride=None, dl_t=None, dl_cb=0,
                       nch_tot=None, qf_t=None, ht_t=None):
    CB = Bt // L
    if y_stride is None:
        y_stride = Bt
    if first_block:
        def minit(e):
            e.memset(
                bufs.M[:].rearrange("p (j u) -> p j u", u=128)[:, :, 0:64], 0.0
            )
            for j in range(NCC):
                e.tensor_copy(bufs.M[:, j * 128 + 64 : j * 128 + 128],
                              consts.i64b[:])

        s.on("vector", minit)
        s.barrier(waiters=("tensor",))

    for c in range(CB):
        emit_chunk_v2_old(s, bufs, psum, consts, Bt, c, y_out_sb, y_base,
                      y_stride, dl_t, dl_cb, nch_tot, qf_t, ht_t)


def emit_chunk_v2_old(s, bufs, psum, consts, Bt, c, y_out_sb, y_base, y_stride,
                  dl_t, dl_cb, nch_tot, qf_t, ht_t):
    CB = Bt // L
    cb = c * L
    hr = lambda hh: slice(64 * hh, 64 * hh + 64)
    tp = lambda hh: (64 * hh, 64 * hh)
    I = lambda hh: consts.i64b[hr(hh), :]
    a_t = lambda j, hh: bufs.ar[hr(hh), j * 2 * Bt + cb : j * 2 * Bt + cb + L]
    b_t = lambda j, hh: bufs.bk[hr(hh), j * 2 * Bt + cb : j * 2 * Bt + cb + L]
    k_t = lambda j, hh: bufs.bk[hr(hh),
                                j * 2 * Bt + Bt + cb : j * 2 * Bt + Bt + cb + L]
    v_t = lambda j, hh: bufs.vf[hr(hh), j * Bt + cb : j * Bt + cb + L]

    def arpair(j, hh):
        r3 = bufs.ar[hr(hh), :].rearrange("p (u t) -> p u t", t=Bt)
        return r3[:, 2 * j : 2 * j + 2, cb : cb + L]

    aU = lambda j, hh: bufs.Xb[hr(hh), j * 128 : j * 128 + 64]
    xSA = lambda j, hh: bufs.Xb[hr(hh), j * 128 + 64 : j * 128 + 128]
    X2 = lambda j, hh: bufs.Xb[hr(hh), j * 128 : j * 128 + 128]
    btm = lambda j, hh: bufs.TMb[hr(hh), j * 192 : j * 192 + 64]
    ktm = lambda j, hh: bufs.TMb[hr(hh), j * 192 + 64 : j * 192 + 128]
    vtm = lambda j, hh: bufs.TMb[hr(hh), j * 192 + 128 : j * 192 + 192]
    GT1 = lambda j, hh: bufs.g1b[hr(hh), j * 256 : j * 256 + 64]
    MRBT = lambda j, hh: bufs.g1b[hr(hh), j * 256 + 64 : j * 256 + 128]
    MAKT = lambda j, hh: bufs.g1b[hr(hh), j * 256 + 128 : j * 256 + 192]
    MRKT = lambda j, hh: bufs.g1b[hr(hh), j * 256 + 192 : j * 256 + 256]

    def Gp(lv, j, hh):
        if lv == 0:
            return bufs.Gb0[hr(hh), j * 64 : j * 64 + 64]
        o = ((lv - 1) * 8 + j) * 64
        return bufs.Gb14[hr(hh), o : o + 64]

    def GpT(lv, j, hh):
        if lv == 0:
            return GT1(j, hh)
        if lv == 1:
            return bufs.GTb1[hr(hh), j * 64 : j * 64 + 64]
        o = ((lv - 2) * 8 + j) * 64
        return bufs.GTb25[hr(hh), o : o + 64]

    J = NCC

    def tmm(e, jlo, jhi):
        for j in range(jlo, jhi):
            for hh in range(2):
                for i, src in enumerate((a_t(j, hh), b_t(j, hh),
                                         k_t(j, hh), v_t(j, hh))):
                    e.transpose(psum[0][hr(hh),
                                        (j - jlo) * 256 + i * 64 :
                                        (j - jlo) * 256 + i * 64 + 64],
                                src, I(hh), tile_position=tp(hh))

    def tmev(e_s, jlo):
        src = psum[0][:, 0:1024].rearrange("p (j u) -> p j u", u=256)
        dU = bufs.Xb[:, jlo * 128 : (jlo + 4) * 128].rearrange(
            "p (j u) -> p j u", u=128)
        dT = bufs.TMb[:, jlo * 192 : (jlo + 4) * 192].rearrange(
            "p (j u) -> p j u", u=192)
        _cp(e_s, dU[:, :, 0:64], src[:, :, 0:64])
        _cp(e_s, dT[:, :, 0:192], src[:, :, 64:256])

    def sqmm(e, lv):
        for j in range(J):
            bk_ = psum[2 + j // 4]
            co = (j % 4) * 128
            for hh in range(2):
                if lv < 5:
                    e.matmul(bk_[hr(hh), co : co + 64],
                             GpT(lv - 1, j, hh), Gp(lv - 1, j, hh),
                             start=True, stop=True, tile_position=tp(hh))
                e.matmul(bk_[hr(hh), co + 64 : co + 128],
                         Gp(lv - 1, j, hh), GpT(lv - 1, j, hh),
                         start=True, stop=True, tile_position=tp(hh))

    def sqev(e_v, e_g, lv):
        for half in range(2):
            src = psum[2 + half][:, 0:512].rearrange("p (j u) -> p j u", u=128)
            if lv < 5:
                dG = (bufs.Gb14[:, ((lv - 1) * 8 + half * 4) * 64 :
                                ((lv - 1) * 8 + half * 4 + 4) * 64]
                      .rearrange("p (j u) -> p j u", u=64))
                _cp(e_v, dG[:, :, :], src[:, :, 0:64])
            if lv == 1:
                dT = bufs.GTb1[:, half * 256 : half * 256 + 256].rearrange(
                    "p (j u) -> p j u", u=64)
            else:
                dT = (bufs.GTb25[:, ((lv - 2) * 8 + half * 4) * 64 :
                                 ((lv - 2) * 8 + half * 4 + 4) * 64]
                      .rearrange("p (j u) -> p j u", u=64))
            _cp(e_g, dT[:, :, :], src[:, :, 64:128])

    def chmm(e, lv):
        for j in range(J):
            bk_ = psum[4 + j // 4]
            e2 = e
            for hh in range(2):
                e2.matmul(bk_[hr(hh), (j % 4) * 128 : (j % 4) * 128 + 128],
                          GpT(lv, j, hh), X2(j, hh),
                          start=True, stop=True, tile_position=tp(hh))

    def chev(e, lv):
        for half in range(2):
            e.tensor_add(bufs.Xb[:, half * 512 : half * 512 + 512],
                         psum[4 + half][:, 0:512],
                         bufs.Xb[:, half * 512 : half * 512 + 512])

    def ph0(e):
        tmm(e, 0, 4)
        for j in range(J):
            bk_ = psum[2 + j // 2]
            co = (j % 2) * 256
            for hh in range(2):
                e.matmul(bk_[hr(hh), co : co + 128], b_t(j, hh), arpair(j, hh),
                         start=True, stop=True, tile_position=tp(hh))
                e.matmul(bk_[hr(hh), co + 128 : co + 256], k_t(j, hh),
                         arpair(j, hh),
                         start=True, stop=True, tile_position=tp(hh))
                e.matmul(psum[6][hr(hh), j * 64 : j * 64 + 64],
                         a_t(j, hh), b_t(j, hh),
                         start=True, stop=True, tile_position=tp(hh))

    s.on("tensor", ph0)
    s.barrier(waiters=("vector", "scalar", "gpsimd"))

    s.on("scalar", lambda e: tmev(e, 0))
    for b_i, eng in ((0, "vector"), (1, "vector"), (2, "vector"),
                     (3, "vector")):
        s.on(eng, lambda e, b_i=b_i: e.tensor_mul(
            bufs.g1b[:, b_i * 512 : b_i * 512 + 512],
            psum[2 + b_i][:, 0:512], consts.mask_g2[:]))
    s.on("vector", lambda e: e.tensor_mul(bufs.Gb0[:], psum[6][:, 0:512],
                                          consts.mask_sl8[:]))
    s.barrier(waiters=("tensor",))

    def ph2(e):
        tmm(e, 4, 8)
        sqmm(e, 1)

    s.on("tensor", ph2)
    s.barrier(waiters=("vector", "scalar", "gpsimd"))

    s.on("scalar", lambda e: tmev(e, 4))
    s.on("vector", lambda e: sqev(e, e, 1))
    s.barrier(waiters=("tensor",))

    def ph4(e):
        for j in range(J):
            for hh in range(2):
                e.matmul(psum[6][hr(hh), j * 64 : j * 64 + 64],
                         MAKT(j, hh), vtm(j, hh),
                         start=True, stop=True, tile_position=tp(hh))
        sqmm(e, 2)

    s.on("tensor", ph4)
    s.barrier(waiters=("vector", "scalar", "gpsimd"))

    def xev(e):
        dSA = bufs.Xb[:].rearrange("p (j u) -> p j u", u=128)
        _cp(e, dSA[:, :, 64:128],
            psum[6][:, 0:512].rearrange("p (j u) -> p j u", u=64))

    s.on("scalar", xev)
    s.on("vector", lambda e: sqev(e, e, 2))
    s.barrier(waiters=("tensor",))

    for lv in range(6):
        def phc(e, lv=lv):
            chmm(e, lv)
            if lv + 3 <= 5:
                sqmm(e, lv + 3)

        s.on("tensor", phc)
        s.barrier(waiters=("vector", "scalar", "gpsimd"))

        def phe_v(e, lv=lv):
            chev(e, lv)

        s.on("vector", phe_v)
        if lv + 3 <= 5:
            s.on("scalar", lambda e, lv=lv: sqev(e, e, lv + 3))
        s.barrier(waiters=("tensor",))

    def ph18(e):
        for j in range(J):
            for hh in range(2):
                e.matmul(psum[6][hr(hh), j * 64 : j * 64 + 64],
                         aU(j, hh), btm(j, hh),
                         start=True, stop=True, tile_position=tp(hh))
                e.matmul(psum[7][hr(hh), j * 64 : j * 64 + 64],
                         aU(j, hh), MRBT(j, hh),
                         start=True, stop=True, tile_position=tp(hh))

    s.on("tensor", ph18)
    s.barrier(waiters=("vector", "scalar", "gpsimd"))

    def fht(e):
        dst = ht_t[:].rearrange("p (j u) -> p j u", u=CB * 64)
        _cp(e, dst[:, :, c * 64 : c * 64 + 64],
            psum[6][:, 0:512].rearrange("p (j u) -> p j u", u=64))

    def fqf(e):
        dst = qf_t[:].rearrange("p (j u) -> p j u", u=CB * 64)
        rsl = bufs.ar[:].rearrange("p (u t) -> p u t", t=Bt)
        e.tensor_add(dst[:, :, c * 64 : c * 64 + 64],
                     psum[7][:, 0:512].rearrange("p (j u) -> p j u", u=64),
                     rsl[:, 1:16:2, cb : cb + 64])

    s.on("scalar", fht)
    s.on("vector", fqf)
    s.barrier(waiters=("tensor",))

    def ph20(e):
        for j in range(J):
            bk_ = psum[2 + j // 4]
            co = (j % 4) * 128
            Mj = lambda hh: bufs.M[hr(hh), j * 128 : j * 128 + 128]
            Zj = lambda hh: bufs.M[hr(hh), j * 128 : j * 128 + 64]
            htj = lambda hh: ht_t[hr(hh), (j * CB + c) * 64 :
                                  (j * CB + c) * 64 + 64]
            qfj = lambda hh: qf_t[hr(hh), (j * CB + c) * 64 :
                                  (j * CB + c) * 64 + 64]
            for hh in range(2):
                e.matmul(bk_[hr(hh), co : co + 128], htj(hh), Mj(hh),
                         start=True, stop=False, tile_position=tp(hh))
                e.matmul(bk_[hr(hh), co : co + 64], btm(j, hh), xSA(j, hh),
                         start=False, stop=False, tile_position=tp(hh))
                e.matmul(bk_[hr(hh), co : co + 64], ktm(j, hh), vtm(j, hh),
                         start=False, stop=True, tile_position=tp(hh))
            for hh in range(2):
                yb = psum[4] if hh == 0 else psum[5]
                e.matmul(yb[hr(hh), j * 64 : j * 64 + 64], xSA(j, hh),
                         MRBT(j, hh),
                         start=True, stop=False, tile_position=tp(hh))
                e.matmul(yb[hr(hh), j * 64 : j * 64 + 64], vtm(j, hh),
                         MRKT(j, hh),
                         start=False, stop=False, tile_position=tp(hh))
                e.matmul(yb[hr(hh), j * 64 : j * 64 + 64], Zj(hh), qfj(hh),
                         start=False, stop=True, tile_position=tp(hh))

    s.on("tensor", ph20)
    s.barrier(waiters=("vector", "scalar", "gpsimd"))

    def ph21v(e):
        e.tensor_add(bufs.mt[:, 0:512], psum[2][:, 0:512], bufs.M[:, 0:512])
        e.tensor_add(bufs.mt[:, 512:1024], psum[3][:, 0:512],
                     bufs.M[:, 512:1024])

    def ph21s(e):
        dst = y_out_sb.rearrange("p (j t) -> p j t", t=y_stride)
        e.activation(dst[0:64, :, y_base + cb : y_base + cb + 64],
                     psum[4][0:64, 0:512].rearrange("p (j u) -> p j u", u=64),
                     AF.Copy)
        e.activation(dst[64:128, :, y_base + cb : y_base + cb + 64],
                     psum[5][64:128, 0:512].rearrange("p (j u) -> p j u", u=64),
                     AF.Copy)

    s.on("vector", ph21v)
    s.on("scalar", ph21s)
    s.barrier(waiters=("vector", "gpsimd"))

    def ph22(e_v, e_g):
        for j in range(J):
            eng = e_v if j < 4 else e_g
            eng.tensor_scalar(
                bufs.M[:, j * 128 : j * 128 + 128],
                bufs.mt[:, j * 128 : j * 128 + 128],
                dl_t[:, j * nch_tot + dl_cb + c : j * nch_tot + dl_cb + c + 1],
                None, ALU.mult)

    s.on("vector", lambda e: ph22(e, e))
    s.barrier(waiters=("tensor",))





class FullBufs:

    def __init__(self, nc, Bt, T):
        assert Bt == 256
        CB = Bt // L
        NCH = T // L
        a = nc.alloc_sbuf_tensor
        self.psum = [nc.alloc_psum_tensor("pb0", [128, 1024], BF16),
                     nc.alloc_psum_tensor("pb1", [128, 1024], BF16)] + [
            nc.alloc_psum_tensor(f"pb{i}", [128, 512], F32) for i in range(2, 8)
        ]
        self.w_r = a("w_r", [128, 8 * 1024], BF16)
        self.w_k = a("w_k", [128, 8 * 1024], BF16)
        self.w_v = a("w_v", [128, 8 * 1024], BF16)
        self.maa_w1 = a("maa_w1", [128, 8 * 192], BF16)
        self.maa_w2a = a("maa_w2a", [128, 1024], BF16)
        self.maa_w2b = a("maa_w2b", [64, 1024], BF16)
        self.dec_w1 = a("dec_w1", [128, 8 * 64], BF16)
        self.dec_w2 = a("dec_w2", [64, 1024], BF16)
        self.aaa_w1 = a("aaa_w1", [128, 8 * 64], BF16)
        self.aaa_w2 = a("aaa_w2", [64, 1024], BF16)
        self.kkk_w1 = a("kkk_w1", [128, 8 * 64], BF16)
        self.kkk_w2 = a("kkk_w2", [64, 1024], BF16)
        self.gate_w1 = a("gate_w1", [128, 8 * 128], BF16)
        self.gate_w2 = a("gate_w2", [128, 1024], BF16)
        for nm in ("maa_x", "maa_r", "maa_w", "maa_k", "maa_v", "maa_a",
                   "maa_g", "td_half", "aaaaa_half", "faaaa", "ln_w", "ln_b",
                   "ffn_bias", "wq_scale"):
            setattr(self, "v_" + nm, a("v_" + nm, [128, 8], F32))
        self.v_scr = a("v_scr", [128, 8], F32)
        self.onehot = a("onehot", [128, 8], F32)
        self.M = a("sc_M", [128, NCC * 128], BF16)
        self.dl_all = a("dl_all", [128, NCC * NCH], F32)
        self.carry = a("sc_carry", [128, NCC], F32)
        self.carry2 = a("sc_carry2", [128, NCC], F32)
        self.offs = a("sc_offs", [128, NCC * CB], F32)
        self.rows = a("rows", [1, 520], F32R)
        self.rows2 = a("rows2", [1, 520], F32R)
        self.rows3 = a("rows3", [2, 2 * Bt], F32R)
        self.rows4 = a("rows4", [2, 2 * Bt], F32R)
        self.dummy = a("sep_dummy", [1, 8], F32)
        self.invrow = a("invrow", [1, Bt], F32R)
        self.sclrow = a("sclrow", [1, Bt], F32R)
        self.amax = a("amax", [128, 2], F32R)
        self.amax2 = a("amax2", [128, 2], F32R)
        self.mid64 = a("mid64", [64, Bt], BF16)
        self.mid128 = a("mid128", [128, Bt], BF16)
        self.mid2_64 = a("mid2_64", [64, Bt], BF16)
        self.mid2_128 = a("mid2_128", [128, Bt], BF16)
        self.zi = a("zi", [128, NCC * 64], BF16)
        self.cum = a("cum", [128, NCC * 64], BF16)
        self.ssel = a("ssel", [128, NCC * 64], BF16)
        self.pfx = a("pfx", [128, NCC * 64], BF16)

        ARENA = 114432
        self.arena = a("arena", [128, ARENA], mybir.dt.uint8)
        base = nc.lookup_mloc(self.arena).addr

        def at(name, shape, dtype, off):
            return nc.alloc_sbuf_tensor_at(name, shape, dtype,
                                           offset=base + off)

        W1 = Bt + 2
        O = {}
        o = 0

        def reg(key, size):
            nonlocal o
            O[key] = o
            o += (size + 255) // 256 * 256

        reg("A", 8448)
        reg("B", 8448)
        reg("C", 8192)
        reg("D", 8192)
        reg("E", 8192)
        reg("F", 8192)
        reg("G", 4352)
        reg("H", 4096)
        reg("I", 4096)
        reg("J", 4096)
        reg("K", 4096)
        reg("L", 8192)
        reg("Mr", 4096)
        reg("Nr", 8192)
        reg("Or", 2816)
        reg("Pr", 16384)
        reg("Q", 4096)
        assert o <= ARENA, o

        self.x_blk = at("x_blk", [128, NCC * W1], F32R, O["A"])
        self.r_p = at("r_p", [128, NCC * Bt], BF16, O["A"])
        self.ktil = at("ktil", [128, NCC * Bt], BF16, O["A"] + 4224)
        self.ln1 = at("ln1", [128, NCC * W1], F32, O["B"])
        self.kk_f = at("kk_f", [128, NCC * Bt], F32R, O["B"])
        self.xx = at("xx", [128, NCC * Bt], F32R, O["C"])
        self.kkn = at("kkn", [128, NCC * Bt], BF16, O["C"])
        self.b_pre = at("b_pre", [128, NCC * Bt], BF16, O["C"] + 4096)
        self.th = at("th", [128, NCC * Bt], F32, O["D"])
        self.coef_blk = at("coef_blk", [2, NCC * Bt], F32, O["D"])
        self.u_s = at("u_s", [128, NCC * Bt], F32, O["E"])
        self.lnq = self.u_s
        self.rkf = at("rkf", [128, NCC * Bt], F32R, O["E"])
        self.src = at("src", [128, NCC * Bt], F32, O["F"])
        self.sq = at("sq", [128, NCC * W1], BF16, O["G"])
        self.xc = at("xc", [128, NCC * Bt], BF16, O["H"])
        self.kmul = at("kmul", [128, NCC * Bt], BF16, O["I"])
        self.a2n = at("a2n", [128, NCC * Bt], BF16, O["J"])
        self.g_p = at("g_p", [128, NCC * Bt], BF16, O["K"])
        self.slog = at("slog", [128, NCC * Bt], F32, O["L"])
        self.y0_blk = at("y0_blk", [128, NCC * Bt], BF16, O["L"])
        self.qf_blk = at("qf_blk", [128, NCC * CB * 64], BF16, O["L"] + 4096)
        self.ep = at("ep", [128, NCC * Bt], BF16, O["Mr"])
        self.ht_blk = at("ht_blk", [128, NCC * CB * 64], BF16, O["Mr"])
        self.em = at("em", [128, NCC * Bt], BF16, O["Nr"])
        self.eprev = at("eprev", [128, NCC * Bt], BF16, O["Nr"] + 4096)
        self.tm = at("sc_tm", [128, 192], BF16, O["Or"])
        self.X = at("sc_X", [128, 128], BF16, O["Or"] + 384)
        self.g1 = at("sc_g1", [128, 256], BF16, O["Or"] + 640)
        self.P = at("sc_P", [128, 704], BF16, O["Or"] + 1152)
        self.HT = at("sc_HT", [128, 64], BF16, O["Or"] + 2560)
        self.Qf = at("sc_Qf", [128, 64], BF16, O["Or"] + 2688)
        self.pfx_all = at("pfx_all", [128, 8 * 512], BF16, O["A"])
        self.ar = at("sc_ar", [128, NCC * 2 * Bt], BF16, O["Pr"])
        self.bk = at("sc_bk", [128, NCC * 2 * Bt], BF16, O["Pr"] + 8192)
        self.vf = at("sc_vf", [128, NCC * Bt], BF16, O["Q"])
        self.TMb = at("sc_tmb", [128, 8 * 192], BF16, O["A"])
        self.Xb = at("sc_xb", [128, 8 * 128], BF16, O["A"] + 3072)
        self.Gb0 = at("sc_gb0", [128, 8 * 64], BF16, O["A"] + 5120)
        self.GTb1 = at("sc_gtb1", [128, 8 * 64], BF16, O["A"] + 6144)
        self.g1b = at("sc_g1b", [128, 8 * 256], BF16, O["B"])
        self.mt = at("sc_mt", [128, 1024], F32, O["B"] + 4096)
        self.Gb14 = at("sc_gb14", [128, 8 * 256], BF16, O["C"])
        self.GTb25 = at("sc_gtb25", [128, 8 * 256], BF16, O["C"] + 4096)
        self.v_p = self.vf
        self.Mg = at("Mg", [128, 8 * 1024], BF16, O["Pr"])
        self.wq_bf = at("wq_bf", [128, 4096], BF16, O["Pr"])
        self.ffn_stage = nc.alloc_sbuf_tensor_at(
            "ffn_stage", [128, 4096], F32,
            offset=nc.lookup_mloc(self.w_v).addr)
        self.w_o = nc.alloc_sbuf_tensor_at(
            "w_o", [128, 8 * 1024], BF16, offset=nc.lookup_mloc(self.w_r).addr)
        self.wq_t = nc.alloc_sbuf_tensor_at(
            "wq_t", [128, 8 * 1024], BF16,
            offset=nc.lookup_mloc(self.w_k).addr)
        self.x_blk2 = at("x_blk2", [128, NCC * Bt], F32R, O["A"])
        self.x3 = at("x3", [128, NCC * Bt], F32, O["A"])
        self.x2 = at("x2", [128, NCC * Bt], F32R, O["B"])
        self.att_s = at("att_s", [128, NCC * Bt], F32R, O["C"])
        self.ln3 = at("ln3", [128, NCC * Bt], F32R, O["C"])
        self.tmp1 = at("tmp_e1", [128, NCC * Bt], F32R, O["E"])
        self.tmp2 = at("tmp_e2", [128, NCC * Bt], F32R, O["F"])
        self.sqy = at("sqy", [128, NCC * Bt], BF16, O["G"])
        self.xq = at("xq", [128, NCC * Bt], BF16, O["H"])
        self.yg = at("yg", [128, NCC * Bt], BF16, O["I"])
        self.coef_p2 = at("coef_p2", [2, NCC * Bt], F32, O["Nr"])
        self.rjA = at("rjA", [2, 2048], F32R, O["D"])
        self.rjB1 = at("rjB1", [2, 1024], F32R, O["Mr"])
        self.rjB2 = at("rjB2", [2, 1024], F32R, O["J"])
        self.rjm = at("rjm", [2, 1024], F32R, O["H"])
        self.rjm2 = at("rjm2", [2, 1024], F32R, O["I"])
        self.y_blk = at("y_blk", [128, NCC * Bt], F32R, O["Pr"])
        self.ffo = at("ffo", [128, NCC * Bt], F32R, O["Pr"] + 8192)


class FullConsts(ConstSB):
    def __init__(self, nc, s):
        super().__init__(nc, s)
        a = nc.alloc_sbuf_tensor
        self.ones_col_r = a("cs_onescol_r", [128, 1], F32R)
        self.ones_col2_r = a("cs_onescol2_r", [128, 2], F32R)
        self.onesrow_r = a("cs_onesrow_r", [1, 128], F32R)
        self.negrow_r = a("cs_negrow_r", [1, 128], F32R)
        self.ones2_128 = a("cs_ones2_128", [2, 128], F32R)
        self.i128_r = a("cs_i128_r", [128, 128], F32R)
        self.i128_b = a("cs_i128_b", [128, 128], BF16)
        self.ones_col_b = a("cs_onescol_b", [128, 1], BF16)
        self.ones_col2_b = a("cs_onescol2_b", [128, 2], BF16)
        self.ones2_128f = a("cs_ones2_128f", [2, 128], F32)
        self.eps1 = a("cs_eps1", [128, 1], F32)
        self.eps64 = a("cs_eps64", [128, 1], F32)
        self.mask_g2 = a("cs_maskg2", [128, 512], BF16)
        self.mask_sl8 = a("cs_masksl8", [128, 512], BF16)

    def declare_inputs(self, nc):
        super().declare_inputs(nc)
        self.d_ones_col = nc.dram_tensor("c_ones_col", [128, 1], F32,
                                         kind="ExternalInput").ap()
        self.d_ones_col2 = nc.dram_tensor("c_ones_col2", [128, 2], F32,
                                          kind="ExternalInput").ap()
        self.d_onesrow = nc.dram_tensor("c_onesrow", [1, 128], F32,
                                        kind="ExternalInput").ap()
        self.d_negrow = nc.dram_tensor("c_negrow", [1, 128], F32,
                                       kind="ExternalInput").ap()
        self.d_ones2_128 = nc.dram_tensor("c_ones2_128", [2, 128], F32,
                                          kind="ExternalInput").ap()
        self.d_i128 = nc.dram_tensor("c_i128", [128, 128], F32,
                                     kind="ExternalInput").ap()
        self.d_eps1 = nc.dram_tensor("c_eps1", [128, 1], F32,
                                     kind="ExternalInput").ap()
        self.d_eps64 = nc.dram_tensor("c_eps64", [128, 1], F32,
                                      kind="ExternalInput").ap()
        self.d_mask_g2 = nc.dram_tensor("c_mask_g2", [128, 512], F32,
                                        kind="ExternalInput").ap()
        self.d_mask_sl8 = nc.dram_tensor("c_mask_sl8", [128, 512], F32,
                                         kind="ExternalInput").ap()

    def emit_load(self, s):
        super().emit_load(s)
        s.dma("gpsimd", "cloadg", self.ones_col_r[:], self.d_ones_col)
        s.dma("gpsimd", "cloadg", self.ones_col2_r[:], self.d_ones_col2)
        s.dma("gpsimd", "cloadg", self.onesrow_r[:], self.d_onesrow)
        s.dma("gpsimd", "cloadg", self.negrow_r[:], self.d_negrow)
        s.dma("gpsimd", "cloadg", self.ones2_128[:], self.d_ones2_128)
        s.dma("gpsimd", "cloadg", self.i128_r[:], self.d_i128)
        s.dma("gpsimd", "cloadg", self.i128_b[:], self.d_i128)
        s.dma("gpsimd", "cloadg", self.ones_col_b[:], self.d_ones_col)
        s.dma("gpsimd", "cloadg", self.ones_col2_b[:], self.d_ones_col2)
        s.dma("sync", "cload", self.ones2_128f[:], self.d_ones2_128)
        s.dma("sync", "cload", self.eps1[:], self.d_eps1)
        s.dma("sync", "cload", self.eps64[:], self.d_eps64)
        s.dma("gpsimd", "cloadg", self.mask_g2[:], self.d_mask_g2)
        s.dma("gpsimd", "cloadg", self.mask_sl8[:], self.d_mask_sl8)


def full_const_arrays():
    c = make_const_arrays()
    oc = np.ones((128, 1), np.float32)
    oc2 = np.zeros((128, 2), np.float32)
    oc2[0:64, 0] = 1.0
    oc2[64:128, 1] = 1.0
    orow = np.ones((1, 128), np.float32)
    o2 = np.zeros((2, 128), np.float32)
    o2[0, 0:64] = 1.0
    o2[1, 64:128] = 1.0
    c.update({
        "ones_col": oc, "ones_col2": oc2, "onesrow": orow,
        "negrow": -orow, "ones2_128": o2, "i128": np.eye(128, dtype=np.float32),
        "eps1": np.full((128, 1), 1e-5, np.float32),
        "eps64": np.full((128, 1), 64e-5, np.float32),
        "mask_g2": np.concatenate([c["mask_g1"], c["mask_g1"]], axis=1),
        "mask_sl8": np.tile(c["mask_sl"], (1, 8)),
    })
    return c


MAGIC = float(np.float32(1.5 * 2 ** 23))


def emit_ln_stats(s, bufs, psum, consts, src, sq, W, eps, rm_off=0):
    rows, rows2 = bufs.rows, bufs.rows2

    def dosq(e):
        for j in range(NCC):
            e.activation(sq[:, j * W : (j + 1) * W], src[:, j * W : (j + 1) * W],
                         AF.Square)

    s.on("scalar", dosq)
    s.barrier(waiters=("tensor",))

    def sums(e):
        for j in range(NCC):
            e.matmul(psum[5][0:1, 0:W], consts.ones_col_r[:],
                     src[:, j * W : (j + 1) * W],
                     start=(j == 0), stop=(j == NCC - 1))
        for j in range(NCC):
            e.matmul(psum[7][0:1, 0:W], consts.ones_col_b[:],
                     sq[:, j * W : (j + 1) * W],
                     start=(j == 0), stop=(j == NCC - 1))

    s.on("tensor", sums)
    s.barrier(waiters=("vector", "scalar"))

    ro = rm_off

    def rowops1(e):
        e.tensor_scalar_mul(rows[0:1, ro : ro + W], psum[5][0:1, 0:W],
                            1.0 / 1024.0)

    s.on("vector", rowops1)
    s.barrier(waiters=("vector",))

    s.on("vector", lambda e: e.tensor_mul(
        rows[0:1, ro + W : ro + 2 * W], rows[0:1, ro : ro + W],
        rows[0:1, ro : ro + W]))
    s.barrier(waiters=("vector",))

    def rowops2(e):
        e.scalar_tensor_tensor(
            rows2[0:1, ro + W : ro + 2 * W],
            rows[0:1, ro + W : ro + 2 * W].bitcast(F32), -1024.0,
            psum[7][0:1, 0:W], ALU.mult, ALU.add)

    s.on("vector", rowops2)
    s.barrier(waiters=("scalar",))

    s.on("scalar", lambda e: e.activation(
        rows[0:1, ro + W : ro + 2 * W], rows2[0:1, ro + W : ro + 2 * W],
        AF.Ln, scale=1.0 / 1024.0, bias=eps[0:1, 0:1]))
    s.barrier(waiters=("scalar",))
    s.on("scalar", lambda e: e.activation(
        rows2[0:1, ro + W : ro + 2 * W], rows[0:1, ro + W : ro + 2 * W],
        AF.Exp, scale=-0.5))
    s.barrier(waiters=("tensor",))

    def bcasts(e):
        e.matmul(psum[2][:, 0:W], consts.onesrow_r[:],
                 rows2[0:1, ro + W : ro + 2 * W], start=True, stop=True)
        e.matmul(psum[6][:, 0:W], consts.onesrow_r[:],
                 rows[0:1, ro : ro + W], start=True, stop=True)

    s.on("tensor", bcasts)
    s.barrier(waiters=("vector", "gpsimd"))


def emit_ln_apply(s, src, out, psum, W, extra_add=None, out2=None):

    def ap1(e):
        for j in range(NCC):
            e.tensor_sub(out[:, j * W : (j + 1) * W],
                         src[:, j * W : (j + 1) * W], psum[6][:, 0:W])

    s.on("vector", ap1)
    s.barrier(waiters=("vector",))

    def ap2(e):
        for j in range(NCC):
            e.tensor_mul(out[:, j * W : (j + 1) * W],
                         out[:, j * W : (j + 1) * W], psum[2][:, 0:W])

    s.on("vector", ap2)
    s.barrier(waiters=("vector", "gpsimd"))
    if extra_add is not None:
        def ap3(e):
            for j in range(NCC):
                e.tensor_add((out2 or out)[:, j * W : (j + 1) * W],
                             out[:, j * W : (j + 1) * W],
                             extra_add[:, j * W : (j + 1) * W])

        s.on("vector", ap3)
        s.barrier()


def emit_proj_1024(s, psum, w_sb, act, Bt, evac, banks=(7, 2, 3, 6),
                   open_group=False, evac_engine=None):

    regions = []
    for o in range(8):
        bank = psum[banks[o % 4]]
        half = (o // 4) * Bt
        regions.append(bank[:, half : half + Bt])

    def mmhalf(e, os_):
        for o in os_:
            for i in range(8):
                e.matmul(regions[o],
                         w_sb[:, i * 1024 + o * 128 : i * 1024 + o * 128 + 128],
                         act[:, i * Bt : (i + 1) * Bt],
                         start=(i == 0),
                         stop=(i == 7 and not open_group))

    if evac_engine is not None:
        s.on("tensor", lambda e: mmhalf(e, range(0, 4)))
        s.barrier(waiters=(evac_engine,))

        def evh(e, os_):
            for o in os_:
                evac(e, o, regions[o])

        s.on("tensor", lambda e: mmhalf(e, range(4, 8)))
        s.on(evac_engine, lambda e: evh(e, range(0, 4)))
        s.barrier(waiters=(evac_engine,))
        s.on(evac_engine, lambda e: evh(e, range(4, 8)))
        s.barrier()
        return regions, None

    s.on("tensor", lambda e: mmhalf(e, range(8)))
    s.barrier(waiters=("vector", "scalar", "gpsimd"))

    if evac is None:
        return regions, None

    def evacs(e):
        for o in range(8):
            evac(e, o, regions[o])

    return regions, evacs


def emit_small_path(s, psum, w1_sb, w2_sb, act, Bt, mid, tanh_bias, evac,
                    accum_regions=None, tanh_scale=1.0, mid_sb=None,
                    evac_engine="scalar"):
    def mm1(e):
        for i in range(8):
            e.matmul(psum[5][0:mid, 0:Bt],
                     w1_sb[:, i * mid : (i + 1) * mid],
                     act[:, i * Bt : (i + 1) * Bt],
                     start=(i == 0), stop=(i == 7))

    s.on("tensor", mm1)
    s.barrier(waiters=("scalar",))

    def tev(e):
        kw = {}
        if tanh_bias is not None:
            kw["bias"] = tanh_bias
        e.activation(mid_sb[0:mid, 0:Bt], psum[5][0:mid, 0:Bt], AF.Tanh,
                     scale=tanh_scale, **kw)

    s.on("scalar", tev)
    s.barrier(waiters=("tensor",))

    if accum_regions is not None:
        def mm2(e):
            for o in range(8):
                e.matmul(accum_regions[o],
                         w2_sb[0:mid, o * 128 : o * 128 + 128],
                         mid_sb[0:mid, 0:Bt],
                         start=False, stop=True)

        s.on("tensor", mm2)
        s.barrier(waiters=("vector", "scalar", "gpsimd"))
        return None
    else:
        regions = []
        for o in range(8):
            bank = psum[(7, 2, 3, 6)[o % 4]]
            half = (o // 4) * Bt
            regions.append(bank[:, half : half + Bt])

        def mm2(e):
            for o in range(8):
                e.matmul(regions[o],
                         w2_sb[0:mid, o * 128 : o * 128 + 128],
                         mid_sb[0:mid, 0:Bt],
                         start=True, stop=True)

        s.on("tensor", mm2)
        s.barrier(waiters=("vector", "scalar", "gpsimd"))

        def evacs(e):
            for o in range(8):
                evac(e, o, regions[o])

        s.on(evac_engine, evacs)
        s.barrier()
        return None


"""Full sandwich-module per-core program builder."""
import os

CEXP = float(np.exp(-0.5))

MAA_NAMES = ("maa_r", "maa_w", "maa_k", "maa_v", "maa_a", "maa_g")
S_ORDER = (1, 4, 5, 0, 3, 2)


def declare_io(nc, T, ncores=8):
    io = {}
    d = lambda nm, shape, dt=F32, kind="ExternalInput": nc.dram_tensor(
        nm, shape, dt, kind=kind
    ).ap()
    io["x"] = d("x_in", [1024, T + 1])
    io["W_r"] = d("W_r", [1024, 1024])
    io["W_k"] = d("W_k", [1024, 1024])
    io["W_v"] = d("W_v", [1024, 1024])
    io["W_o"] = d("W_o", [1024, 1024])
    io["W_ffn"] = d("W_ffn", [1024, 1024])
    io["maa_w1"] = d("maa_w1_d", [1024, 192])
    io["maa_w2a"] = d("maa_w2a_d", [128, 1024])
    io["maa_w2b"] = d("maa_w2b_d", [64, 1024])
    io["dec_w1"] = d("dec_w1_d", [1024, 64])
    io["dec_w2"] = d("dec_w2_d", [64, 1024])
    io["aaa_w1"] = d("aaa_w1_d", [1024, 64])
    io["aaa_w2"] = d("aaa_w2_d", [64, 1024])
    io["kkk_w1"] = d("kkk_w1_d", [1024, 64])
    io["kkk_w2"] = d("kkk_w2_d", [64, 1024])
    io["gate_w1"] = d("gate_w1_d", [1024, 128])
    io["gate_w2"] = d("gate_w2_d", [128, 1024])
    for nm in ("maa_x",) + MAA_NAMES + ("td", "aaaaa", "faaaa", "ln_w",
                                        "ln_b", "ffn_bias"):
        io[nm] = d("v_" + nm + "_d", [128, 8])
    io["onehot"] = d("onehot_d", [128, 8])
    io["out"] = d("x_out", [1024, T], F32, kind="ExternalOutput")
    NCH = T // L
    io["g_hbm"] = nc.dram_tensor("g_hbm", [128, NCC * T], BF16)
    io["v_hbm"] = nc.dram_tensor("v_hbm", [128, NCC * T], BF16)
    io["y0_hbm"] = nc.dram_tensor("y0_hbm", [128, NCC * T], BF16)
    io["qf_hbm"] = nc.dram_tensor("qf_hbm", [128, NCC * NCH * 64], BF16)
    io["ht_hbm"] = nc.dram_tensor("ht_hbm", [128, NCC * NCH * 64], BF16)
    io["coef_hbm"] = nc.dram_tensor("coef_hbm", [2, NCC * T], F32)
    io["agin"] = nc.dram_tensor("agin", [128, 1024], BF16)
    io["agout"] = nc.dram_tensor("agout", [ncores, 128, 1024], BF16)
    io["ncores"] = ncores
    return io


def _blkslice(ap, T, bB, Bt):
    return ap.rearrange("p (j t) -> p j t", j=NCC)[:, :, bB : bB + Bt]


def emit_weight_load(s, nc, bufs, consts, io):
    consts.emit_load(s)
    g = lambda out, in_: s.dma("gpsimd", "wload", out, in_, defer=True)

    def re_w(ap):
        return ap.rearrange("(i p) c -> p i c", p=128)

    def re_o(t):
        return t[:].rearrange("p (i c) -> p i c", i=8)

    g(re_o(bufs.w_r), re_w(io["W_r"]))
    g(re_o(bufs.w_k), re_w(io["W_k"]))
    g(re_o(bufs.w_v), re_w(io["W_v"]))
    g(bufs.maa_w1[:].rearrange("p (i c) -> p i c", i=8),
      io["maa_w1"].rearrange("(i p) c -> p i c", p=128))
    g(bufs.maa_w2a[:], io["maa_w2a"])
    g(bufs.maa_w2b[:], io["maa_w2b"])
    for nm in ("dec_w1", "aaa_w1", "kkk_w1"):
        g(getattr(bufs, nm)[:].rearrange("p (i c) -> p i c", i=8),
          io[nm].rearrange("(i p) c -> p i c", p=128))
    for nm in ("dec_w2", "aaa_w2", "kkk_w2", "gate_w2"):
        g(getattr(bufs, nm)[:], io[nm])
    g(bufs.gate_w1[:].rearrange("p (i c) -> p i c", i=8),
      io["gate_w1"].rearrange("(i p) c -> p i c", p=128))
    for nm in ("maa_x",) + MAA_NAMES + ("faaaa", "ln_w", "ln_b", "ffn_bias"):
        s.dma("sync", "vload", getattr(bufs, "v_" + nm)[:], io[nm])
    s.dma("sync", "vload", bufs.v_td_half[:], io["td"])
    s.dma("sync", "vload", bufs.v_aaaaa_half[:], io["aaaaa"])
    s.dma("sync", "vload", bufs.onehot[:], io["onehot"])
    s.barrier()

    def halve(e):
        e.tensor_scalar_mul(bufs.v_td_half[:], bufs.v_td_half[:], 0.5)
        e.tensor_scalar_mul(bufs.v_aaaaa_half[:], bufs.v_aaaaa_half[:], 0.5)
        e.memset(bufs.dummy[:], 0.0)

    s.on("vector", halve)
    s.barrier()


def emit_prep2(s, nc, bufs, consts, io):
    psum = bufs.psum
    s.dma("gpsimd", "wload2", bufs.w_o[:].rearrange("p (i c) -> p i c", i=8),
          io["W_o"].rearrange("(i p) c -> p i c", p=128))
    for hhalf in range(2):
        s.barrier()
        s.dma("sync", "wload2s",
              bufs.ffn_stage[:].rearrange("p (q c) -> p q c", q=4),
              io["W_ffn"].rearrange("(q p) c -> p q c", p=128)[
                  :, hhalf * 4 : hhalf * 4 + 4, :])
        s.barrier()

        def amax(e, hhalf=hhalf):
            for q in range(4):
                e.reduce_max(
                    bufs.v_wq_scale[:, hhalf * 4 + q : hhalf * 4 + q + 1],
                    bufs.ffn_stage[:, q * 1024 : (q + 1) * 1024],
                    mybir.AxisListType.X,
                    apply_absolute_value=True,
                )

        s.on("vector", amax)
        s.barrier(waiters=("vector",))

        def winv(e, hhalf=hhalf):
            for q in range(4):
                e.reciprocal(
                    bufs.v_scr[:, q : q + 1],
                    bufs.v_wq_scale[:, hhalf * 4 + q : hhalf * 4 + q + 1],
                )

        s.on("vector", winv)
        s.barrier(waiters=("vector",))

        def wq1(e):
            for q in range(4):
                e.tensor_scalar_mul(bufs.v_scr[:, q : q + 1],
                                    bufs.v_scr[:, q : q + 1], 127.0)

        s.on("vector", wq1)
        s.barrier(waiters=("vector",))

        def wq2(e):
            for q in range(4):
                e.tensor_scalar(
                    bufs.ffn_stage[:, q * 1024 : (q + 1) * 1024],
                    bufs.ffn_stage[:, q * 1024 : (q + 1) * 1024],
                    bufs.v_scr[:, q : q + 1], MAGIC, ALU.mult, ALU.add,
                )

        s.on("vector", wq2)
        s.barrier(waiters=("vector",))

        def wq3(e):
            for q in range(4):
                e.tensor_scalar_add(
                    bufs.ffn_stage[:, q * 1024 : (q + 1) * 1024],
                    bufs.ffn_stage[:, q * 1024 : (q + 1) * 1024], -MAGIC,
                )

        s.on("vector", wq3)
        s.barrier(waiters=("vector",))

        def wqc(e):
            for q in range(4):
                e.tensor_copy(bufs.wq_bf[:, q * 1024 : (q + 1) * 1024],
                              bufs.ffn_stage[:, q * 1024 : (q + 1) * 1024])

        s.on("vector", wqc)
        s.barrier(waiters=("tensor",))

        for qp in range(2):
            def wtg(e, qp=qp):
                for dq in range(2):
                    q = qp * 2 + dq
                    for ii in range(8):
                        e.transpose(
                            bufs.psum[dq][:, ii * 128 : ii * 128 + 128],
                            bufs.wq_bf[:, q * 1024 + ii * 128 :
                                       q * 1024 + ii * 128 + 128],
                            consts.i128_b[:],
                        )

            s.on("tensor", wtg)
            s.barrier(waiters=("vector",))

            def wte(e, qp=qp, hhalf=hhalf):
                for dq in range(2):
                    q = qp * 2 + dq
                    co = hhalf * 4 + q
                    d3 = bufs.wq_t[:].rearrange(
                        "p (i c) -> p i c", i=8)[:, :, co * 128 : co * 128 + 128]
                    s3 = bufs.psum[dq][:, 0:1024].rearrange(
                        "p (i c) -> p i c", i=8)
                    e.tensor_copy(d3, s3)

            s.on("vector", wte)
            s.barrier(waiters=("tensor",))


    def wsc(e):
        e.tensor_scalar_mul(bufs.v_wq_scale[:], bufs.v_wq_scale[:],
                            1.0 / 127.0)

    s.on("vector", wsc)
    s.barrier()


def emit_mix(s, bufs, consts, Bt, sx, ln1b, co=None):
    co = co or {}
    psum = bufs.psum
    if sx < 4:
        lhsw, rbase = bufs.maa_w2a, 32 * sx
        rhs_mid = bufs.mid128[32 * sx : 32 * sx + 32, 0:Bt]
    else:
        lhsw, rbase = bufs.maa_w2b, 32 * (sx - 4)
        rhs_mid = bufs.mid64[32 * (sx - 4) : 32 * (sx - 4) + 32, 0:Bt]

    regions = []
    for o in range(8):
        bank = psum[(7, 2, 3, 6)[o % 4]]
        half = (o // 4) * Bt
        regions.append(bank[:, half : half + Bt])

    def msmm(e, lhsw=lhsw, rhs_mid=rhs_mid, rbase=rbase):
        for o in range(8):
            e.matmul(regions[o],
                     lhsw[rbase : rbase + 32, o * 128 : o * 128 + 128],
                     rhs_mid, start=True, stop=True,
                     tile_position=(rbase, 0))

    s.on("tensor", msmm)
    for eng, fn in co.get(1, ()):
        s.on(eng, fn)
    s.barrier(waiters=("vector", "scalar") if co else ("vector",))

    vmaa = getattr(bufs, "v_" + MAA_NAMES[sx])

    def msev(e, vmaa=vmaa):
        for o in range(8):
            e.scalar_tensor_tensor(
                bufs.u_s[:, o * Bt : (o + 1) * Bt], regions[o],
                vmaa[:, o : o + 1],
                bufs.xx[:].bitcast(F32)[:, o * Bt : (o + 1) * Bt],
                ALU.add, ALU.mult,
            )

    s.on("vector", msev)
    for eng, fn in co.get(2, ()):
        s.on(eng, fn)
    s.barrier(waiters=("gpsimd", "scalar") if co else ("gpsimd",))

    def msadd(e):
        for o in range(8):
            e.tensor_add(bufs.xc[:, o * Bt : (o + 1) * Bt],
                         bufs.u_s[:, o * Bt : (o + 1) * Bt], ln1b(o))

    s.on("gpsimd", msadd)
    for eng, fn in co.get(3, ()):
        s.on(eng, fn)
    s.barrier(waiters=("tensor", "vector", "scalar"))


def emit_pass1_block(s, nc, bufs, consts, io, T, Bt, b):
    psum = bufs.psum
    CB = Bt // L
    NCH = T // L
    W1 = Bt + 2
    bB = b * Bt

    s.dma("gpsimd", "xload",
          bufs.x_blk[:].rearrange("p (j t) -> p j t", j=NCC)[:, :, 0 : Bt + 1],
          io["x"].rearrange("(j p) t -> p j t", p=128)[:, :, bB : bB + Bt + 1])
    s.on("vector", lambda e: e.memset(bufs.x_blk[:].bitcast(F32).rearrange(
        "p (j t) -> p j t", j=NCC)[:, :, Bt + 1 : Bt + 2], 0.0))
    s.barrier()

    emit_ln_stats(s, bufs, psum, consts, bufs.x_blk[:], bufs.sq[:], W1, consts.eps1)
    emit_ln_apply(s, bufs.x_blk[:].bitcast(F32), bufs.ln1[:], psum, W1)

    def mkxx(e):
        for j in range(NCC):
            e.tensor_sub(bufs.xx[:, j * Bt : (j + 1) * Bt].bitcast(F32),
                         bufs.ln1[:, j * W1 : j * W1 + Bt],
                         bufs.ln1[:, j * W1 + 1 : j * W1 + Bt + 1])

    s.on("vector", mkxx)
    s.barrier(waiters=("vector",))

    ln1b = lambda j: bufs.ln1[:, j * W1 + 1 : j * W1 + Bt + 1]

    def mkxmix(e):
        for j in range(NCC):
            e.scalar_tensor_tensor(
                bufs.xc[:, j * Bt : (j + 1) * Bt],
                bufs.xx[:].bitcast(F32)[:, j * Bt : (j + 1) * Bt],
                bufs.v_maa_x[:, j : j + 1], ln1b(j), ALU.mult, ALU.add,
            )

    s.on("vector", mkxmix)
    if b == 0:
        s.dma_commit("wload")
    s.barrier(waiters=("tensor",))

    def mixmm(e):
        for i in range(8):
            e.matmul(psum[5][0:128, 0:Bt],
                     bufs.maa_w1[:, i * 192 : i * 192 + 128],
                     bufs.xc[:, i * Bt : (i + 1) * Bt],
                     start=(i == 0), stop=(i == 7))
        for i in range(8):
            e.matmul(psum[4][0:64, 0:Bt],
                     bufs.maa_w1[:, i * 192 + 128 : i * 192 + 192],
                     bufs.xc[:, i * Bt : (i + 1) * Bt],
                     start=(i == 0), stop=(i == 7))

    s.on("tensor", mixmm)
    s.barrier(waiters=("scalar",))

    def mixev(e):
        e.activation(bufs.mid128[:, 0:Bt], psum[5][0:128, 0:Bt], AF.Tanh)
        e.activation(bufs.mid64[:, 0:Bt], psum[4][0:64, 0:Bt], AF.Tanh)

    s.on("scalar", mixev)
    s.barrier(waiters=("tensor",))

    co_pending = None
    for sx in S_ORDER:
        emit_mix(s, bufs, consts, Bt, sx, ln1b, co=co_pending)
        co_pending = None
        if sx == 1:
            def decev(e, o, region):
                e.activation(bufs.th[:, o * Bt : (o + 1) * Bt], region,
                             AF.Tanh, scale=0.5,
                             bias=bufs.v_td_half[:, o : o + 1])

            emit_small_path(s, psum, bufs.dec_w1, bufs.dec_w2, bufs.xc, Bt,
                            64, None, decev, mid_sb=bufs.mid2_64)

            def mksrc(e):
                for j in range(NCC):
                    e.tensor_scalar(
                        bufs.src[:, j * Bt : (j + 1) * Bt],
                        bufs.th[:, j * Bt : (j + 1) * Bt],
                        1.0, CEXP / 2.0, ALU.add, ALU.mult,
                    )

            def mklnq(e):
                for j in range(NCC):
                    e.activation(bufs.th[:, j * Bt : (j + 1) * Bt],
                                 bufs.src[:, j * Bt : (j + 1) * Bt], AF.Ln)

            def mkkmul(e):
                for j in range(NCC):
                    e.activation(bufs.kmul[:, j * Bt : (j + 1) * Bt],
                                 bufs.th[:, j * Bt : (j + 1) * Bt],
                                 AF.Exp, scale=0.5)

            co_pending = {1: [("vector", mksrc)], 2: [("scalar", mklnq)],
                          3: [("scalar", mkkmul)]}
        elif sx == 4:
            def aaaev(e, o, region):
                e.activation(bufs.u_s[:, o * Bt : (o + 1) * Bt], region,
                             AF.Tanh, scale=0.5,
                             bias=bufs.v_aaaaa_half[:, o : o + 1])

            emit_small_path(s, psum, bufs.aaa_w1, bufs.aaa_w2, bufs.xc, Bt,
                            64, None, aaaev, mid_sb=bufs.mid2_64)

            def mka2n(e):
                for j in range(NCC):
                    e.tensor_scalar(
                        bufs.a2n[:, j * Bt : (j + 1) * Bt],
                        bufs.u_s[:, j * Bt : (j + 1) * Bt],
                        1.0, -1.0, ALU.add, ALU.mult,
                    )

            co_pending = {1: [("vector", mka2n)]}
        elif sx == 5:
            def gev(e, o, region):
                e.tensor_copy(bufs.g_p[:, o * Bt : (o + 1) * Bt], region)

            emit_small_path(s, psum, bufs.gate_w1, bufs.gate_w2, bufs.xc, Bt,
                            128, None, gev, mid_sb=bufs.mid2_128,
                            evac_engine="vector")
            s.dma("sync", "spill", _blkslice(io["g_hbm"].ap(), T, bB, Bt),
                  bufs.g_p[:].rearrange("p (j t) -> p j t", j=NCC))
        elif sx == 0:
            def rev(e, o, region):
                e.tensor_copy(bufs.r_p[:, o * Bt : (o + 1) * Bt], region)

            _, evr = emit_proj_1024(s, psum, bufs.w_r, bufs.xc, Bt, rev)
            s.on("vector", evr)
            s.barrier()
        elif sx == 3:
            def vev(e, o, region):
                e.tensor_copy(bufs.v_p[:, o * Bt : (o + 1) * Bt], region)

            _, evv = emit_proj_1024(s, psum, bufs.w_v, bufs.xc, Bt, vev)
            s.on("vector", evv)
            s.barrier()
            s.dma("sync", "spill", _blkslice(io["v_hbm"].ap(), T, bB, Bt),
                  bufs.v_p[:].rearrange("p (j t) -> p j t", j=NCC))
        elif sx == 2:
            def kkmm1(e):
                for i in range(8):
                    e.matmul(psum[5][0:64, 0:Bt],
                             bufs.kkk_w1[:, i * 64 : (i + 1) * 64],
                             bufs.xc[:, i * Bt : (i + 1) * Bt],
                             start=(i == 0), stop=(i == 7))

            s.on("tensor", kkmm1)
            s.barrier(waiters=("scalar",))
            s.on("scalar", lambda e: e.activation(
                bufs.mid2_64[0:64, 0:Bt], psum[5][0:64, 0:Bt], AF.Tanh))

            kregions, _ = emit_proj_1024(s, psum, bufs.w_k, bufs.xc, Bt,
                                         None)

            def kev(e):
                for o in range(8):
                    e.tensor_mul(bufs.ktil[:, o * Bt : (o + 1) * Bt],
                                 kregions[o],
                                 bufs.kmul[:, o * Bt : (o + 1) * Bt])

            s.on("vector", kev)
            s.barrier(waiters=("tensor",))

            def kacc(e):
                for o in range(8):
                    e.matmul(kregions[o],
                             bufs.kkk_w2[0:64, o * 128 : o * 128 + 128],
                             bufs.mid2_64[0:64, 0:Bt],
                             start=False, stop=True, skip_group_check=True)

            s.on("tensor", kacc)
            s.barrier(waiters=("vector",))

            def kkev(e):
                for o in range(8):
                    e.tensor_copy(bufs.kk_f[:, o * Bt : (o + 1) * Bt],
                                  kregions[o])

            s.on("vector", kkev)
            s.barrier(waiters=("tensor", "scalar"))

            def kksq(e):
                for j in range(NCC):
                    e.activation(bufs.sq[:, j * Bt : (j + 1) * Bt],
                                 bufs.kk_f[:, j * Bt : (j + 1) * Bt],
                                 AF.Square)

            s.on("scalar", kksq)
            s.barrier(waiters=("tensor",))

            def kksum(e):
                for j in range(NCC):
                    e.matmul(psum[5][0:1, 0:Bt], consts.ones_col_b[:],
                             bufs.sq[:, j * Bt : (j + 1) * Bt],
                             start=(j == 0), stop=(j == NCC - 1))

            s.on("tensor", kksum)
            s.barrier(waiters=("scalar",))
            s.on("scalar", lambda e: e.activation(
                bufs.rows[0:1, 0:Bt], psum[5][0:1, 0:Bt], AF.Ln))
            s.barrier(waiters=("scalar",))
            s.on("scalar", lambda e: e.activation(
                bufs.rows2[0:1, 0:Bt], bufs.rows[0:1, 0:Bt], AF.Exp,
                scale=-0.5))
            s.barrier(waiters=("vector",))
            s.on("vector", lambda e: e.tensor_scalar_min(
                bufs.rows2[0:1, 0:Bt], bufs.rows2[0:1, 0:Bt], 1e12))
            s.barrier(waiters=("tensor",))
            s.on("tensor", lambda e: e.matmul(
                psum[2][:, 0:Bt], consts.negrow_r[:], bufs.rows2[0:1, 0:Bt],
                start=True, stop=True))
            s.barrier(waiters=("vector",))

            def mkkkn(e):
                for j in range(NCC):
                    e.tensor_mul(bufs.kkn[:, j * Bt : (j + 1) * Bt],
                                 bufs.kk_f[:, j * Bt : (j + 1) * Bt],
                                 psum[2][:, 0:Bt])

            s.on("vector", mkkkn)
            s.barrier(waiters=("vector", "gpsimd"))

            def mkbpre(e):
                for j in range(NCC):
                    e.tensor_mul(bufs.b_pre[:, j * Bt : (j + 1) * Bt],
                                 bufs.kkn[:, j * Bt : (j + 1) * Bt],
                                 bufs.a2n[:, j * Bt : (j + 1) * Bt])

            s.on("gpsimd", mkbpre)
            s.barrier(waiters=("vector",))

            def mkrkf(e):
                for j in range(NCC):
                    e.scalar_tensor_tensor(
                        bufs.rkf[:, j * Bt : (j + 1) * Bt],
                        bufs.r_p[:, j * Bt : (j + 1) * Bt],
                        bufs.v_faaaa[:, j : j + 1],
                        bufs.ktil[:, j * Bt : (j + 1) * Bt],
                        ALU.mult, ALU.mult,
                    )

            s.on("vector", mkrkf)
            s.barrier(waiters=("tensor",))

            CSLOT = ((4, 0), (5, 0), (6, 0), (7, 0))

            def coefmm(e):
                for q, (bk_, ba) in enumerate(CSLOT):
                    e.matmul(psum[bk_][ba : ba + 2, 0:512],
                             consts.ones_col2_r[:],
                             bufs.rkf[:, q * 512 : q * 512 + 512],
                             start=True, stop=True)

            s.on("tensor", coefmm)
            s.barrier(waiters=("vector",))

            def coefev(e):
                for q, (bk_, ba) in enumerate(CSLOT):
                    e.tensor_copy(bufs.coef_blk[0:2, q * 512 : q * 512 + 512],
                                  psum[bk_][ba : ba + 2, 0:512])

            s.on("vector", coefev)
            s.barrier()
            s.dma("sync", "spill", _blkslice(io["coef_hbm"].ap(), T, bB, Bt),
                  bufs.coef_blk[:].rearrange("p (j t) -> p j t", j=NCC))

    emit_scan_prep(s, bufs, psum, consts, Bt, bufs.src, -1.0,
                   bufs.r_p, bufs.ktil, bufs.vf, bufs.kkn, bufs.b_pre,
                   first_block=(b == 0), dl_t=bufs.dl_all, dl_cb=b * CB,
                   nch_tot=NCH, blk_par=b % 2)
    (emit_scan_block_v2_old if os.environ.get("SCAN_OLD") else
     emit_scan_block_v2)(s, bufs, psum, consts, Bt, bufs.y0_blk,
                    first_block=(b == 0), y_base=0, y_stride=Bt,
                    dl_t=bufs.dl_all, dl_cb=b * CB, nch_tot=NCH,
                    qf_t=bufs.qf_blk, ht_t=bufs.ht_blk)
    s.barrier()
    s.dma("sync", "spill", _blkslice(io["y0_hbm"].ap(), T, bB, Bt),
          bufs.y0_blk[:].rearrange("p (j t) -> p j t", j=NCC))
    for nm, t in (("qf_hbm", bufs.qf_blk), ("ht_hbm", bufs.ht_blk)):
        s.dma("sync", "spill", io[nm].ap().rearrange(
            "p (j c) -> p j c", j=NCC)[:, :, b * CB * 64 : (b + 1) * CB * 64],
            t[:].rearrange("p (j c) -> p j c", j=NCC))
    s.barrier()


def emit_exchange(s, nc, bufs, consts, io):
    psum = bufs.psum
    ncores = io["ncores"]
    s.dma("sync", "ag", io["agin"].ap(), bufs.M[:])
    s.barrier()
    ccsem = nc.alloc_semaphore(name="ccsem")

    def cc(e):
        e.collective_compute(
            "AllGather", mybir.AluOpType.bypass,
            replica_groups=[list(range(ncores))],
            ins=[io["agin"].ap().opt()],
            outs=[io["agout"].ap().opt()],
        ).then_inc(ccsem, 1)

    s.q["gpsimd"].append(("op", cc))
    s.q["gpsimd"].append(("wait", ccsem, 1))
    s.dirty.add("gpsimd")
    s.barrier()
    s.dma("sync", "ag2",
          bufs.Mg[:, 0 : ncores * 1024].rearrange("p (k c) -> p k c", k=ncores),
          io["agout"].ap().rearrange("k p c -> p k c"))
    s.barrier()

    def zinit(e):
        e.memset(bufs.cum[:], 0.0)
        e.memset(bufs.ssel[:], 0.0)

    s.on("vector", zinit)
    s.barrier(waiters=("vector", "tensor"))

    for kp in range((ncores + 3) // 4):
        def ptrb(e, kp=kp):
            for dk in range(4):
                k = kp * 4 + dk
                if k >= ncores:
                    continue
                pb = psum[dk // 2]
                base = (dk % 2) * 512
                for j in range(NCC):
                    for hh in range(2):
                        e.transpose(
                            pb[64 * hh : 64 * hh + 64,
                               base + j * 64 : base + j * 64 + 64],
                            bufs.Mg[64 * hh : 64 * hh + 64,
                                    k * 1024 + j * 128 + 64 :
                                    k * 1024 + j * 128 + 128],
                            consts.i64b[64 * hh : 64 * hh + 64, :],
                            tile_position=(64 * hh, 64 * hh),
                        )

        s.on("tensor", ptrb)
        s.barrier(waiters=("vector",))

        def ptev(e, kp=kp):
            e.tensor_copy(bufs.pfx_all[:, kp * 2048 : kp * 2048 + 1024],
                          psum[0][:, 0:1024])
            e.tensor_copy(bufs.pfx_all[:, kp * 2048 + 1024 : kp * 2048 + 2048],
                          psum[1][:, 0:1024])

        s.on("vector", ptev)
        s.barrier(waiters=("tensor",))

    for k in range(ncores):
        def sel(e, k=k):
            e.scalar_tensor_tensor(
                bufs.ssel[:], bufs.cum[:], bufs.onehot[:, k : k + 1],
                bufs.ssel[:], ALU.mult, ALU.add,
            )

        s.on("vector", sel)
        s.barrier(waiters=("tensor",))

        def comp(e, k=k):
            for j in range(NCC):
                for hh in range(2):
                    hr = slice(64 * hh, 64 * hh + 64)
                    e.matmul(psum[4][hr, j * 64 : j * 64 + 64],
                             bufs.pfx_all[hr, k * 512 + j * 64 :
                                          k * 512 + j * 64 + 64],
                             bufs.cum[hr, j * 64 : j * 64 + 64],
                             start=True, stop=False,
                             tile_position=(64 * hh, 64 * hh))
                    e.matmul(psum[4][hr, j * 64 : j * 64 + 64],
                             consts.i64b[hr, :],
                             bufs.Mg[hr, k * 1024 + j * 128 : k * 1024 + j * 128 + 64],
                             start=False, stop=True,
                             tile_position=(64 * hh, 64 * hh))

        s.on("tensor", comp)
        s.barrier(waiters=("vector",))
        s.on("vector", lambda e: e.tensor_copy(bufs.cum[:], psum[4][:, 0:512]))
        s.barrier(waiters=("vector", "tensor"))

    s.on("vector", lambda e: e.tensor_copy(bufs.zi[:], bufs.ssel[:]))
    s.barrier()


def emit_pass2_block(s, nc, bufs, consts, io, T, Bt, b):
    psum = bufs.psum
    CB = Bt // L
    NCH = T // L
    bB = b * Bt

    s.barrier()
    s.dma("gpsimd", "xload",
          bufs.x_blk2[:].rearrange("p (j t) -> p j t", j=NCC),
          io["x"].rearrange("(j p) t -> p j t", p=128)[:, :, bB + 1 : bB + Bt + 1])
    s.dma("sync", "p2late",
          bufs.g_p[:].rearrange("p (j t) -> p j t", j=NCC),
          _blkslice(io["g_hbm"].ap(), T, bB, Bt), defer=True)
    s.dma("sync", "p2late",
          bufs.v_p[:].rearrange("p (j t) -> p j t", j=NCC),
          _blkslice(io["v_hbm"].ap(), T, bB, Bt), defer=True)
    s.dma("sync", "p2load",
          bufs.y0_blk[:].rearrange("p (j t) -> p j t", j=NCC),
          _blkslice(io["y0_hbm"].ap(), T, bB, Bt))
    s.dma("sync", "p2late",
          bufs.coef_p2[:].rearrange("p (j t) -> p j t", j=NCC),
          _blkslice(io["coef_hbm"].ap(), T, bB, Bt), defer=True)
    s.dma("sync", "p2load",
          bufs.qf_blk[:].rearrange("p (j c) -> p j c", j=NCC),
          io["qf_hbm"].ap().rearrange("p (j c) -> p j c", j=NCC)[
              :, :, b * CB * 64 : (b + 1) * CB * 64])
    s.dma("sync", "p2load",
          bufs.ht_blk[:].rearrange("p (j c) -> p j c", j=NCC),
          io["ht_hbm"].ap().rearrange("p (j c) -> p j c", j=NCC)[
              :, :, b * CB * 64 : (b + 1) * CB * 64])
    s.barrier()

    for c in range(CB):
        def corr(e, c=c):
            for j in range(NCC):
                col = (j * CB + c) * 64
                for hh in range(2):
                    hr = slice(64 * hh, 64 * hh + 64)
                    yb = psum[2] if hh == 0 else psum[6]
                    e.matmul(yb[hr, j * 64 : j * 64 + 64],
                             bufs.zi[hr, j * 64 : j * 64 + 64],
                             bufs.qf_blk[hr, col : col + 64],
                             start=True, stop=True,
                             tile_position=(64 * hh, 64 * hh))
                    e.matmul(psum[7][hr, j * 64 : j * 64 + 64],
                             bufs.ht_blk[hr, col : col + 64],
                             bufs.zi[hr, j * 64 : j * 64 + 64],
                             start=True, stop=False,
                             tile_position=(64 * hh, 64 * hh))
                    e.matmul(psum[7][hr, j * 64 : j * 64 + 64],
                             consts.i64b[hr, :],
                             bufs.zi[hr, j * 64 : j * 64 + 64],
                             start=False, stop=True,
                             tile_position=(64 * hh, 64 * hh))

        s.on("tensor", corr)
        s.barrier(waiters=("vector", "scalar"))

        def zev(e, c=c):
            for j in range(NCC):
                e.tensor_scalar(
                    bufs.zi[:, j * 64 : j * 64 + 64],
                    psum[7][:, j * 64 : j * 64 + 64],
                    bufs.dl_all[:, j * NCH + b * CB + c :
                                j * NCH + b * CB + c + 1],
                    None, ALU.mult,
                )

        def yaddc(e, c=c):
            yv = bufs.y_blk[:].rearrange("p (j t) -> p j t", t=Bt)
            y0 = bufs.y0_blk[:].rearrange("p (j t) -> p j t", t=Bt)
            e.tensor_add(yv[0:64, :, c * 64 : c * 64 + 64],
                         psum[2][0:64, 0:512].rearrange(
                             "p (j u) -> p j u", u=64),
                         y0[0:64, :, c * 64 : c * 64 + 64])
            e.tensor_add(yv[64:128, :, c * 64 : c * 64 + 64],
                         psum[6][64:128, 0:512].rearrange(
                             "p (j u) -> p j u", u=64),
                         y0[64:128, :, c * 64 : c * 64 + 64])

        s.on("vector", zev)
        s.on("vector", yaddc)
        s.barrier(waiters=("tensor",))

    s.dma_commit("p2late")
    s.barrier(waiters=("scalar",))

    def gnsq(e):
        for j in range(NCC):
            e.activation(bufs.sqy[:, j * Bt : (j + 1) * Bt],
                         bufs.y_blk[:, j * Bt : (j + 1) * Bt], AF.Square)

    s.on("scalar", gnsq)
    s.barrier(waiters=("tensor",))

    GSL_S = ((2, 0), (3, 0), (4, 0), (5, 0))
    GSL_Q = ((2, 32), (3, 32), (2, 64), (3, 64))

    def rowslot(j):
        if j < 4:
            return bufs.rjA, j * 512
        if j < 6:
            return bufs.rjB1, (j - 4) * 512
        return bufs.rjB2, (j - 6) * 512

    def gnsum(e):
        for q in range(4):
            bk_, ba = GSL_S[q]
            e.matmul(psum[bk_][ba : ba + 2, 0:512], consts.ones_col2_r[:],
                     bufs.y_blk[:, q * 512 : q * 512 + 512],
                     start=True, stop=True)
            bk_, ba = GSL_Q[q]
            e.matmul(psum[bk_][ba : ba + 2, 0:512], consts.ones_col2_b[:],
                     bufs.sqy[:, q * 512 : q * 512 + 512],
                     start=True, stop=True)

    s.on("tensor", gnsum)
    s.barrier(waiters=("vector",))

    def rjmx(j):
        return (bufs.rjm if j < 4 else bufs.rjm2), (j % 4) * 256

    def rmath(e):
        for j in range(NCC):
            t, o = rowslot(j)
            q, hh = j // 2, j % 2
            sb, sa = GSL_S[q]
            qb, qa = GSL_Q[q]
            e.tensor_scalar_mul(
                t[0:2, o : o + 256],
                psum[sb][sa : sa + 2, hh * 256 : hh * 256 + 256],
                1.0 / 64.0)
            e.tensor_scalar_mul(
                t[0:2, o + 256 : o + 512],
                psum[qb][qa : qa + 2, hh * 256 : hh * 256 + 256],
                1.0 / 64.0)

    s.on("vector", rmath)
    s.barrier(waiters=("vector",))

    def rm2(e):
        for j in range(NCC):
            t, o = rowslot(j)
            m, mo = rjmx(j)
            e.tensor_mul(m[0:2, mo : mo + 256],
                         t[0:2, o : o + 256], t[0:2, o : o + 256])

    s.on("vector", rm2)
    s.barrier(waiters=("vector",))

    def rvar(e):
        for j in range(NCC):
            t, o = rowslot(j)
            m, mo = rjmx(j)
            e.tensor_sub(t[0:2, o + 256 : o + 512],
                         t[0:2, o + 256 : o + 512], m[0:2, mo : mo + 256])

    s.on("vector", rvar)
    s.barrier(waiters=("scalar",))

    def rln(e):
        for j in range(NCC):
            t, o = rowslot(j)
            m, mo = rjmx(j)
            e.activation(m[0:2, mo : mo + 256], t[0:2, o + 256 : o + 512],
                         AF.Ln, bias=consts.eps64[0:2, 0:1])

    s.on("scalar", rln)
    s.barrier(waiters=("scalar",))

    def rexp(e):
        for j in range(NCC):
            t, o = rowslot(j)
            m, mo = rjmx(j)
            e.activation(t[0:2, o + 256 : o + 512], m[0:2, mo : mo + 256],
                         AF.Exp, scale=-0.5)

    s.on("scalar", rexp)
    s.barrier(waiters=("vector",))

    s.barrier(waiters=("tensor",))
    for half in range(2):
        def gbc(e, half=half):
            for dj in range(4):
                j = 4 * half + dj
                t, o = rowslot(j)
                bb, bo = 2 + dj // 2, (dj % 2) * 256
                e.matmul(psum[bb][:, bo : bo + 256], consts.ones2_128[:],
                         t[0:2, o : o + 256], start=True, stop=True)
                e.matmul(psum[bb + 2][:, bo : bo + 256], consts.ones2_128[:],
                         t[0:2, o + 256 : o + 512], start=True, stop=True)
                e.matmul(psum[bb + 4][:, bo : bo + 256], consts.ones2_128f[:],
                         bufs.coef_p2[0:2, j * Bt : (j + 1) * Bt],
                         start=True, stop=True)

        s.on("tensor", gbc)
        s.barrier(waiters=("vector",))

        def pj(dj, half):
            j = 4 * half + dj
            return (slice(j * Bt, (j + 1) * Bt),
                    2 + dj // 2, slice((dj % 2) * 256, (dj % 2) * 256 + 256))

        def at1(e, half=half):
            for dj in range(4):
                sl, bb, pc = pj(dj, half)
                e.tensor_sub(bufs.tmp1[:, sl], bufs.y_blk[:, sl],
                             psum[bb][:, pc])

        s.on("vector", at1)
        s.barrier(waiters=("vector",))

        def at2(e, half=half):
            for dj in range(4):
                sl, bb, pc = pj(dj, half)
                e.tensor_mul(bufs.tmp2[:, sl], psum[bb + 4][:, pc],
                             bufs.v_p[:, sl])
                e.tensor_mul(bufs.tmp1[:, sl], bufs.tmp1[:, sl],
                             psum[bb + 2][:, pc])

        s.on("vector", at2)
        s.barrier(waiters=("vector",))

        def at3(e, half=half):
            for dj in range(4):
                sl, bb, pc = pj(dj, half)
                j = 4 * half + dj
                e.scalar_tensor_tensor(
                    bufs.tmp1[:, sl], bufs.tmp1[:, sl].bitcast(F32),
                    bufs.v_ln_w[:, j : j + 1], bufs.tmp2[:, sl].bitcast(F32),
                    ALU.mult, ALU.add)

        s.on("vector", at3)
        s.barrier(waiters=("vector",))

        def at4(e, half=half):
            for dj in range(4):
                sl, bb, pc = pj(dj, half)
                j = 4 * half + dj
                e.scalar_tensor_tensor(
                    bufs.yg[:, sl], bufs.tmp1[:, sl].bitcast(F32),
                    bufs.v_ln_b[:, j : j + 1], bufs.g_p[:, sl],
                    ALU.add, ALU.mult)

        s.on("vector", at4)
        s.barrier(waiters=("tensor",))

    def attev(e, o, region):
        e.activation(bufs.att_s[:, o * Bt : (o + 1) * Bt], region, AF.Copy)

    _, eva = emit_proj_1024(s, psum, bufs.w_o, bufs.yg, Bt, attev)
    s.on("scalar", eva)
    s.barrier()

    emit_ln_stats(s, bufs, psum, consts, bufs.att_s[:], bufs.sqy[:], Bt, consts.eps1)

    def l2a1(e):
        for j in range(NCC):
            e.tensor_sub(bufs.tmp1[:, j * Bt : (j + 1) * Bt],
                         bufs.att_s[:, j * Bt : (j + 1) * Bt],
                         psum[6][:, 0:Bt])

    s.on("vector", l2a1)
    s.barrier(waiters=("vector",))

    def l2a2(e):
        for j in range(NCC):
            e.tensor_mul(bufs.tmp2[:, j * Bt : (j + 1) * Bt],
                         bufs.tmp1[:, j * Bt : (j + 1) * Bt], psum[2][:, 0:Bt])

    s.on("vector", l2a2)
    s.barrier(waiters=("gpsimd",))

    def l2a3(e):
        for j in range(NCC):
            e.tensor_add(bufs.x2[:, j * Bt : (j + 1) * Bt],
                         bufs.tmp2[:, j * Bt : (j + 1) * Bt].bitcast(F32),
                         bufs.x_blk2[:, j * Bt : (j + 1) * Bt].bitcast(F32))

    s.on("gpsimd", l2a3)
    s.barrier()

    emit_ln_stats(s, bufs, psum, consts, bufs.x2[:], bufs.sqy[:], Bt, consts.eps1)
    emit_ln_apply(s, bufs.x2[:].bitcast(F32), bufs.ln3[:], psum, Bt)
    s.barrier(waiters=("tensor",))

    def ftr(e):
        for tt in range(2):
            for i in range(8):
                bank = psum[((4, 5) if tt == 0 else (2, 3))[i // 4]]
                e.transpose(
                    bank[:, (i % 4) * 128 : (i % 4) * 128 + 128].bitcast(F32R),
                    bufs.ln3[:, i * Bt + tt * 128 : i * Bt + tt * 128 + 128],
                    consts.i128_r[:],
                )

    s.on("tensor", ftr)
    s.barrier(waiters=("vector",))

    def fam(e):
        e.reduce_max(bufs.amax[:, 0:1], psum[4][:, 0:512],
                     mybir.AxisListType.X, apply_absolute_value=True)
        e.reduce_max(bufs.amax2[:, 0:1], psum[5][:, 0:512],
                     mybir.AxisListType.X, apply_absolute_value=True)
        e.reduce_max(bufs.amax[:, 1:2], psum[2][:, 0:512],
                     mybir.AxisListType.X, apply_absolute_value=True)
        e.reduce_max(bufs.amax2[:, 1:2], psum[3][:, 0:512],
                     mybir.AxisListType.X, apply_absolute_value=True)

    s.on("vector", fam)
    s.barrier(waiters=("vector",))

    def fam2(e):
        for tt in range(2):
            e.tensor_max(bufs.amax[:, tt : tt + 1],
                         bufs.amax2[:, tt : tt + 1],
                         bufs.amax[:, tt : tt + 1])

    s.on("vector", fam2)
    s.barrier(waiters=("tensor",))

    def atr(e):
        for tt in range(2):
            e.transpose(psum[4][0:1, tt * 128 : tt * 128 + 128].bitcast(F32R),
                        bufs.amax[:, tt : tt + 1], consts.i128_r[:])

    s.on("tensor", atr)
    s.barrier(waiters=("vector",))

    def arec2(e):
        e.tensor_copy(bufs.sclrow[0:1, 0:Bt], psum[4][0:1, 0:Bt])
        with nc.allow_low_precision(reason="f32r row scratch, full 32-bit"):
            e.reciprocal(bufs.invrow[0:1, 0:Bt],
                         psum[4][0:1, 0:Bt].bitcast(F32))

    s.on("vector", arec2)
    s.barrier(waiters=("vector",))

    def arow3(e):
        e.tensor_scalar_mul(bufs.invrow[0:1, 0:Bt], bufs.invrow[0:1, 0:Bt],
                            127.0)
        e.tensor_scalar_mul(bufs.sclrow[0:1, 0:Bt], bufs.sclrow[0:1, 0:Bt],
                            1.0 / 127.0)

    s.on("vector", arow3)
    s.barrier(waiters=("tensor",))

    def qbc(e):
        e.matmul(psum[2][:, 0:Bt], consts.onesrow_r[:], bufs.invrow[0:1, 0:Bt],
                 start=True, stop=True)
        e.matmul(psum[6][:, 0:Bt], consts.onesrow_r[:], bufs.sclrow[0:1, 0:Bt],
                 start=True, stop=True)

    s.on("tensor", qbc)
    s.barrier(waiters=("vector",))

    def xq1(e):
        for j in range(NCC):
            e.tensor_mul(bufs.tmp1[:, j * Bt : (j + 1) * Bt],
                         bufs.ln3[:, j * Bt : (j + 1) * Bt].bitcast(F32),
                         psum[2][:, 0:Bt])

    s.on("vector", xq1)
    s.barrier(waiters=("vector",))

    def xq2(e):
        for j in range(NCC):
            e.tensor_scalar(
                bufs.xq[:, j * Bt : (j + 1) * Bt],
                bufs.tmp1[:, j * Bt : (j + 1) * Bt],
                MAGIC, -MAGIC, ALU.add, ALU.add,
            )

    s.on("vector", xq2)
    s.barrier(waiters=("tensor",))

    def ffev(e, o, region):
        e.activation(bufs.tmp1[:, o * Bt : (o + 1) * Bt], region, AF.Copy,
                     scale=bufs.v_wq_scale[:, o : o + 1])

    _, evf = emit_proj_1024(s, psum, bufs.wq_t, bufs.xq, Bt, ffev,
                            banks=(7, 3, 4, 5))
    s.on("scalar", evf)
    s.barrier()

    def ffs(e):
        for j in range(NCC):
            e.tensor_mul(bufs.tmp2[:, j * Bt : (j + 1) * Bt],
                         bufs.tmp1[:, j * Bt : (j + 1) * Bt], psum[6][:, 0:Bt])

    s.on("vector", ffs)
    s.barrier(waiters=("vector",))

    def ffb(e):
        for j in range(NCC):
            e.tensor_scalar(
                bufs.ffo[:, j * Bt : (j + 1) * Bt],
                bufs.tmp2[:, j * Bt : (j + 1) * Bt],
                bufs.v_ffn_bias[:, j : j + 1], None, ALU.add,
            )

    s.on("vector", ffb)
    s.barrier()

    emit_ln_stats(s, bufs, psum, consts, bufs.ffo[:], bufs.sqy[:], Bt, consts.eps1)

    def l4a1(e):
        for j in range(NCC):
            e.tensor_sub(bufs.tmp1[:, j * Bt : (j + 1) * Bt],
                         bufs.ffo[:, j * Bt : (j + 1) * Bt], psum[6][:, 0:Bt])

    s.on("vector", l4a1)
    s.barrier(waiters=("vector",))

    def l4a2(e):
        for j in range(NCC):
            e.tensor_mul(bufs.tmp2[:, j * Bt : (j + 1) * Bt],
                         bufs.tmp1[:, j * Bt : (j + 1) * Bt], psum[2][:, 0:Bt])

    s.on("vector", l4a2)
    s.barrier(waiters=("gpsimd",))

    def l4a3(e):
        for j in range(NCC):
            e.tensor_add(bufs.x3[:, j * Bt : (j + 1) * Bt],
                         bufs.tmp2[:, j * Bt : (j + 1) * Bt].bitcast(F32),
                         bufs.x2[:, j * Bt : (j + 1) * Bt].bitcast(F32))

    s.on("gpsimd", l4a3)
    s.barrier()

    s.dma("sync", "ostore",
          io["out"].rearrange("(j p) t -> p j t", p=128)[:, :, bB : bB + Bt],
          bufs.x3[:].rearrange("p (j t) -> p j t", j=NCC))


def build_core_program(T, Bt, ncores=8):
    nc = bass.Bass("TRN2", target_bir_lowering=False)
    io = declare_io(nc, T, ncores)
    s = Sched(nc)
    consts = FullConsts(nc, s)
    consts.declare_inputs(nc)
    bufs = FullBufs(nc, Bt, T)
    s.dummy = bufs.dummy
    NBLK = T // Bt

    with nc.Block() as block:
        emit_weight_load(s, nc, bufs, consts, io)
        for b in range(NBLK):
            emit_pass1_block(s, nc, bufs, consts, io, T, Bt, b)
        emit_exchange(s, nc, bufs, consts, io)
        emit_prep2(s, nc, bufs, consts, io)
        for b in range(NBLK):
            emit_pass2_block(s, nc, bufs, consts, io, T, Bt, b)
        s.barrier()
        s.finalize(block)
    return nc, io


from concourse.bass_utils import run_bass_kernel_spmd

_T_FULL = 16384
_NCORES = 8
_T_CORE = _T_FULL // _NCORES
_BT = 256
_CACHE = {}


def _fmvec(v):
    return np.ascontiguousarray(np.asarray(v, np.float32).reshape(8, 128).T)


def _get_program():
    if "nc" not in _CACHE:
        _CACHE["nc"] = build_core_program(_T_CORE, _BT, _NCORES)
    return _CACHE["nc"]


def kernel(**inputs):
    nc, io = _get_program()
    x = np.asarray(inputs["L0_attn_x"], np.float32)[0]
    xT = np.ascontiguousarray(x.T)
    carr = full_const_arrays()
    w2 = np.asarray(inputs["L0_attn_time_maa_w2"], np.float32)

    shared = {
        "W_r": np.asarray(inputs["L0_attn_W_r"], np.float32),
        "W_k": np.asarray(inputs["L0_attn_W_k"], np.float32),
        "W_v": np.asarray(inputs["L0_attn_W_v"], np.float32),
        "W_o": np.asarray(inputs["L0_attn_W_o"], np.float32),
        "W_ffn": np.asarray(inputs["L0_ffn_weight"], np.float32),
        "maa_w1_d": np.asarray(inputs["L0_attn_time_maa_w1"], np.float32),
        "maa_w2a_d": np.ascontiguousarray(w2[0:4].reshape(128, 1024)),
        "maa_w2b_d": np.ascontiguousarray(w2[4:6].reshape(64, 1024)),
        "dec_w1_d": np.asarray(inputs["L0_attn_time_decay_w1"], np.float32),
        "dec_w2_d": np.asarray(inputs["L0_attn_time_decay_w2"], np.float32),
        "aaa_w1_d": np.asarray(inputs["L0_attn_time_aaa_w1"], np.float32),
        "aaa_w2_d": np.asarray(inputs["L0_attn_time_aaa_w2"], np.float32),
        "kkk_w1_d": np.asarray(inputs["L0_attn_time_kkk_w1"], np.float32),
        "kkk_w2_d": np.asarray(inputs["L0_attn_time_kkk_w2"], np.float32),
        "gate_w1_d": np.asarray(inputs["L0_attn_gate_w1"], np.float32),
        "gate_w2_d": np.asarray(inputs["L0_attn_gate_w2"], np.float32),
        "v_maa_x_d": _fmvec(np.asarray(inputs["L0_attn_time_maa_x"]).ravel()),
        "v_maa_r_d": _fmvec(np.asarray(inputs["L0_attn_time_maa_r"]).ravel()),
        "v_maa_w_d": _fmvec(np.asarray(inputs["L0_attn_time_maa_w"]).ravel()),
        "v_maa_k_d": _fmvec(np.asarray(inputs["L0_attn_time_maa_k"]).ravel()),
        "v_maa_v_d": _fmvec(np.asarray(inputs["L0_attn_time_maa_v"]).ravel()),
        "v_maa_a_d": _fmvec(np.asarray(inputs["L0_attn_time_maa_a"]).ravel()),
        "v_maa_g_d": _fmvec(np.asarray(inputs["L0_attn_time_maa_g"]).ravel()),
        "v_td_d": _fmvec(np.asarray(inputs["L0_attn_time_decay"]).ravel()),
        "v_aaaaa_d": _fmvec(np.asarray(inputs["L0_attn_time_aaaaa"]).ravel()),
        "v_faaaa_d": _fmvec(np.asarray(inputs["L0_attn_time_faaaa"]).ravel()),
        "v_ln_w_d": _fmvec(np.asarray(inputs["L0_attn_ln_w"]).ravel()),
        "v_ln_b_d": _fmvec(np.asarray(inputs["L0_attn_ln_b"]).ravel()),
        "v_ffn_bias_d": _fmvec(np.asarray(inputs["L0_ffn_bias"]).ravel()),
        "c_mask_g1": carr["mask_g1"], "c_mask_sl": carr["mask_sl"],
        "c_i64b_f": carr["i64b"], "c_zeros1": carr["zeros1"],
        "c_ones_col": carr["ones_col"], "c_ones_col2": carr["ones_col2"],
        "c_onesrow": carr["onesrow"], "c_negrow": carr["negrow"],
        "c_ones2_128": carr["ones2_128"], "c_i128": carr["i128"],
        "c_eps1": carr["eps1"], "c_eps64": carr["eps64"],
        "c_mask_g2": carr["mask_g2"], "c_mask_sl8": carr["mask_sl8"],
    }

    maps = []
    for c in range(_NCORES):
        t0 = c * _T_CORE
        xh = np.zeros((1024, _T_CORE + 1), np.float32)
        if t0 > 0:
            xh[:, 0] = xT[:, t0 - 1]
        xh[:, 1:] = xT[:, t0 : t0 + _T_CORE]
        onehot = np.zeros((128, 8), np.float32)
        onehot[:, c] = 1.0
        m = dict(shared)
        m["x_in"] = xh
        m["onehot_d"] = onehot
        maps.append(m)

    res = run_bass_kernel_spmd(nc, maps, core_ids=list(range(_NCORES)))
    outs = [np.asarray(res.results[c]["x_out"]) for c in range(_NCORES)]
    got = np.concatenate([o.T for o in outs], axis=0)
    return np.ascontiguousarray(got, dtype=np.float32)[None]


# version 24
# speedup vs baseline: 1.0113x; 1.0113x over previous
import numpy as np
import concourse.bass as bass
import concourse.mybir as mybir

F32 = mybir.dt.float32
F32R = mybir.dt.float32r
BF16 = mybir.dt.bfloat16
AF = mybir.ActivationFunctionType
ALU = mybir.AluOpType

H = 16
N = 64
C = 1024
L = 64
NCC = 8


class Sched:

    ENGINES = ("tensor", "vector", "scalar", "gpsimd", "sync")

    def __init__(self, nc):
        self.nc = nc
        self.q = {e: [] for e in self.ENGINES}
        self.psem = {e: nc.alloc_semaphore(name=f"ph_{e}") for e in self.ENGINES}
        self.pcount = {e: 0 for e in self.ENGINES}
        self.dirty = set()
        self.dsem = {}
        self.dtot = {}
        self.dpend = {}
        self.seen = {e: {} for e in self.ENGINES}

    def dma_sem(self, name):
        if name not in self.dsem:
            self.dsem[name] = self.nc.alloc_semaphore(name=f"dma_{name}")
            self.dtot[name] = 0
        return self.dsem[name]

    def on(self, eng, fn):
        self.q[eng].append(("op", fn))
        self.dirty.add(eng)

    def sep(self, e, n=3):
        for k in range(n):
            i = self._sepi = (getattr(self, "_sepi", 0) + 1) % 4
            _cp(e, self.dummy[0:1, i : i + 1], self.dummy[0:1, 4 + i : 5 + i])

    def dma(self, eng, semname, out, in_, defer=False, **kw):
        sem = self.dma_sem(semname)
        if defer:
            self.dpend[semname] = self.dpend.get(semname, 0) + 16
        else:
            self.dtot[semname] += 16

        def fn(e, out=out, in_=in_, sem=sem, kw=kw):
            e.dma_start(out, in_, **kw).then_inc(sem, 16)

        self.q[eng].append(("op", fn))
        self.dirty.add(eng)

    def dma_commit(self, semname):
        if semname in self.dtot:
            self.dtot[semname] += self.dpend.pop(semname, 0)

    def barrier(self, waiters=None):
        if waiters is None:
            waiters = self.ENGINES
        for e in sorted(self.dirty):
            self.pcount[e] += 1
            self.q[e].append(("inc", self.psem[e], 1))
        self.dirty = set()
        for e in waiters:
            seen = self.seen[e]
            for p in self.ENGINES:
                if p == e:
                    continue
                if self.pcount[p] > seen.get(p, 0):
                    self.q[e].append(("wait", self.psem[p], self.pcount[p]))
                    seen[p] = self.pcount[p]
            for name in sorted(self.dtot):
                key = "dma_" + name
                if self.dtot[name] > seen.get(key, 0):
                    self.q[e].append(("wait", self.dsem[name], self.dtot[name]))
                    seen[key] = self.dtot[name]

    def finalize(self, block):
        for ename in self.ENGINES:
            ops = self.q[ename]
            if not ops:
                continue

            def make(ops=ops):
                def run(e):
                    rec = _EngRec(e)
                    for item in ops:
                        if item[0] == "op":
                            item[1](rec)
                        elif item[0] == "inc":
                            _, sem, val = item
                            e.drain().then_inc(sem, val)
                            rec.last = None
                        else:
                            _, sem, val = item
                            e.wait_ge(sem, val)
                            rec.last = None

                return run

            getattr(block, ename)(make())


class _EngRec:

    def __init__(self, eng):
        self._eng = eng
        self.last = None

    def __getattr__(self, name):
        attr = getattr(self._eng, name)
        if not callable(attr):
            return attr

        def wrap(*a, **kw):
            res = attr(*a, **kw)
            if isinstance(res, bass.BassInstruction) and not name.startswith("dma"):
                self.last = res
            return res

        return wrap


def make_const_arrays():
    su = np.triu(np.ones((64, 64), np.float32), 1)
    ud = np.triu(np.ones((64, 64), np.float32), 0)
    sl = np.tril(np.ones((64, 64), np.float32), -1)
    i64 = np.eye(64, dtype=np.float32)
    mask_g1 = np.concatenate([su, ud, su, ud], axis=1)
    consts = {
        "mask_g1": np.concatenate([mask_g1, mask_g1], axis=0),
        "mask_sl": np.concatenate([sl, sl], axis=0),
        "i64b": np.concatenate([i64, i64], axis=0).astype(np.float32),
        "zeros1": np.zeros((128, 1), np.float32),
    }
    return consts


class ScanBufs:
    def __init__(self, nc, Bt):
        CB = Bt // L
        a = nc.alloc_sbuf_tensor
        self.ar = a("sc_ar", [128, NCC * 2 * Bt], BF16)
        self.bk = a("sc_bk", [128, NCC * 2 * Bt], BF16)
        self.vf = a("sc_vf", [128, NCC * Bt], BF16)
        self.tm = a("sc_tm", [128, 192], BF16)
        self.X = a("sc_X", [128, 128], BF16)
        self.g1 = a("sc_g1", [128, 256], BF16)
        self.P = a("sc_P", [128, 64 + 5 * 128], BF16)
        self.HT = a("sc_HT", [128, 64], BF16)
        self.Qf = a("sc_Qf", [128, 64], BF16)
        self.M = a("sc_M", [128, NCC * 128], BF16)
        self.slog = a("sc_slog", [128, NCC * Bt], F32)
        self.ep = a("sc_ep", [128, NCC * Bt], BF16)
        self.em = a("sc_em", [128, NCC * Bt], BF16)
        self.eprev = a("sc_eprev", [128, NCC * Bt], BF16)
        self.carry = a("sc_carry", [128, NCC], F32)
        self.carry2 = a("sc_carry2", [128, NCC], F32)
        self.offs = a("sc_offs", [128, NCC * CB], F32)
        self.dl = a("sc_dl", [128, NCC * CB], F32)


class ConstSB:

    def __init__(self, nc, s):
        a = nc.alloc_sbuf_tensor
        self.mask_g1 = a("cs_maskg1", [128, 256], F32)
        self.mask_sl = a("cs_masksl", [128, 64], F32)
        self.i64b = a("cs_i64b", [128, 64], BF16)
        self.zeros1 = a("cs_zeros1", [128, 1], F32)

    def declare_inputs(self, nc):
        self.d_mask_g1 = nc.dram_tensor("c_mask_g1", [128, 256], F32,
                                        kind="ExternalInput").ap()
        self.d_mask_sl = nc.dram_tensor("c_mask_sl", [128, 64], F32,
                                        kind="ExternalInput").ap()
        self.d_i64b = nc.dram_tensor("c_i64b_f", [128, 64], F32,
                                     kind="ExternalInput").ap()
        self.d_zeros1 = nc.dram_tensor("c_zeros1", [128, 1], F32,
                                       kind="ExternalInput").ap()

    def emit_load(self, s):
        s.dma("sync", "cload", self.mask_g1[:], self.d_mask_g1)
        s.dma("sync", "cload", self.mask_sl[:], self.d_mask_sl)
        s.dma("gpsimd", "cloadg", self.i64b[:], self.d_i64b)
        s.dma("sync", "cload", self.zeros1[:], self.d_zeros1)




def emit_scan_prep(s, bufs, psum, consts, Bt, src_sb, exp_scale,
                   r_src, k_src, v_src, a_src, b_src, first_block,
                   dl_t=None, dl_cb=0, nch_tot=None, blk_par=0):
    CB = Bt // L
    carry_in = bufs.carry if blk_par == 0 else bufs.carry2
    carry_out = bufs.carry2 if blk_par == 0 else bufs.carry
    if first_block:
        s.on("vector", lambda e: e.memset(carry_in[:], 0.0))
        s.barrier(waiters=("vector",))

    def cums(e):
        for j in range(NCC):
            e.tensor_tensor_scan(
                bufs.slog[:, j * Bt : (j + 1) * Bt],
                src_sb[:, j * Bt : (j + 1) * Bt],
                consts.zeros1[:].broadcast_to([128, Bt]),
                carry_in[:, j : j + 1],
                ALU.add,
                ALU.add,
            )

    s.on("vector", cums)
    s.barrier(waiters=("vector",))

    def mkoffs(e):
        for j in range(NCC):
            e.tensor_copy(bufs.offs[:, j * CB : j * CB + 1],
                          carry_in[:, j : j + 1])
            if CB > 1:
                sl3 = bufs.slog[:, j * Bt : (j + 1) * Bt].rearrange(
                    "p (c l) -> p c l", l=L
                )
                e.tensor_copy(
                    bufs.offs[:, j * CB + 1 : (j + 1) * CB].unsqueeze(-1),
                    sl3[:, 0 : CB - 1, L - 1 : L],
                )
    def carryupd(e):
        for j in range(NCC):
            e.tensor_copy(carry_out[:, j : j + 1],
                          bufs.slog[:, (j + 1) * Bt - 1 : (j + 1) * Bt])

    s.on("vector", mkoffs)
    s.on("vector", carryupd)
    s.barrier(waiters=("vector", "scalar"))

    def mkrel(e):
        for j in range(NCC):
            ob = bufs.offs[:, j * CB : (j + 1) * CB].unsqueeze(-1)
            e.tensor_sub(
                bufs.slog[:, j * Bt : (j + 1) * Bt].rearrange(
                    "p (c l) -> p c l", l=L
                ),
                bufs.slog[:, j * Bt : (j + 1) * Bt].rearrange(
                    "p (c l) -> p c l", l=L
                ),
                ob.broadcast_to([128, CB, L]),
            )

    s.on("vector", mkrel)
    s.barrier(waiters=("vector", "scalar"))

    if dl_t is None:
        dl_t, dl_cb, nch_tot = bufs.dl, 0, CB
    rel = bufs.slog

    def mkexp(e):
        sc = exp_scale
        for j in range(NCC):
            sl = slice(j * Bt, (j + 1) * Bt)
            e.activation(bufs.ep[:, sl], rel[:, sl], AF.Exp, scale=sc)
            e.activation(bufs.em[:, sl], rel[:, sl], AF.Exp, scale=-sc)
            r3 = rel[:, sl].rearrange("p (c l) -> p c l", l=L)
            p3 = bufs.eprev[:, sl].rearrange("p (c l) -> p c l", l=L)
            e.activation(p3[:, :, 1:L], r3[:, :, 0 : L - 1], AF.Exp, scale=sc)
        for j in range(NCC):
            r3 = rel[:, j * Bt : (j + 1) * Bt].rearrange(
                "p (c l) -> p c l", l=L
            )
            e.activation(
                dl_t[:, j * nch_tot + dl_cb : j * nch_tot + dl_cb + CB
                     ].unsqueeze(-1),
                r3[:, :, L - 1 : L],
                AF.Exp,
                scale=sc,
            )

    def epfix(e):
        for j in range(NCC):
            p3 = bufs.eprev[:, j * Bt : (j + 1) * Bt].rearrange(
                "p (c l) -> p c l", l=L
            )
            e.memset(p3[:, :, 0:1], 1.0)

    s.on("scalar", mkexp)
    s.on("vector", epfix)
    s.barrier(waiters=("vector", "gpsimd"))

    def mkhats_v(e):
        for j in range(NCC):
            sl = slice(j * Bt, (j + 1) * Bt)
            e.tensor_mul(bufs.ar[:, j * 2 * Bt + Bt : (j + 1) * 2 * Bt],
                         r_src[:, sl], bufs.ep[:, sl])
            e.tensor_mul(bufs.ar[:, j * 2 * Bt : j * 2 * Bt + Bt],
                         a_src[:, sl], bufs.eprev[:, sl])
            if v_src is not bufs.vf:
                e.tensor_copy(bufs.vf[:, sl], v_src[:, sl])

    def mkhats_g(e):
        for j in range(NCC):
            sl = slice(j * Bt, (j + 1) * Bt)
            e.tensor_mul(bufs.bk[:, j * 2 * Bt : j * 2 * Bt + Bt],
                         b_src[:, sl], bufs.em[:, sl])
            e.tensor_mul(bufs.bk[:, j * 2 * Bt + Bt : (j + 1) * 2 * Bt],
                         k_src[:, sl], bufs.em[:, sl])

    s.on("vector", mkhats_v)
    s.on("gpsimd", mkhats_g)
    s.barrier()


def emit_scan_block(s, bufs, psum, consts, Bt, y_out_sb, first_block,
                    y_base=0, y_stride=None, dl_t=None, dl_cb=0, nch_tot=None,
                    qf_t=None, ht_t=None):
    CB = Bt // L
    if y_stride is None:
        y_stride = Bt
    if dl_t is None:
        dl_t, dl_cb, nch_tot = bufs.dl, 0, CB
    if qf_t is None:
        qf_t, ht_t = bufs.qf_blk if hasattr(bufs, "qf_blk") else None, None
    if first_block:
        def minit(e):
            e.memset(
                bufs.M[:].rearrange("p (j u) -> p j u", u=128)[:, :, 0:64], 0.0
            )
            for j in range(NCC):
                e.tensor_copy(bufs.M[:, j * 128 + 64 : j * 128 + 128],
                              consts.i64b[:])

        s.on("vector", minit)
        s.barrier(waiters=("tensor",))

    for j in range(NCC):
        for c in range(CB):
            emit_chunk(s, bufs, psum, consts, Bt, j, c,
                       dl_t, dl_cb, nch_tot, qf_t, ht_t)
        yb0 = j * y_stride + y_base

        def yev(e, yb0=yb0):
            e.tensor_copy(y_out_sb[0:64, yb0 : yb0 + Bt], psum[2][0:64, 0:Bt])
            e.tensor_copy(y_out_sb[64:128, yb0 : yb0 + Bt],
                          psum[6][64:128, 0:Bt])

        s.on("vector", yev)
        s.barrier(waiters=("tensor",))


def emit_chunk(s, bufs, psum, consts, Bt, j, c,
               dl_t, dl_cb, nch_tot, qf_t, ht_t):
    CB = Bt // L
    cb = c * L
    if qf_t is None:
        qf_t, qf_col = bufs.Qf, 0
        ht_col = 0
        ht_t = bufs.HT
    else:
        qf_col = (j * CB + c) * 64
        ht_col = (j * CB + c) * 64

    def hrows(hh):
        return slice(64 * hh, 64 * hh + 64)

    def fmt(t, hh, base):
        return t[hrows(hh), base + cb : base + cb + L]

    a_t = lambda hh: fmt(bufs.ar, hh, j * 2 * Bt)
    r_t = lambda hh: fmt(bufs.ar, hh, j * 2 * Bt + Bt)
    b_t = lambda hh: fmt(bufs.bk, hh, j * 2 * Bt)
    k_t = lambda hh: fmt(bufs.bk, hh, j * 2 * Bt + Bt)
    v_t = lambda hh: fmt(bufs.vf, hh, j * Bt)

    def arpair(hh):
        r3 = bufs.ar[hrows(hh), :].rearrange("p (u t) -> p u t", t=Bt)
        return r3[:, 2 * j : 2 * j + 2, cb : cb + L]

    tp = lambda hh: (64 * hh, 64 * hh)
    I = lambda hh: consts.i64b[hrows(hh), :]

    btm = lambda hh: bufs.tm[hrows(hh), 0:64]
    ktm = lambda hh: bufs.tm[hrows(hh), 64:128]
    vtm = lambda hh: bufs.tm[hrows(hh), 128:192]
    aU = lambda hh: bufs.X[hrows(hh), 0:64]
    xSA = lambda hh: bufs.X[hrows(hh), 64:128]
    GT1 = lambda hh: bufs.g1[hrows(hh), 0:64]
    MRBT = lambda hh: bufs.g1[hrows(hh), 64:128]
    MAKT = lambda hh: bufs.g1[hrows(hh), 128:192]
    MRKT = lambda hh: bufs.g1[hrows(hh), 192:256]

    def Gp(lv, hh):
        if lv == 0:
            return bufs.P[hrows(hh), 0:64]
        return bufs.P[hrows(hh), 64 + (lv - 1) * 128 : 64 + (lv - 1) * 128 + 64]

    def GpT(lv, hh):
        if lv == 0:
            return GT1(hh)
        return bufs.P[hrows(hh), 64 + (lv - 1) * 128 + 64 : 64 + lv * 128]

    def tmm(e):
        for hh in range(2):
            for i, src in enumerate((a_t(hh), b_t(hh), k_t(hh), v_t(hh))):
                e.transpose(
                    psum[0][hrows(hh), i * 64 : i * 64 + 64],
                    src,
                    I(hh),
                    tile_position=tp(hh),
                )

    s.on("tensor", tmm)
    s.barrier(waiters=("vector",))

    def tmev(e):
        e.tensor_copy(bufs.X[:, 0:64], psum[0][:, 0:64])
        e.tensor_copy(bufs.tm[:, 0:192], psum[0][:, 64:256])

    s.on("vector", tmev)
    s.barrier(waiters=("tensor",))

    def gmm(e):
        for hh in range(2):
            e.matmul(psum[1][hrows(hh), 0:128], b_t(hh), arpair(hh),
                     start=True, stop=True, tile_position=tp(hh))
            e.matmul(psum[1][hrows(hh), 128:256], k_t(hh), arpair(hh),
                     start=True, stop=True, tile_position=tp(hh))
            e.matmul(psum[3][hrows(hh), 256:320], a_t(hh), b_t(hh),
                     start=True, stop=True, tile_position=tp(hh))

    s.on("tensor", gmm)
    s.barrier(waiters=("vector",))

    def gev(e):
        e.tensor_mul(bufs.g1[:], psum[1][:, 0:256], consts.mask_g1[:])
        e.tensor_mul(bufs.P[:, 0:64], psum[3][:, 256:320], consts.mask_sl[:])

    s.on("vector", gev)
    s.barrier(waiters=("tensor",))

    for lv in range(1, 6):
        def sqmm(e, lv=lv):
            for hh in range(2):
                if lv < 5:
                    e.matmul(psum[3][hrows(hh), 0:64],
                             GpT(lv - 1, hh), Gp(lv - 1, hh),
                             start=True, stop=True, tile_position=tp(hh))
                e.matmul(psum[3][hrows(hh), 64:128],
                         Gp(lv - 1, hh), GpT(lv - 1, hh),
                         start=True, stop=True, tile_position=tp(hh))

        s.on("tensor", sqmm)
        s.barrier(waiters=("vector",))

        def sqev(e, lv=lv):
            if lv < 5:
                e.tensor_copy(bufs.P[:, 64 + (lv - 1) * 128 : 64 + lv * 128],
                              psum[3][:, 0:128])
            else:
                e.tensor_copy(
                    bufs.P[:, 64 + (lv - 1) * 128 + 64 : 64 + lv * 128],
                    psum[3][:, 64:128],
                )

        s.on("vector", sqev)
        s.barrier(waiters=("tensor",))

    def xloc(e):
        for hh in range(2):
            e.matmul(psum[4][hrows(hh), 64:128], MAKT(hh), vtm(hh),
                     start=True, stop=True, tile_position=tp(hh))

    s.on("tensor", xloc)
    s.barrier(waiters=("vector",))

    s.on("vector", lambda e: e.tensor_copy(bufs.X[:, 64:128],
                                           psum[4][:, 64:128]))
    s.barrier(waiters=("tensor",))

    for lv in range(6):
        def chmm(e, lv=lv):
            for hh in range(2):
                e.matmul(psum[4][hrows(hh), 0:128], GpT(lv, hh),
                         bufs.X[hrows(hh), 0:128],
                         start=True, stop=False, tile_position=tp(hh))
                e.matmul(psum[4][hrows(hh), 0:128], I(hh),
                         bufs.X[hrows(hh), 0:128],
                         start=False, stop=True, tile_position=tp(hh))

        s.on("tensor", chmm)
        s.barrier(waiters=("vector",))

        s.on("vector", lambda e: e.tensor_copy(bufs.X[:, 0:128],
                                               psum[4][:, 0:128]))
        s.barrier(waiters=("tensor",))

    def fmm(e):
        for hh in range(2):
            e.matmul(psum[5][hrows(hh), 0:64], aU(hh), btm(hh),
                     start=True, stop=True, tile_position=tp(hh))
            e.matmul(psum[5][hrows(hh), 64:128], aU(hh), MRBT(hh),
                     start=True, stop=True, tile_position=tp(hh))
            yb = psum[2] if hh == 0 else psum[6]
            e.matmul(yb[hrows(hh), cb : cb + 64], xSA(hh), MRBT(hh),
                     start=True, stop=False, tile_position=tp(hh))
            e.matmul(yb[hrows(hh), cb : cb + 64], vtm(hh), MRKT(hh),
                     start=False, stop=False, tile_position=tp(hh))

    s.on("tensor", fmm)
    s.barrier(waiters=("vector",))

    def fev(e):
        e.tensor_copy(ht_t[:, ht_col : ht_col + 64], psum[5][:, 0:64])
        e.tensor_add(qf_t[:, qf_col : qf_col + 64], psum[5][:, 64:128],
                     bufs.ar[:, j * 2 * Bt + Bt + cb : j * 2 * Bt + Bt + cb + 64])

    s.on("vector", fev)
    s.barrier(waiters=("tensor",))

    Mj = lambda hh: bufs.M[hrows(hh), j * 128 : j * 128 + 128]
    Zj = lambda hh: bufs.M[hrows(hh), j * 128 : j * 128 + 64]

    def smm(e):
        for hh in range(2):
            e.matmul(psum[7][hrows(hh), 0:128],
                     ht_t[hrows(hh), ht_col : ht_col + 64], Mj(hh),
                     start=True, stop=False, tile_position=tp(hh))
            e.matmul(psum[7][hrows(hh), 0:64], btm(hh), xSA(hh),
                     start=False, stop=False, tile_position=tp(hh))
            e.matmul(psum[7][hrows(hh), 0:64], ktm(hh), vtm(hh),
                     start=False, stop=False, tile_position=tp(hh))
            e.matmul(psum[7][hrows(hh), 0:128], I(hh), Mj(hh),
                     start=False, stop=True, tile_position=tp(hh))
        for hh in range(2):
            yb = psum[2] if hh == 0 else psum[6]
            e.matmul(yb[hrows(hh), cb : cb + 64], Zj(hh),
                     qf_t[hrows(hh), qf_col : qf_col + 64],
                     start=False, stop=True, tile_position=tp(hh))

    s.on("tensor", smm)
    s.barrier(waiters=("vector",))

    def sev(e):
        e.tensor_scalar(
            bufs.M[:, j * 128 : j * 128 + 128],
            psum[7][:, 0:128],
            dl_t[:, j * nch_tot + dl_cb + c : j * nch_tot + dl_cb + c + 1],
            None,
            ALU.mult,
        )

    s.on("vector", sev)
    s.barrier(waiters=("tensor",))


def _cp(e, dst, src):
    if getattr(e, "_kind", None) == "scalar":
        e.activation(dst, src, AF.Copy)
        return
    try:
        e.tensor_copy(dst, src)
    except AttributeError:
        e.activation(dst, src, AF.Copy)


def emit_scan_block_v2(s, bufs, psum, consts, Bt, y_out_sb, first_block,
                       y_base=0, y_stride=None, dl_t=None, dl_cb=0,
                       nch_tot=None, qf_t=None, ht_t=None):
    CB = Bt // L
    if y_stride is None:
        y_stride = Bt
    if first_block:
        def minit(e):
            e.memset(
                bufs.M[:].rearrange("p (j u) -> p j u", u=128)[:, :, 0:64], 0.0
            )
            for j in range(NCC):
                e.tensor_copy(bufs.M[:, j * 128 + 64 : j * 128 + 128],
                              consts.i64b[:])

        s.on("vector", minit)
        s.barrier(waiters=("tensor",))

    builders = [emit_chunk_v2(s, bufs, psum, consts, Bt, c, y_out_sb, y_base,
                              y_stride, dl_t, dl_cb, nch_tot, qf_t, ht_t)
                for c in range(CB)]
    NSTEP = len(builders[0])
    starts = []
    for i in range(2 * CB):
        starts.append(i if i < 2 else starts[i - 2] + 20)
    total = starts[-1] + 2 * NSTEP
    for p in range(total):
        trec, vrec, srec = _Rec("tensor"), _Rec("vector"), _Rec("scalar")
        for i in range(2 * CB):
            c, h = i // 2, i % 2
            ladder = builders[c]
            rel = p - starts[i]
            if rel < 0 or rel >= 2 * NSTEP:
                continue
            k, kind = rel // 2, rel % 2
            if kind == 0:
                if ladder[k][0] is not None:
                    ladder[k][0](trec, h)
            else:
                if ladder[k][1] is not None:
                    ladder[k][1](vrec, srec, h)
        any_ = False
        for eng, rec in (("tensor", trec), ("vector", vrec), ("scalar", srec)):
            if rec.ops:
                s.on(eng, _replay(rec.ops))
                any_ = True
        if any_:
            s.barrier(waiters=("tensor", "vector", "scalar"))

def emit_chunk_v2(s, bufs, psum, consts, Bt, c, y_out_sb, y_base, y_stride,
                  dl_t, dl_cb, nch_tot, qf_t, ht_t):
    CB = Bt // L
    cb = c * L
    hr = lambda hh: slice(64 * hh, 64 * hh + 64)
    tp = lambda hh: (64 * hh, 64 * hh)
    I = lambda hh: consts.i64b[hr(hh), :]
    a_t = lambda j, hh: bufs.ar[hr(hh), j * 2 * Bt + cb : j * 2 * Bt + cb + L]
    b_t = lambda j, hh: bufs.bk[hr(hh), j * 2 * Bt + cb : j * 2 * Bt + cb + L]
    k_t = lambda j, hh: bufs.bk[hr(hh),
                                j * 2 * Bt + Bt + cb : j * 2 * Bt + Bt + cb + L]
    v_t = lambda j, hh: bufs.vf[hr(hh), j * Bt + cb : j * Bt + cb + L]

    def arpair(j, hh):
        r3 = bufs.ar[hr(hh), :].rearrange("p (u t) -> p u t", t=Bt)
        return r3[:, 2 * j : 2 * j + 2, cb : cb + L]

    aU = lambda j, hh: bufs.Xb[hr(hh), j * 128 : j * 128 + 64]
    xSA = lambda j, hh: bufs.Xb[hr(hh), j * 128 + 64 : j * 128 + 128]
    X2 = lambda j, hh: bufs.Xb[hr(hh), j * 128 : j * 128 + 128]
    btm = lambda j, hh: bufs.TMb[hr(hh), j * 192 : j * 192 + 64]
    ktm = lambda j, hh: bufs.TMb[hr(hh), j * 192 + 64 : j * 192 + 128]
    vtm = lambda j, hh: bufs.TMb[hr(hh), j * 192 + 128 : j * 192 + 192]
    GT1 = lambda j, hh: bufs.g1b[hr(hh), j * 256 : j * 256 + 64]
    MRBT = lambda j, hh: bufs.g1b[hr(hh), j * 256 + 64 : j * 256 + 128]
    MAKT = lambda j, hh: bufs.g1b[hr(hh), j * 256 + 128 : j * 256 + 192]
    MRKT = lambda j, hh: bufs.g1b[hr(hh), j * 256 + 192 : j * 256 + 256]

    def Gp(lv, j, hh):
        if lv == 0:
            return bufs.Gb0[hr(hh), j * 64 : j * 64 + 64]
        o = ((lv - 1) * 8 + j) * 64
        return bufs.Gb14[hr(hh), o : o + 64]

    def GpT(lv, j, hh):
        if lv == 0:
            return GT1(j, hh)
        if lv == 1:
            return bufs.GTb1[hr(hh), j * 64 : j * 64 + 64]
        o = ((lv - 2) * 8 + j) * 64
        return bufs.GTb25[hr(hh), o : o + 64]

    import os as _os0
    BK_OLD = bool(_os0.environ.get("SCAN_BANKS_OLD"))

    def F(h, i):
        if BK_OLD:
            return psum[(2 + h) if i == 0 else (4 + h)]
        return psum[2 + 2 * h + i]

    def six(h):
        if BK_OLD:
            return psum[6][:, 4 * h * 64 : 4 * h * 64 + 256]
        return psum[6 + h][:, 0:256]

    def sixh(h, hh):
        if BK_OLD:
            return psum[6][hr(hh), h * 256 : h * 256 + 256]
        return psum[6 + h][hr(hh), 0:256]

    def sevh(h, hh):
        return psum[6 + h][hr(hh), 256:512]

    def js(h):
        return range(4 * h, 4 * h + 4)

    def pe_tmgr(e, h):
        pbh = psum[h]
        for j in js(h):
            for hh in range(2):
                for i, srct in enumerate((a_t(j, hh), b_t(j, hh),
                                          k_t(j, hh), v_t(j, hh))):
                    e.transpose(pbh[hr(hh), (j % 4) * 256 + i * 64 :
                                (j % 4) * 256 + i * 64 + 64],
                                srct, I(hh), tile_position=tp(hh))
        for j in js(h):
            bk_ = F(h, (j % 4) // 2)
            co = (j % 2) * 256
            for hh in range(2):
                e.matmul(bk_[hr(hh), co : co + 128], b_t(j, hh),
                         arpair(j, hh), start=True, stop=True,
                         tile_position=tp(hh))
                e.matmul(bk_[hr(hh), co + 128 : co + 256], k_t(j, hh),
                         arpair(j, hh), start=True, stop=True,
                         tile_position=tp(hh))
                gdst = (psum[6][hr(hh), j * 64 : j * 64 + 64] if BK_OLD else
                        sixh(h, hh)[:, (j % 4) * 64 : (j % 4) * 64 + 64])
                e.matmul(gdst, a_t(j, hh), b_t(j, hh),
                         start=True, stop=True, tile_position=tp(hh))

    def ev_tmgr(eV, eS, h):
        pbh = psum[h]
        src = pbh[:, 0:1024].rearrange("p (j u) -> p j u", u=256)
        dU = bufs.Xb[:, h * 512 : h * 512 + 512].rearrange(
            "p (j u) -> p j u", u=128)
        dT = bufs.TMb[:, h * 768 : h * 768 + 768].rearrange(
            "p (j u) -> p j u", u=192)
        _cp(eS, dU[:, :, 0:64], src[:, :, 0:64])
        _cp(eS, dT[:, :, 0:192], src[:, :, 64:256])
        eV.tensor_mul(bufs.g1b[:, h * 1024 : h * 1024 + 512],
                      F(h, 0)[:, 0:512], consts.mask_g2[:])
        eV.tensor_mul(bufs.g1b[:, h * 1024 + 512 : h * 1024 + 1024],
                      F(h, 1)[:, 0:512], consts.mask_g2[:])
        eV.tensor_mul(bufs.Gb0[:, h * 256 : h * 256 + 256], six(h),
                      consts.mask_sl8[:, 0:256])

    def pe_sq(e, h, lv):
        for j in js(h):
            bk_ = psum[2 + j // 4] if BK_OLD else F(h, 0)
            co = (j % 4) * 128
            for hh in range(2):
                if lv < 5:
                    e.matmul(bk_[hr(hh), co : co + 64],
                             GpT(lv - 1, j, hh), Gp(lv - 1, j, hh),
                             start=True, stop=True, tile_position=tp(hh))
                e.matmul(bk_[hr(hh), co + 64 : co + 128],
                         Gp(lv - 1, j, hh), GpT(lv - 1, j, hh),
                         start=True, stop=True, tile_position=tp(hh))

    def ev_sq(eV, eS, h, lv):
        sbank = psum[2 + h] if BK_OLD else F(h, 0)
        src = sbank[:, 0:512].rearrange("p (j u) -> p j u", u=128)
        if lv < 5:
            dG = (bufs.Gb14[:, ((lv - 1) * 8 + 4 * h) * 64 :
                            ((lv - 1) * 8 + 4 * h + 4) * 64]
                  .rearrange("p (j u) -> p j u", u=64))
            _cp(eV, dG[:, :, :], src[:, :, 0:64])
        if lv == 1:
            dT = bufs.GTb1[:, h * 256 : h * 256 + 256].rearrange(
                "p (j u) -> p j u", u=64)
        else:
            dT = (bufs.GTb25[:, ((lv - 2) * 8 + 4 * h) * 64 :
                             ((lv - 2) * 8 + 4 * h + 4) * 64]
                  .rearrange("p (j u) -> p j u", u=64))
        _cp(eV, dT[:, :, :], src[:, :, 64:128])

    def pe_xloc(e, h):
        for j in js(h):
            for hh in range(2):
                xdst = (psum[6][hr(hh), j * 64 : j * 64 + 64] if BK_OLD else
                        sixh(h, hh)[:, (j % 4) * 64 : (j % 4) * 64 + 64])
                e.matmul(xdst, MAKT(j, hh), vtm(j, hh),
                         start=True, stop=True, tile_position=tp(hh))

    def ev_xloc(eV, eS, h):
        dSA = bufs.Xb[:, h * 512 : h * 512 + 512].rearrange(
            "p (j u) -> p j u", u=128)
        _cp(eV, dSA[:, :, 64:128],
            six(h).rearrange("p (j u) -> p j u", u=64))

    def pe_chain(e, h, lv):
        for j in js(h):
            cbank = psum[4 + j // 4] if BK_OLD else F(h, 1)
            for hh in range(2):
                e.matmul(cbank[hr(hh), (j % 4) * 128 : (j % 4) * 128 + 128],
                         GpT(lv, j, hh), X2(j, hh),
                         start=True, stop=True, tile_position=tp(hh))

    def ev_chain(eV, eS, h, lv):
        cbank = psum[4 + h] if BK_OLD else F(h, 1)
        eV.tensor_add(bufs.Xb[:, h * 512 : h * 512 + 512],
                      cbank[:, 0:512],
                      bufs.Xb[:, h * 512 : h * 512 + 512])

    import os as _os
    FMM_OLD = bool(_os.environ.get("SCAN_FMM_OLD"))
    SMM_OLD = bool(_os.environ.get("SCAN_SMM_OLD"))

    def pe_fmm(e, h):
        for j in js(h):
            for hh in range(2):
                if FMM_OLD:
                    e.matmul(psum[6][hr(hh), j * 64 : j * 64 + 64],
                             aU(j, hh), btm(j, hh),
                             start=True, stop=True, tile_position=tp(hh))
                    e.matmul(psum[7][hr(hh), j * 64 : j * 64 + 64],
                             aU(j, hh), MRBT(j, hh),
                             start=True, stop=True, tile_position=tp(hh))
                    continue
                e.matmul(sixh(h, hh)[:, (j % 4) * 64 : (j % 4) * 64 + 64],
                         aU(j, hh), btm(j, hh),
                         start=True, stop=True, tile_position=tp(hh))
                e.matmul(sevh(h, hh)[:, (j % 4) * 64 : (j % 4) * 64 + 64],
                         aU(j, hh), MRBT(j, hh),
                         start=True, stop=True, tile_position=tp(hh))

    def ev_fmm(eV, eS, h):
        dst = ht_t[:].rearrange("p (j u) -> p j u", u=CB * 64)
        srcht = (psum[6][:, 4 * h * 64 : 4 * h * 64 + 256] if FMM_OLD
                 else six(h))
        _cp(eV, dst[:, 4 * h : 4 * h + 4, c * 64 : c * 64 + 64],
            srcht.rearrange("p (j u) -> p j u", u=64))
        dq = qf_t[:].rearrange("p (j u) -> p j u", u=CB * 64)
        rsl = bufs.ar[:].rearrange("p (u t) -> p u t", t=Bt)
        srcqf = (psum[7][:, 4 * h * 64 : 4 * h * 64 + 256] if FMM_OLD
                 else psum[6 + h][:, 256:512])
        eV.tensor_add(dq[:, 4 * h : 4 * h + 4, c * 64 : c * 64 + 64],
                      srcqf.rearrange("p (j u) -> p j u", u=64),
                      rsl[:, 8 * h + 1 : 8 * h + 8 : 2, cb : cb + 64])

    def pe_smm(e, h):
        for j in js(h):
            bk_ = psum[2 + j // 4] if BK_OLD else F(h, 0)
            co = (j % 4) * 128
            Mj = lambda hh: bufs.M[hr(hh), j * 128 : j * 128 + 128]
            Zj = lambda hh: bufs.M[hr(hh), j * 128 : j * 128 + 64]
            htj = lambda hh: ht_t[hr(hh), (j * CB + c) * 64 :
                                  (j * CB + c) * 64 + 64]
            qfj = lambda hh: qf_t[hr(hh), (j * CB + c) * 64 :
                                  (j * CB + c) * 64 + 64]
            for hh in range(2):
                e.matmul(bk_[hr(hh), co : co + 128], htj(hh), Mj(hh),
                         start=True, stop=False, tile_position=tp(hh))
                e.matmul(bk_[hr(hh), co : co + 64], btm(j, hh), xSA(j, hh),
                         start=False, stop=False, tile_position=tp(hh))
                e.matmul(bk_[hr(hh), co : co + 64], ktm(j, hh), vtm(j, hh),
                         start=False, stop=True, tile_position=tp(hh))
            for hh in range(2):
                if SMM_OLD:
                    yb = psum[4] if hh == 0 else psum[5]
                    yo = j * 64
                else:
                    yb = F(h, 1)
                    yo = (j % 4) * 64 + hh * 256
                e.matmul(yb[hr(hh), yo : yo + 64], xSA(j, hh), MRBT(j, hh),
                         start=True, stop=False, tile_position=tp(hh))
                e.matmul(yb[hr(hh), yo : yo + 64], vtm(j, hh), MRKT(j, hh),
                         start=False, stop=False, tile_position=tp(hh))
                e.matmul(yb[hr(hh), yo : yo + 64], Zj(hh), qfj(hh),
                         start=False, stop=True, tile_position=tp(hh))

    def ev_smm(eV, eS, h):
        mbank = psum[2 + h] if BK_OLD else F(h, 0)
        eV.tensor_add(bufs.mt[:, h * 512 : h * 512 + 512],
                      mbank[:, 0:512], bufs.M[:, h * 512 : h * 512 + 512])
        dst = y_out_sb.rearrange("p (j t) -> p j t", t=y_stride)
        if SMM_OLD:
            _cp(eS, dst[0:64, 4 * h : 4 * h + 4,
                        y_base + cb : y_base + cb + 64],
                psum[4][0:64, 4 * h * 64 : 4 * h * 64 + 256].rearrange(
                    "p (j u) -> p j u", u=64))
            _cp(eS, dst[64:128, 4 * h : 4 * h + 4,
                        y_base + cb : y_base + cb + 64],
                psum[5][64:128, 4 * h * 64 : 4 * h * 64 + 256].rearrange(
                    "p (j u) -> p j u", u=64))
        else:
            yb = F(h, 1)
            _cp(eV, dst[0:64, 4 * h : 4 * h + 4,
                        y_base + cb : y_base + cb + 64],
                yb[0:64, 0:256].rearrange("p (j u) -> p j u", u=64))
            _cp(eV, dst[64:128, 4 * h : 4 * h + 4,
                        y_base + cb : y_base + cb + 64],
                yb[64:128, 256:512].rearrange("p (j u) -> p j u", u=64))

    def ev_mdl(eV, eS, h):
        for j in js(h):
            eS.activation(
                bufs.M[:, j * 128 : j * 128 + 128],
                bufs.mt[:, j * 128 : j * 128 + 128], AF.Copy,
                scale=dl_t[:, j * nch_tot + dl_cb + c :
                           j * nch_tot + dl_cb + c + 1])

    ladder = [(pe_tmgr, ev_tmgr),
              (lambda e, h: (pe_sq(e, h, 1), pe_xloc(e, h)),
               lambda v, sc, h: (ev_sq(v, sc, h, 1), ev_xloc(v, sc, h))),
              (lambda e, h: (pe_sq(e, h, 2), pe_chain(e, h, 0)),
               lambda v, sc, h: (ev_sq(v, sc, h, 2), ev_chain(v, sc, h, 0))),
              (lambda e, h: (pe_sq(e, h, 3), pe_chain(e, h, 1)),
               lambda v, sc, h: (ev_sq(v, sc, h, 3), ev_chain(v, sc, h, 1))),
              (lambda e, h: (pe_sq(e, h, 4), pe_chain(e, h, 2)),
               lambda v, sc, h: (ev_sq(v, sc, h, 4), ev_chain(v, sc, h, 2))),
              (lambda e, h: (pe_sq(e, h, 5), pe_chain(e, h, 3)),
               lambda v, sc, h: (ev_sq(v, sc, h, 5), ev_chain(v, sc, h, 3))),
              (lambda e, h: pe_chain(e, h, 4),
               lambda v, sc, h: ev_chain(v, sc, h, 4)),
              (lambda e, h: pe_chain(e, h, 5),
               lambda v, sc, h: ev_chain(v, sc, h, 5)),
              (pe_fmm, ev_fmm),
              (pe_smm, ev_smm),
              (None, ev_mdl)]

    return ladder


class _Rec:

    def __init__(self, kind):
        self.__dict__["_kind"] = kind
        self.__dict__["ops"] = []

    def __getattr__(self, name):
        def call(*a, **kw):
            self.ops.append((name, a, kw))
        return call


def _replay(ops):
    def f(e):
        for name, a, kw in ops:
            getattr(e, name)(*a, **kw)
    return f


def emit_scan_block_v2_old(s, bufs, psum, consts, Bt, y_out_sb, first_block,
                       y_base=0, y_stride=None, dl_t=None, dl_cb=0,
                       nch_tot=None, qf_t=None, ht_t=None):
    CB = Bt // L
    if y_stride is None:
        y_stride = Bt
    if first_block:
        def minit(e):
            e.memset(
                bufs.M[:].rearrange("p (j u) -> p j u", u=128)[:, :, 0:64], 0.0
            )
            for j in range(NCC):
                e.tensor_copy(bufs.M[:, j * 128 + 64 : j * 128 + 128],
                              consts.i64b[:])

        s.on("vector", minit)
        s.barrier(waiters=("tensor",))

    for c in range(CB):
        emit_chunk_v2_old(s, bufs, psum, consts, Bt, c, y_out_sb, y_base,
                      y_stride, dl_t, dl_cb, nch_tot, qf_t, ht_t)


def emit_chunk_v2_old(s, bufs, psum, consts, Bt, c, y_out_sb, y_base, y_stride,
                  dl_t, dl_cb, nch_tot, qf_t, ht_t):
    CB = Bt // L
    cb = c * L
    hr = lambda hh: slice(64 * hh, 64 * hh + 64)
    tp = lambda hh: (64 * hh, 64 * hh)
    I = lambda hh: consts.i64b[hr(hh), :]
    a_t = lambda j, hh: bufs.ar[hr(hh), j * 2 * Bt + cb : j * 2 * Bt + cb + L]
    b_t = lambda j, hh: bufs.bk[hr(hh), j * 2 * Bt + cb : j * 2 * Bt + cb + L]
    k_t = lambda j, hh: bufs.bk[hr(hh),
                                j * 2 * Bt + Bt + cb : j * 2 * Bt + Bt + cb + L]
    v_t = lambda j, hh: bufs.vf[hr(hh), j * Bt + cb : j * Bt + cb + L]

    def arpair(j, hh):
        r3 = bufs.ar[hr(hh), :].rearrange("p (u t) -> p u t", t=Bt)
        return r3[:, 2 * j : 2 * j + 2, cb : cb + L]

    aU = lambda j, hh: bufs.Xb[hr(hh), j * 128 : j * 128 + 64]
    xSA = lambda j, hh: bufs.Xb[hr(hh), j * 128 + 64 : j * 128 + 128]
    X2 = lambda j, hh: bufs.Xb[hr(hh), j * 128 : j * 128 + 128]
    btm = lambda j, hh: bufs.TMb[hr(hh), j * 192 : j * 192 + 64]
    ktm = lambda j, hh: bufs.TMb[hr(hh), j * 192 + 64 : j * 192 + 128]
    vtm = lambda j, hh: bufs.TMb[hr(hh), j * 192 + 128 : j * 192 + 192]
    GT1 = lambda j, hh: bufs.g1b[hr(hh), j * 256 : j * 256 + 64]
    MRBT = lambda j, hh: bufs.g1b[hr(hh), j * 256 + 64 : j * 256 + 128]
    MAKT = lambda j, hh: bufs.g1b[hr(hh), j * 256 + 128 : j * 256 + 192]
    MRKT = lambda j, hh: bufs.g1b[hr(hh), j * 256 + 192 : j * 256 + 256]

    def Gp(lv, j, hh):
        if lv == 0:
            return bufs.Gb0[hr(hh), j * 64 : j * 64 + 64]
        o = ((lv - 1) * 8 + j) * 64
        return bufs.Gb14[hr(hh), o : o + 64]

    def GpT(lv, j, hh):
        if lv == 0:
            return GT1(j, hh)
        if lv == 1:
            return bufs.GTb1[hr(hh), j * 64 : j * 64 + 64]
        o = ((lv - 2) * 8 + j) * 64
        return bufs.GTb25[hr(hh), o : o + 64]

    J = NCC

    def tmm(e, jlo, jhi):
        for j in range(jlo, jhi):
            for hh in range(2):
                for i, src in enumerate((a_t(j, hh), b_t(j, hh),
                                         k_t(j, hh), v_t(j, hh))):
                    e.transpose(psum[0][hr(hh),
                                        (j - jlo) * 256 + i * 64 :
                                        (j - jlo) * 256 + i * 64 + 64],
                                src, I(hh), tile_position=tp(hh))

    def tmev(e_s, jlo):
        src = psum[0][:, 0:1024].rearrange("p (j u) -> p j u", u=256)
        dU = bufs.Xb[:, jlo * 128 : (jlo + 4) * 128].rearrange(
            "p (j u) -> p j u", u=128)
        dT = bufs.TMb[:, jlo * 192 : (jlo + 4) * 192].rearrange(
            "p (j u) -> p j u", u=192)
        _cp(e_s, dU[:, :, 0:64], src[:, :, 0:64])
        _cp(e_s, dT[:, :, 0:192], src[:, :, 64:256])

    def sqmm(e, lv):
        for j in range(J):
            bk_ = psum[2 + j // 4]
            co = (j % 4) * 128
            for hh in range(2):
                if lv < 5:
                    e.matmul(bk_[hr(hh), co : co + 64],
                             GpT(lv - 1, j, hh), Gp(lv - 1, j, hh),
                             start=True, stop=True, tile_position=tp(hh))
                e.matmul(bk_[hr(hh), co + 64 : co + 128],
                         Gp(lv - 1, j, hh), GpT(lv - 1, j, hh),
                         start=True, stop=True, tile_position=tp(hh))

    def sqev(e_v, e_g, lv):
        for half in range(2):
            src = psum[2 + half][:, 0:512].rearrange("p (j u) -> p j u", u=128)
            if lv < 5:
                dG = (bufs.Gb14[:, ((lv - 1) * 8 + half * 4) * 64 :
                                ((lv - 1) * 8 + half * 4 + 4) * 64]
                      .rearrange("p (j u) -> p j u", u=64))
                _cp(e_v, dG[:, :, :], src[:, :, 0:64])
            if lv == 1:
                dT = bufs.GTb1[:, half * 256 : half * 256 + 256].rearrange(
                    "p (j u) -> p j u", u=64)
            else:
                dT = (bufs.GTb25[:, ((lv - 2) * 8 + half * 4) * 64 :
                                 ((lv - 2) * 8 + half * 4 + 4) * 64]
                      .rearrange("p (j u) -> p j u", u=64))
            _cp(e_g, dT[:, :, :], src[:, :, 64:128])

    def chmm(e, lv):
        for j in range(J):
            bk_ = psum[4 + j // 4]
            e2 = e
            for hh in range(2):
                e2.matmul(bk_[hr(hh), (j % 4) * 128 : (j % 4) * 128 + 128],
                          GpT(lv, j, hh), X2(j, hh),
                          start=True, stop=True, tile_position=tp(hh))

    def chev(e, lv):
        for half in range(2):
            e.tensor_add(bufs.Xb[:, half * 512 : half * 512 + 512],
                         psum[4 + half][:, 0:512],
                         bufs.Xb[:, half * 512 : half * 512 + 512])

    def ph0(e):
        tmm(e, 0, 4)
        for j in range(J):
            bk_ = psum[2 + j // 2]
            co = (j % 2) * 256
            for hh in range(2):
                e.matmul(bk_[hr(hh), co : co + 128], b_t(j, hh), arpair(j, hh),
                         start=True, stop=True, tile_position=tp(hh))
                e.matmul(bk_[hr(hh), co + 128 : co + 256], k_t(j, hh),
                         arpair(j, hh),
                         start=True, stop=True, tile_position=tp(hh))
                e.matmul(psum[6][hr(hh), j * 64 : j * 64 + 64],
                         a_t(j, hh), b_t(j, hh),
                         start=True, stop=True, tile_position=tp(hh))

    s.on("tensor", ph0)
    s.barrier(waiters=("vector", "scalar", "gpsimd"))

    s.on("scalar", lambda e: tmev(e, 0))
    for b_i, eng in ((0, "vector"), (1, "vector"), (2, "vector"),
                     (3, "vector")):
        s.on(eng, lambda e, b_i=b_i: e.tensor_mul(
            bufs.g1b[:, b_i * 512 : b_i * 512 + 512],
            psum[2 + b_i][:, 0:512], consts.mask_g2[:]))
    s.on("vector", lambda e: e.tensor_mul(bufs.Gb0[:], psum[6][:, 0:512],
                                          consts.mask_sl8[:]))
    s.barrier(waiters=("tensor",))

    def ph2(e):
        tmm(e, 4, 8)
        sqmm(e, 1)

    s.on("tensor", ph2)
    s.barrier(waiters=("vector", "scalar", "gpsimd"))

    s.on("scalar", lambda e: tmev(e, 4))
    s.on("vector", lambda e: sqev(e, e, 1))
    s.barrier(waiters=("tensor",))

    def ph4(e):
        for j in range(J):
            for hh in range(2):
                e.matmul(psum[6][hr(hh), j * 64 : j * 64 + 64],
                         MAKT(j, hh), vtm(j, hh),
                         start=True, stop=True, tile_position=tp(hh))
        sqmm(e, 2)

    s.on("tensor", ph4)
    s.barrier(waiters=("vector", "scalar", "gpsimd"))

    def xev(e):
        dSA = bufs.Xb[:].rearrange("p (j u) -> p j u", u=128)
        _cp(e, dSA[:, :, 64:128],
            psum[6][:, 0:512].rearrange("p (j u) -> p j u", u=64))

    s.on("scalar", xev)
    s.on("vector", lambda e: sqev(e, e, 2))
    s.barrier(waiters=("tensor",))

    for lv in range(6):
        def phc(e, lv=lv):
            chmm(e, lv)
            if lv + 3 <= 5:
                sqmm(e, lv + 3)

        s.on("tensor", phc)
        s.barrier(waiters=("vector", "scalar", "gpsimd"))

        def phe_v(e, lv=lv):
            chev(e, lv)

        s.on("vector", phe_v)
        if lv + 3 <= 5:
            s.on("scalar", lambda e, lv=lv: sqev(e, e, lv + 3))
        s.barrier(waiters=("tensor",))

    def ph18(e):
        for j in range(J):
            for hh in range(2):
                e.matmul(psum[6][hr(hh), j * 64 : j * 64 + 64],
                         aU(j, hh), btm(j, hh),
                         start=True, stop=True, tile_position=tp(hh))
                e.matmul(psum[7][hr(hh), j * 64 : j * 64 + 64],
                         aU(j, hh), MRBT(j, hh),
                         start=True, stop=True, tile_position=tp(hh))

    s.on("tensor", ph18)
    s.barrier(waiters=("vector", "scalar", "gpsimd"))

    def fht(e):
        dst = ht_t[:].rearrange("p (j u) -> p j u", u=CB * 64)
        _cp(e, dst[:, :, c * 64 : c * 64 + 64],
            psum[6][:, 0:512].rearrange("p (j u) -> p j u", u=64))

    def fqf(e):
        dst = qf_t[:].rearrange("p (j u) -> p j u", u=CB * 64)
        rsl = bufs.ar[:].rearrange("p (u t) -> p u t", t=Bt)
        e.tensor_add(dst[:, :, c * 64 : c * 64 + 64],
                     psum[7][:, 0:512].rearrange("p (j u) -> p j u", u=64),
                     rsl[:, 1:16:2, cb : cb + 64])

    s.on("scalar", fht)
    s.on("vector", fqf)
    s.barrier(waiters=("tensor",))

    def ph20(e):
        for j in range(J):
            bk_ = psum[2 + j // 4]
            co = (j % 4) * 128
            Mj = lambda hh: bufs.M[hr(hh), j * 128 : j * 128 + 128]
            Zj = lambda hh: bufs.M[hr(hh), j * 128 : j * 128 + 64]
            htj = lambda hh: ht_t[hr(hh), (j * CB + c) * 64 :
                                  (j * CB + c) * 64 + 64]
            qfj = lambda hh: qf_t[hr(hh), (j * CB + c) * 64 :
                                  (j * CB + c) * 64 + 64]
            for hh in range(2):
                e.matmul(bk_[hr(hh), co : co + 128], htj(hh), Mj(hh),
                         start=True, stop=False, tile_position=tp(hh))
                e.matmul(bk_[hr(hh), co : co + 64], btm(j, hh), xSA(j, hh),
                         start=False, stop=False, tile_position=tp(hh))
                e.matmul(bk_[hr(hh), co : co + 64], ktm(j, hh), vtm(j, hh),
                         start=False, stop=True, tile_position=tp(hh))
            for hh in range(2):
                yb = psum[4] if hh == 0 else psum[5]
                e.matmul(yb[hr(hh), j * 64 : j * 64 + 64], xSA(j, hh),
                         MRBT(j, hh),
                         start=True, stop=False, tile_position=tp(hh))
                e.matmul(yb[hr(hh), j * 64 : j * 64 + 64], vtm(j, hh),
                         MRKT(j, hh),
                         start=False, stop=False, tile_position=tp(hh))
                e.matmul(yb[hr(hh), j * 64 : j * 64 + 64], Zj(hh), qfj(hh),
                         start=False, stop=True, tile_position=tp(hh))

    s.on("tensor", ph20)
    s.barrier(waiters=("vector", "scalar", "gpsimd"))

    def ph21v(e):
        e.tensor_add(bufs.mt[:, 0:512], psum[2][:, 0:512], bufs.M[:, 0:512])
        e.tensor_add(bufs.mt[:, 512:1024], psum[3][:, 0:512],
                     bufs.M[:, 512:1024])

    def ph21s(e):
        dst = y_out_sb.rearrange("p (j t) -> p j t", t=y_stride)
        e.activation(dst[0:64, :, y_base + cb : y_base + cb + 64],
                     psum[4][0:64, 0:512].rearrange("p (j u) -> p j u", u=64),
                     AF.Copy)
        e.activation(dst[64:128, :, y_base + cb : y_base + cb + 64],
                     psum[5][64:128, 0:512].rearrange("p (j u) -> p j u", u=64),
                     AF.Copy)

    s.on("vector", ph21v)
    s.on("scalar", ph21s)
    s.barrier(waiters=("vector", "gpsimd"))

    def ph22(e_v, e_g):
        for j in range(J):
            eng = e_v if j < 4 else e_g
            eng.tensor_scalar(
                bufs.M[:, j * 128 : j * 128 + 128],
                bufs.mt[:, j * 128 : j * 128 + 128],
                dl_t[:, j * nch_tot + dl_cb + c : j * nch_tot + dl_cb + c + 1],
                None, ALU.mult)

    s.on("vector", lambda e: ph22(e, e))
    s.barrier(waiters=("tensor",))





class FullBufs:

    def __init__(self, nc, Bt, T):
        assert Bt == 256
        CB = Bt // L
        NCH = T // L
        a = nc.alloc_sbuf_tensor
        self.psum = [nc.alloc_psum_tensor("pb0", [128, 1024], BF16),
                     nc.alloc_psum_tensor("pb1", [128, 1024], BF16)] + [
            nc.alloc_psum_tensor(f"pb{i}", [128, 512], F32) for i in range(2, 8)
        ]
        self.w_r = a("w_r", [128, 8 * 1024], BF16)
        self.w_k = a("w_k", [128, 8 * 1024], BF16)
        self.w_v = a("w_v", [128, 8 * 1024], BF16)
        self.maa_w1 = a("maa_w1", [128, 8 * 192], BF16)
        self.maa_w2a = a("maa_w2a", [128, 1024], BF16)
        self.maa_w2b = a("maa_w2b", [64, 1024], BF16)
        self.dec_w1 = a("dec_w1", [128, 8 * 64], BF16)
        self.dec_w2 = a("dec_w2", [64, 1024], BF16)
        self.aaa_w1 = a("aaa_w1", [128, 8 * 64], BF16)
        self.aaa_w2 = a("aaa_w2", [64, 1024], BF16)
        self.kkk_w1 = a("kkk_w1", [128, 8 * 64], BF16)
        self.kkk_w2 = a("kkk_w2", [64, 1024], BF16)
        self.gate_w1 = a("gate_w1", [128, 8 * 128], BF16)
        self.gate_w2 = a("gate_w2", [128, 1024], BF16)
        for nm in ("maa_x", "maa_r", "maa_w", "maa_k", "maa_v", "maa_a",
                   "maa_g", "td_half", "aaaaa_half", "faaaa", "ln_w", "ln_b",
                   "ffn_bias", "wq_scale"):
            setattr(self, "v_" + nm, a("v_" + nm, [128, 8], F32))
        self.v_scr = a("v_scr", [128, 8], F32)
        self.onehot = a("onehot", [128, 8], F32)
        self.M = a("sc_M", [128, NCC * 128], BF16)
        self.dl_all = a("dl_all", [128, NCC * NCH], F32)
        self.carry = a("sc_carry", [128, NCC], F32)
        self.carry2 = a("sc_carry2", [128, NCC], F32)
        self.offs = a("sc_offs", [128, NCC * CB], F32)
        self.rows = a("rows", [1, 520], F32R)
        self.rows2 = a("rows2", [1, 520], F32R)
        self.rows3 = a("rows3", [2, 2 * Bt], F32R)
        self.rows4 = a("rows4", [2, 2 * Bt], F32R)
        self.dummy = a("sep_dummy", [1, 8], F32)
        self.invrow = a("invrow", [1, Bt], F32R)
        self.sclrow = a("sclrow", [1, Bt], F32R)
        self.amax = a("amax", [128, 2], F32R)
        self.amax2 = a("amax2", [128, 2], F32R)
        self.mid64 = a("mid64", [64, Bt], BF16)
        self.mid128 = a("mid128", [128, Bt], BF16)
        self.mid2_64 = a("mid2_64", [64, Bt], BF16)
        self.mid2_128 = a("mid2_128", [128, Bt], BF16)
        self.zi = a("zi", [128, NCC * 64], BF16)
        self.cum = a("cum", [128, NCC * 64], BF16)
        self.ssel = a("ssel", [128, NCC * 64], BF16)
        self.pfx = a("pfx", [128, NCC * 64], BF16)

        ARENA = 114432
        self.arena = a("arena", [128, ARENA], mybir.dt.uint8)
        base = nc.lookup_mloc(self.arena).addr

        def at(name, shape, dtype, off):
            return nc.alloc_sbuf_tensor_at(name, shape, dtype,
                                           offset=base + off)

        W1 = Bt + 2
        O = {}
        o = 0

        def reg(key, size):
            nonlocal o
            O[key] = o
            o += (size + 255) // 256 * 256

        reg("A", 8448)
        reg("B", 8448)
        reg("C", 8192)
        reg("D", 8192)
        reg("E", 8192)
        reg("F", 8192)
        reg("G", 4352)
        reg("H", 4096)
        reg("I", 4096)
        reg("J", 4096)
        reg("K", 4096)
        reg("L", 8192)
        reg("Mr", 4096)
        reg("Nr", 8192)
        reg("Or", 2816)
        reg("Pr", 16384)
        reg("Q", 4096)
        assert o <= ARENA, o

        self.x_blk = at("x_blk", [128, NCC * W1], F32R, O["A"])
        self.r_p = at("r_p", [128, NCC * Bt], BF16, O["A"])
        self.ktil = at("ktil", [128, NCC * Bt], BF16, O["A"] + 4224)
        self.ln1 = at("ln1", [128, NCC * W1], F32, O["B"])
        self.kk_f = at("kk_f", [128, NCC * Bt], F32R, O["B"])
        self.xx = at("xx", [128, NCC * Bt], F32R, O["C"])
        self.kkn = at("kkn", [128, NCC * Bt], BF16, O["C"])
        self.b_pre = at("b_pre", [128, NCC * Bt], BF16, O["C"] + 4096)
        self.th = at("th", [128, NCC * Bt], F32, O["D"])
        self.coef_blk = at("coef_blk", [2, NCC * Bt], F32, O["D"])
        self.u_s = at("u_s", [128, NCC * Bt], F32, O["E"])
        self.lnq = self.u_s
        self.rkf = at("rkf", [128, NCC * Bt], F32R, O["E"])
        self.src = at("src", [128, NCC * Bt], F32, O["F"])
        self.sq = at("sq", [128, NCC * W1], BF16, O["G"])
        self.xc = at("xc", [128, NCC * Bt], BF16, O["H"])
        self.kmul = at("kmul", [128, NCC * Bt], BF16, O["I"])
        self.a2n = at("a2n", [128, NCC * Bt], BF16, O["J"])
        self.g_p = at("g_p", [128, NCC * Bt], BF16, O["K"])
        self.slog = at("slog", [128, NCC * Bt], F32, O["L"])
        self.y0_blk = at("y0_blk", [128, NCC * Bt], BF16, O["L"])
        self.qf_blk = at("qf_blk", [128, NCC * CB * 64], BF16, O["L"] + 4096)
        self.ep = at("ep", [128, NCC * Bt], BF16, O["Mr"])
        self.ht_blk = at("ht_blk", [128, NCC * CB * 64], BF16, O["Mr"])
        self.em = at("em", [128, NCC * Bt], BF16, O["Nr"])
        self.eprev = at("eprev", [128, NCC * Bt], BF16, O["Nr"] + 4096)
        self.tm = at("sc_tm", [128, 192], BF16, O["Or"])
        self.X = at("sc_X", [128, 128], BF16, O["Or"] + 384)
        self.g1 = at("sc_g1", [128, 256], BF16, O["Or"] + 640)
        self.P = at("sc_P", [128, 704], BF16, O["Or"] + 1152)
        self.HT = at("sc_HT", [128, 64], BF16, O["Or"] + 2560)
        self.Qf = at("sc_Qf", [128, 64], BF16, O["Or"] + 2688)
        self.pfx_all = at("pfx_all", [128, 8 * 512], BF16, O["A"])
        self.ar = at("sc_ar", [128, NCC * 2 * Bt], BF16, O["Pr"])
        self.bk = at("sc_bk", [128, NCC * 2 * Bt], BF16, O["Pr"] + 8192)
        self.vf = at("sc_vf", [128, NCC * Bt], BF16, O["Q"])
        self.TMb = at("sc_tmb", [128, 8 * 192], BF16, O["A"])
        self.Xb = at("sc_xb", [128, 8 * 128], BF16, O["A"] + 3072)
        self.Gb0 = at("sc_gb0", [128, 8 * 64], BF16, O["A"] + 5120)
        self.GTb1 = at("sc_gtb1", [128, 8 * 64], BF16, O["A"] + 6144)
        self.g1b = at("sc_g1b", [128, 8 * 256], BF16, O["B"])
        self.mt = at("sc_mt", [128, 1024], F32, O["B"] + 4096)
        self.Gb14 = at("sc_gb14", [128, 8 * 256], BF16, O["C"])
        self.GTb25 = at("sc_gtb25", [128, 8 * 256], BF16, O["C"] + 4096)
        self.v_p = self.vf
        self.Mg = at("Mg", [128, 8 * 1024], BF16, O["Pr"])
        self.wq_bf = at("wq_bf", [128, 4096], BF16, O["Pr"])
        self.ffn_stage = nc.alloc_sbuf_tensor_at(
            "ffn_stage", [128, 4096], F32,
            offset=nc.lookup_mloc(self.w_v).addr)
        self.w_o = nc.alloc_sbuf_tensor_at(
            "w_o", [128, 8 * 1024], BF16, offset=nc.lookup_mloc(self.w_r).addr)
        self.wq_t = nc.alloc_sbuf_tensor_at(
            "wq_t", [128, 8 * 1024], BF16,
            offset=nc.lookup_mloc(self.w_k).addr)
        self.x_blk2 = at("x_blk2", [128, NCC * Bt], F32R, O["A"])
        self.x3 = at("x3", [128, NCC * Bt], F32, O["A"])
        self.x2 = at("x2", [128, NCC * Bt], F32R, O["B"])
        self.att_s = at("att_s", [128, NCC * Bt], F32R, O["C"])
        self.ln3 = at("ln3", [128, NCC * Bt], F32R, O["C"])
        self.tmp1 = at("tmp_e1", [128, NCC * Bt], F32R, O["E"])
        self.tmp2 = at("tmp_e2", [128, NCC * Bt], F32R, O["F"])
        self.sqy = at("sqy", [128, NCC * Bt], BF16, O["G"])
        self.xq = at("xq", [128, NCC * Bt], BF16, O["H"])
        self.yg = at("yg", [128, NCC * Bt], BF16, O["I"])
        self.coef_p2 = at("coef_p2", [2, NCC * Bt], F32, O["Nr"])
        self.rjA = at("rjA", [2, 2048], F32R, O["D"])
        self.rjB1 = at("rjB1", [2, 1024], F32R, O["Mr"])
        self.rjB2 = at("rjB2", [2, 1024], F32R, O["J"])
        self.rjm = at("rjm", [2, 1024], F32R, O["H"])
        self.rjm2 = at("rjm2", [2, 1024], F32R, O["I"])
        self.y_blk = at("y_blk", [128, NCC * Bt], F32R, O["Pr"])
        self.ffo = at("ffo", [128, NCC * Bt], F32R, O["Pr"] + 8192)


class FullConsts(ConstSB):
    def __init__(self, nc, s):
        super().__init__(nc, s)
        a = nc.alloc_sbuf_tensor
        self.ones_col_r = a("cs_onescol_r", [128, 1], F32R)
        self.ones_col2_r = a("cs_onescol2_r", [128, 2], F32R)
        self.onesrow_r = a("cs_onesrow_r", [1, 128], F32R)
        self.negrow_r = a("cs_negrow_r", [1, 128], F32R)
        self.ones2_128 = a("cs_ones2_128", [2, 128], F32R)
        self.i128_r = a("cs_i128_r", [128, 128], F32R)
        self.i128_b = a("cs_i128_b", [128, 128], BF16)
        self.ones_col_b = a("cs_onescol_b", [128, 1], BF16)
        self.ones_col2_b = a("cs_onescol2_b", [128, 2], BF16)
        self.ones2_128f = a("cs_ones2_128f", [2, 128], F32)
        self.eps1 = a("cs_eps1", [128, 1], F32)
        self.eps64 = a("cs_eps64", [128, 1], F32)
        self.mask_g2 = a("cs_maskg2", [128, 512], BF16)
        self.mask_sl8 = a("cs_masksl8", [128, 512], BF16)

    def declare_inputs(self, nc):
        super().declare_inputs(nc)
        self.d_ones_col = nc.dram_tensor("c_ones_col", [128, 1], F32,
                                         kind="ExternalInput").ap()
        self.d_ones_col2 = nc.dram_tensor("c_ones_col2", [128, 2], F32,
                                          kind="ExternalInput").ap()
        self.d_onesrow = nc.dram_tensor("c_onesrow", [1, 128], F32,
                                        kind="ExternalInput").ap()
        self.d_negrow = nc.dram_tensor("c_negrow", [1, 128], F32,
                                       kind="ExternalInput").ap()
        self.d_ones2_128 = nc.dram_tensor("c_ones2_128", [2, 128], F32,
                                          kind="ExternalInput").ap()
        self.d_i128 = nc.dram_tensor("c_i128", [128, 128], F32,
                                     kind="ExternalInput").ap()
        self.d_eps1 = nc.dram_tensor("c_eps1", [128, 1], F32,
                                     kind="ExternalInput").ap()
        self.d_eps64 = nc.dram_tensor("c_eps64", [128, 1], F32,
                                      kind="ExternalInput").ap()
        self.d_mask_g2 = nc.dram_tensor("c_mask_g2", [128, 512], F32,
                                        kind="ExternalInput").ap()
        self.d_mask_sl8 = nc.dram_tensor("c_mask_sl8", [128, 512], F32,
                                         kind="ExternalInput").ap()

    def emit_load(self, s):
        super().emit_load(s)
        s.dma("gpsimd", "cloadg", self.ones_col_r[:], self.d_ones_col)
        s.dma("gpsimd", "cloadg", self.ones_col2_r[:], self.d_ones_col2)
        s.dma("gpsimd", "cloadg", self.onesrow_r[:], self.d_onesrow)
        s.dma("gpsimd", "cloadg", self.negrow_r[:], self.d_negrow)
        s.dma("gpsimd", "cloadg", self.ones2_128[:], self.d_ones2_128)
        s.dma("gpsimd", "cloadg", self.i128_r[:], self.d_i128)
        s.dma("gpsimd", "cloadg", self.i128_b[:], self.d_i128)
        s.dma("gpsimd", "cloadg", self.ones_col_b[:], self.d_ones_col)
        s.dma("gpsimd", "cloadg", self.ones_col2_b[:], self.d_ones_col2)
        s.dma("sync", "cload", self.ones2_128f[:], self.d_ones2_128)
        s.dma("sync", "cload", self.eps1[:], self.d_eps1)
        s.dma("sync", "cload", self.eps64[:], self.d_eps64)
        s.dma("gpsimd", "cloadg", self.mask_g2[:], self.d_mask_g2)
        s.dma("gpsimd", "cloadg", self.mask_sl8[:], self.d_mask_sl8)


def full_const_arrays():
    c = make_const_arrays()
    oc = np.ones((128, 1), np.float32)
    oc2 = np.zeros((128, 2), np.float32)
    oc2[0:64, 0] = 1.0
    oc2[64:128, 1] = 1.0
    orow = np.ones((1, 128), np.float32)
    o2 = np.zeros((2, 128), np.float32)
    o2[0, 0:64] = 1.0
    o2[1, 64:128] = 1.0
    c.update({
        "ones_col": oc, "ones_col2": oc2, "onesrow": orow,
        "negrow": -orow, "ones2_128": o2, "i128": np.eye(128, dtype=np.float32),
        "eps1": np.full((128, 1), 1e-5, np.float32),
        "eps64": np.full((128, 1), 64e-5, np.float32),
        "mask_g2": np.concatenate([c["mask_g1"], c["mask_g1"]], axis=1),
        "mask_sl8": np.tile(c["mask_sl"], (1, 8)),
    })
    return c


MAGIC = float(np.float32(1.5 * 2 ** 23))


def emit_ln_stats(s, bufs, psum, consts, src, sq, W, eps, rm_off=0):
    rows, rows2 = bufs.rows, bufs.rows2

    def dosq(e):
        for j in range(NCC):
            e.activation(sq[:, j * W : (j + 1) * W], src[:, j * W : (j + 1) * W],
                         AF.Square)

    s.on("scalar", dosq)
    s.barrier(waiters=("tensor",))

    def sums(e):
        for j in range(NCC):
            e.matmul(psum[5][0:1, 0:W], consts.ones_col_r[:],
                     src[:, j * W : (j + 1) * W],
                     start=(j == 0), stop=(j == NCC - 1))
        for j in range(NCC):
            e.matmul(psum[7][0:1, 0:W], consts.ones_col_b[:],
                     sq[:, j * W : (j + 1) * W],
                     start=(j == 0), stop=(j == NCC - 1))

    s.on("tensor", sums)
    s.barrier(waiters=("vector", "scalar"))

    ro = rm_off

    def rowops1(e):
        e.tensor_scalar_mul(rows[0:1, ro : ro + W], psum[5][0:1, 0:W],
                            1.0 / 1024.0)

    s.on("vector", rowops1)
    s.barrier(waiters=("vector",))

    s.on("vector", lambda e: e.tensor_mul(
        rows[0:1, ro + W : ro + 2 * W], rows[0:1, ro : ro + W],
        rows[0:1, ro : ro + W]))
    s.barrier(waiters=("vector",))

    def rowops2(e):
        e.scalar_tensor_tensor(
            rows2[0:1, ro + W : ro + 2 * W],
            rows[0:1, ro + W : ro + 2 * W].bitcast(F32), -1024.0,
            psum[7][0:1, 0:W], ALU.mult, ALU.add)

    s.on("vector", rowops2)
    s.barrier(waiters=("scalar",))

    s.on("scalar", lambda e: e.activation(
        rows[0:1, ro + W : ro + 2 * W], rows2[0:1, ro + W : ro + 2 * W],
        AF.Ln, scale=1.0 / 1024.0, bias=eps[0:1, 0:1]))
    s.barrier(waiters=("scalar",))
    s.on("scalar", lambda e: e.activation(
        rows2[0:1, ro + W : ro + 2 * W], rows[0:1, ro + W : ro + 2 * W],
        AF.Exp, scale=-0.5))
    s.barrier(waiters=("tensor",))

    def bcasts(e):
        e.matmul(psum[2][:, 0:W], consts.onesrow_r[:],
                 rows2[0:1, ro + W : ro + 2 * W], start=True, stop=True)
        e.matmul(psum[6][:, 0:W], consts.onesrow_r[:],
                 rows[0:1, ro : ro + W], start=True, stop=True)

    s.on("tensor", bcasts)
    s.barrier(waiters=("vector", "gpsimd"))


def emit_ln_apply(s, src, out, psum, W, extra_add=None, out2=None):

    def ap1(e):
        for j in range(NCC):
            e.tensor_sub(out[:, j * W : (j + 1) * W],
                         src[:, j * W : (j + 1) * W], psum[6][:, 0:W])

    s.on("vector", ap1)
    s.barrier(waiters=("vector",))

    def ap2(e):
        for j in range(NCC):
            e.tensor_mul(out[:, j * W : (j + 1) * W],
                         out[:, j * W : (j + 1) * W], psum[2][:, 0:W])

    s.on("vector", ap2)
    s.barrier(waiters=("vector", "gpsimd"))
    if extra_add is not None:
        def ap3(e):
            for j in range(NCC):
                e.tensor_add((out2 or out)[:, j * W : (j + 1) * W],
                             out[:, j * W : (j + 1) * W],
                             extra_add[:, j * W : (j + 1) * W])

        s.on("vector", ap3)
        s.barrier()


def emit_proj_1024(s, psum, w_sb, act, Bt, evac, banks=(7, 2, 3, 6),
                   open_group=False, evac_engine=None):

    regions = []
    for o in range(8):
        bank = psum[banks[o % 4]]
        half = (o // 4) * Bt
        regions.append(bank[:, half : half + Bt])

    def mmhalf(e, os_):
        for o in os_:
            for i in range(8):
                e.matmul(regions[o],
                         w_sb[:, i * 1024 + o * 128 : i * 1024 + o * 128 + 128],
                         act[:, i * Bt : (i + 1) * Bt],
                         start=(i == 0),
                         stop=(i == 7 and not open_group))

    if evac_engine is not None:
        s.on("tensor", lambda e: mmhalf(e, range(0, 4)))
        s.barrier(waiters=(evac_engine,))

        def evh(e, os_):
            for o in os_:
                evac(e, o, regions[o])

        s.on("tensor", lambda e: mmhalf(e, range(4, 8)))
        s.on(evac_engine, lambda e: evh(e, range(0, 4)))
        s.barrier(waiters=(evac_engine,))
        s.on(evac_engine, lambda e: evh(e, range(4, 8)))
        s.barrier()
        return regions, None

    s.on("tensor", lambda e: mmhalf(e, range(8)))
    s.barrier(waiters=("vector", "scalar", "gpsimd"))

    if evac is None:
        return regions, None

    def evacs(e):
        for o in range(8):
            evac(e, o, regions[o])

    return regions, evacs


def emit_small_path(s, psum, w1_sb, w2_sb, act, Bt, mid, tanh_bias, evac,
                    accum_regions=None, tanh_scale=1.0, mid_sb=None,
                    evac_engine="scalar"):
    def mm1(e):
        for i in range(8):
            e.matmul(psum[5][0:mid, 0:Bt],
                     w1_sb[:, i * mid : (i + 1) * mid],
                     act[:, i * Bt : (i + 1) * Bt],
                     start=(i == 0), stop=(i == 7))

    s.on("tensor", mm1)
    s.barrier(waiters=("scalar",))

    def tev(e):
        kw = {}
        if tanh_bias is not None:
            kw["bias"] = tanh_bias
        e.activation(mid_sb[0:mid, 0:Bt], psum[5][0:mid, 0:Bt], AF.Tanh,
                     scale=tanh_scale, **kw)

    s.on("scalar", tev)
    s.barrier(waiters=("tensor",))

    if accum_regions is not None:
        def mm2(e):
            for o in range(8):
                e.matmul(accum_regions[o],
                         w2_sb[0:mid, o * 128 : o * 128 + 128],
                         mid_sb[0:mid, 0:Bt],
                         start=False, stop=True)

        s.on("tensor", mm2)
        s.barrier(waiters=("vector", "scalar", "gpsimd"))
        return None
    else:
        regions = []
        for o in range(8):
            bank = psum[(7, 2, 3, 6)[o % 4]]
            half = (o // 4) * Bt
            regions.append(bank[:, half : half + Bt])

        def mm2(e):
            for o in range(8):
                e.matmul(regions[o],
                         w2_sb[0:mid, o * 128 : o * 128 + 128],
                         mid_sb[0:mid, 0:Bt],
                         start=True, stop=True)

        s.on("tensor", mm2)
        s.barrier(waiters=("vector", "scalar", "gpsimd"))

        def evacs(e):
            for o in range(8):
                evac(e, o, regions[o])

        s.on(evac_engine, evacs)
        s.barrier()
        return None


"""Full sandwich-module per-core program builder."""
import os

CEXP = float(np.exp(-0.5))

MAA_NAMES = ("maa_r", "maa_w", "maa_k", "maa_v", "maa_a", "maa_g")
S_ORDER = (1, 4, 5, 0, 3, 2)


def declare_io(nc, T, ncores=8):
    io = {}
    d = lambda nm, shape, dt=F32, kind="ExternalInput": nc.dram_tensor(
        nm, shape, dt, kind=kind
    ).ap()
    io["x"] = d("x_in", [1024, T + 1])
    io["W_r"] = d("W_r", [1024, 1024])
    io["W_k"] = d("W_k", [1024, 1024])
    io["W_v"] = d("W_v", [1024, 1024])
    io["W_o"] = d("W_o", [1024, 1024])
    io["W_ffn"] = d("W_ffn", [1024, 1024])
    io["maa_w1"] = d("maa_w1_d", [1024, 192])
    io["maa_w2a"] = d("maa_w2a_d", [128, 1024])
    io["maa_w2b"] = d("maa_w2b_d", [64, 1024])
    io["dec_w1"] = d("dec_w1_d", [1024, 64])
    io["dec_w2"] = d("dec_w2_d", [64, 1024])
    io["aaa_w1"] = d("aaa_w1_d", [1024, 64])
    io["aaa_w2"] = d("aaa_w2_d", [64, 1024])
    io["kkk_w1"] = d("kkk_w1_d", [1024, 64])
    io["kkk_w2"] = d("kkk_w2_d", [64, 1024])
    io["gate_w1"] = d("gate_w1_d", [1024, 128])
    io["gate_w2"] = d("gate_w2_d", [128, 1024])
    for nm in ("maa_x",) + MAA_NAMES + ("td", "aaaaa", "faaaa", "ln_w",
                                        "ln_b", "ffn_bias"):
        io[nm] = d("v_" + nm + "_d", [128, 8])
    io["onehot"] = d("onehot_d", [128, 8])
    io["out"] = d("x_out", [1024, T], F32, kind="ExternalOutput")
    NCH = T // L
    io["g_hbm"] = nc.dram_tensor("g_hbm", [128, NCC * T], BF16)
    io["v_hbm"] = nc.dram_tensor("v_hbm", [128, NCC * T], BF16)
    io["y0_hbm"] = nc.dram_tensor("y0_hbm", [128, NCC * T], BF16)
    io["qf_hbm"] = nc.dram_tensor("qf_hbm", [128, NCC * NCH * 64], BF16)
    io["ht_hbm"] = nc.dram_tensor("ht_hbm", [128, NCC * NCH * 64], BF16)
    io["coef_hbm"] = nc.dram_tensor("coef_hbm", [2, NCC * T], F32)
    io["agin"] = nc.dram_tensor("agin", [128, 1024], BF16)
    io["agout"] = nc.dram_tensor("agout", [ncores, 128, 1024], BF16)
    io["ncores"] = ncores
    return io


def _blkslice(ap, T, bB, Bt):
    return ap.rearrange("p (j t) -> p j t", j=NCC)[:, :, bB : bB + Bt]


def emit_weight_load(s, nc, bufs, consts, io):
    consts.emit_load(s)
    g = lambda out, in_: s.dma("gpsimd", "wload", out, in_, defer=True)

    def re_w(ap):
        return ap.rearrange("(i p) c -> p i c", p=128)

    def re_o(t):
        return t[:].rearrange("p (i c) -> p i c", i=8)

    g(re_o(bufs.w_r), re_w(io["W_r"]))
    g(re_o(bufs.w_k), re_w(io["W_k"]))
    g(re_o(bufs.w_v), re_w(io["W_v"]))
    g(bufs.maa_w1[:].rearrange("p (i c) -> p i c", i=8),
      io["maa_w1"].rearrange("(i p) c -> p i c", p=128))
    g(bufs.maa_w2a[:], io["maa_w2a"])
    g(bufs.maa_w2b[:], io["maa_w2b"])
    for nm in ("dec_w1", "aaa_w1", "kkk_w1"):
        g(getattr(bufs, nm)[:].rearrange("p (i c) -> p i c", i=8),
          io[nm].rearrange("(i p) c -> p i c", p=128))
    for nm in ("dec_w2", "aaa_w2", "kkk_w2", "gate_w2"):
        g(getattr(bufs, nm)[:], io[nm])
    g(bufs.gate_w1[:].rearrange("p (i c) -> p i c", i=8),
      io["gate_w1"].rearrange("(i p) c -> p i c", p=128))
    for nm in ("maa_x",) + MAA_NAMES + ("faaaa", "ln_w", "ln_b", "ffn_bias"):
        s.dma("sync", "vload", getattr(bufs, "v_" + nm)[:], io[nm])
    s.dma("sync", "vload", bufs.v_td_half[:], io["td"])
    s.dma("sync", "vload", bufs.v_aaaaa_half[:], io["aaaaa"])
    s.dma("sync", "vload", bufs.onehot[:], io["onehot"])
    s.barrier()

    def halve(e):
        e.tensor_scalar_mul(bufs.v_td_half[:], bufs.v_td_half[:], 0.5)
        e.tensor_scalar_mul(bufs.v_aaaaa_half[:], bufs.v_aaaaa_half[:], 0.5)
        e.memset(bufs.dummy[:], 0.0)

    s.on("vector", halve)
    s.barrier()


def emit_prep2(s, nc, bufs, consts, io):
    psum = bufs.psum
    s.dma("gpsimd", "wload2", bufs.w_o[:].rearrange("p (i c) -> p i c", i=8),
          io["W_o"].rearrange("(i p) c -> p i c", p=128))
    for hhalf in range(2):
        s.barrier()
        s.dma("sync", "wload2s",
              bufs.ffn_stage[:].rearrange("p (q c) -> p q c", q=4),
              io["W_ffn"].rearrange("(q p) c -> p q c", p=128)[
                  :, hhalf * 4 : hhalf * 4 + 4, :])
        s.barrier()

        def amax(e, hhalf=hhalf):
            for q in range(4):
                e.reduce_max(
                    bufs.v_wq_scale[:, hhalf * 4 + q : hhalf * 4 + q + 1],
                    bufs.ffn_stage[:, q * 1024 : (q + 1) * 1024],
                    mybir.AxisListType.X,
                    apply_absolute_value=True,
                )

        s.on("vector", amax)
        s.barrier(waiters=("vector",))

        def winv(e, hhalf=hhalf):
            for q in range(4):
                e.reciprocal(
                    bufs.v_scr[:, q : q + 1],
                    bufs.v_wq_scale[:, hhalf * 4 + q : hhalf * 4 + q + 1],
                )

        s.on("vector", winv)
        s.barrier(waiters=("vector",))

        def wq1(e):
            for q in range(4):
                e.tensor_scalar_mul(bufs.v_scr[:, q : q + 1],
                                    bufs.v_scr[:, q : q + 1], 127.0)

        s.on("vector", wq1)
        s.barrier(waiters=("vector",))

        def wq2(e):
            for q in range(4):
                e.tensor_scalar(
                    bufs.ffn_stage[:, q * 1024 : (q + 1) * 1024],
                    bufs.ffn_stage[:, q * 1024 : (q + 1) * 1024],
                    bufs.v_scr[:, q : q + 1], MAGIC, ALU.mult, ALU.add,
                )

        s.on("vector", wq2)
        s.barrier(waiters=("vector",))

        def wq3(e):
            for q in range(4):
                e.tensor_scalar_add(
                    bufs.ffn_stage[:, q * 1024 : (q + 1) * 1024],
                    bufs.ffn_stage[:, q * 1024 : (q + 1) * 1024], -MAGIC,
                )

        s.on("vector", wq3)
        s.barrier(waiters=("vector",))

        def wqc(e):
            for q in range(4):
                e.tensor_copy(bufs.wq_bf[:, q * 1024 : (q + 1) * 1024],
                              bufs.ffn_stage[:, q * 1024 : (q + 1) * 1024])

        s.on("vector", wqc)
        s.barrier(waiters=("tensor",))

        for qp in range(2):
            def wtg(e, qp=qp):
                for dq in range(2):
                    q = qp * 2 + dq
                    for ii in range(8):
                        e.transpose(
                            bufs.psum[dq][:, ii * 128 : ii * 128 + 128],
                            bufs.wq_bf[:, q * 1024 + ii * 128 :
                                       q * 1024 + ii * 128 + 128],
                            consts.i128_b[:],
                        )

            s.on("tensor", wtg)
            s.barrier(waiters=("vector",))

            def wte(e, qp=qp, hhalf=hhalf):
                for dq in range(2):
                    q = qp * 2 + dq
                    co = hhalf * 4 + q
                    d3 = bufs.wq_t[:].rearrange(
                        "p (i c) -> p i c", i=8)[:, :, co * 128 : co * 128 + 128]
                    s3 = bufs.psum[dq][:, 0:1024].rearrange(
                        "p (i c) -> p i c", i=8)
                    e.tensor_copy(d3, s3)

            s.on("vector", wte)
            s.barrier(waiters=("tensor",))


    def wsc(e):
        e.tensor_scalar_mul(bufs.v_wq_scale[:], bufs.v_wq_scale[:],
                            1.0 / 127.0)

    s.on("vector", wsc)
    s.barrier()


def emit_mix(s, bufs, consts, Bt, sx, ln1b, co=None):
    co = co or {}
    psum = bufs.psum
    if sx < 4:
        lhsw, rbase = bufs.maa_w2a, 32 * sx
        rhs_mid = bufs.mid128[32 * sx : 32 * sx + 32, 0:Bt]
    else:
        lhsw, rbase = bufs.maa_w2b, 32 * (sx - 4)
        rhs_mid = bufs.mid64[32 * (sx - 4) : 32 * (sx - 4) + 32, 0:Bt]

    regions = []
    for o in range(8):
        bank = psum[(7, 2, 3, 6)[o % 4]]
        half = (o // 4) * Bt
        regions.append(bank[:, half : half + Bt])

    def msmm(e, lhsw=lhsw, rhs_mid=rhs_mid, rbase=rbase):
        for o in range(8):
            e.matmul(regions[o],
                     lhsw[rbase : rbase + 32, o * 128 : o * 128 + 128],
                     rhs_mid, start=True, stop=True,
                     tile_position=(rbase, 0))

    s.on("tensor", msmm)
    for eng, fn in co.get(1, ()):
        s.on(eng, fn)
    s.barrier(waiters=("vector", "scalar") if co else ("vector",))

    vmaa = getattr(bufs, "v_" + MAA_NAMES[sx])

    def msev(e, vmaa=vmaa):
        for o in range(8):
            e.scalar_tensor_tensor(
                bufs.u_s[:, o * Bt : (o + 1) * Bt], regions[o],
                vmaa[:, o : o + 1],
                bufs.xx[:].bitcast(F32)[:, o * Bt : (o + 1) * Bt],
                ALU.add, ALU.mult,
            )

    s.on("vector", msev)
    for eng, fn in co.get(2, ()):
        s.on(eng, fn)
    s.barrier(waiters=("gpsimd", "scalar") if co else ("gpsimd",))

    def msadd(e):
        for o in range(8):
            e.tensor_add(bufs.xc[:, o * Bt : (o + 1) * Bt],
                         bufs.u_s[:, o * Bt : (o + 1) * Bt], ln1b(o))

    s.on("gpsimd", msadd)
    for eng, fn in co.get(3, ()):
        s.on(eng, fn)
    s.barrier(waiters=("tensor", "vector", "scalar"))


def emit_pass1_block(s, nc, bufs, consts, io, T, Bt, b):
    psum = bufs.psum
    CB = Bt // L
    NCH = T // L
    W1 = Bt + 2
    bB = b * Bt

    s.dma("gpsimd", "xload",
          bufs.x_blk[:].rearrange("p (j t) -> p j t", j=NCC)[:, :, 0 : Bt + 1],
          io["x"].rearrange("(j p) t -> p j t", p=128)[:, :, bB : bB + Bt + 1])
    s.on("vector", lambda e: e.memset(bufs.x_blk[:].bitcast(F32).rearrange(
        "p (j t) -> p j t", j=NCC)[:, :, Bt + 1 : Bt + 2], 0.0))
    s.barrier()

    emit_ln_stats(s, bufs, psum, consts, bufs.x_blk[:], bufs.sq[:], W1, consts.eps1)
    emit_ln_apply(s, bufs.x_blk[:].bitcast(F32), bufs.ln1[:], psum, W1)

    def mkxx(e):
        for j in range(NCC):
            e.tensor_sub(bufs.xx[:, j * Bt : (j + 1) * Bt].bitcast(F32),
                         bufs.ln1[:, j * W1 : j * W1 + Bt],
                         bufs.ln1[:, j * W1 + 1 : j * W1 + Bt + 1])

    s.on("vector", mkxx)
    s.barrier(waiters=("vector",))

    ln1b = lambda j: bufs.ln1[:, j * W1 + 1 : j * W1 + Bt + 1]

    def mkxmix(e):
        for j in range(NCC):
            e.scalar_tensor_tensor(
                bufs.xc[:, j * Bt : (j + 1) * Bt],
                bufs.xx[:].bitcast(F32)[:, j * Bt : (j + 1) * Bt],
                bufs.v_maa_x[:, j : j + 1], ln1b(j), ALU.mult, ALU.add,
            )

    s.on("vector", mkxmix)
    if b == 0:
        s.dma_commit("wload")
    s.barrier(waiters=("tensor",))

    def mixmm(e):
        for i in range(8):
            e.matmul(psum[5][0:128, 0:Bt],
                     bufs.maa_w1[:, i * 192 : i * 192 + 128],
                     bufs.xc[:, i * Bt : (i + 1) * Bt],
                     start=(i == 0), stop=(i == 7))
        for i in range(8):
            e.matmul(psum[4][0:64, 0:Bt],
                     bufs.maa_w1[:, i * 192 + 128 : i * 192 + 192],
                     bufs.xc[:, i * Bt : (i + 1) * Bt],
                     start=(i == 0), stop=(i == 7))

    s.on("tensor", mixmm)
    s.barrier(waiters=("scalar",))

    def mixev(e):
        e.activation(bufs.mid128[:, 0:Bt], psum[5][0:128, 0:Bt], AF.Tanh)
        e.activation(bufs.mid64[:, 0:Bt], psum[4][0:64, 0:Bt], AF.Tanh)

    s.on("scalar", mixev)
    s.barrier(waiters=("tensor",))

    s.dma_commit("spill")
    co_pending = None
    for sx in S_ORDER:
        emit_mix(s, bufs, consts, Bt, sx, ln1b, co=co_pending)
        co_pending = None
        if sx == 1:
            def decev(e, o, region):
                e.activation(bufs.th[:, o * Bt : (o + 1) * Bt], region,
                             AF.Tanh, scale=0.5,
                             bias=bufs.v_td_half[:, o : o + 1])

            emit_small_path(s, psum, bufs.dec_w1, bufs.dec_w2, bufs.xc, Bt,
                            64, None, decev, mid_sb=bufs.mid2_64)

            def mksrc(e):
                for j in range(NCC):
                    e.tensor_scalar(
                        bufs.src[:, j * Bt : (j + 1) * Bt],
                        bufs.th[:, j * Bt : (j + 1) * Bt],
                        1.0, CEXP / 2.0, ALU.add, ALU.mult,
                    )

            def mklnq(e):
                for j in range(NCC):
                    e.activation(bufs.th[:, j * Bt : (j + 1) * Bt],
                                 bufs.src[:, j * Bt : (j + 1) * Bt], AF.Ln)

            def mkkmul(e):
                for j in range(NCC):
                    e.activation(bufs.kmul[:, j * Bt : (j + 1) * Bt],
                                 bufs.th[:, j * Bt : (j + 1) * Bt],
                                 AF.Exp, scale=0.5)

            co_pending = {1: [("vector", mksrc)], 2: [("scalar", mklnq)],
                          3: [("scalar", mkkmul)]}
        elif sx == 4:
            def aaaev(e, o, region):
                e.activation(bufs.u_s[:, o * Bt : (o + 1) * Bt], region,
                             AF.Tanh, scale=0.5,
                             bias=bufs.v_aaaaa_half[:, o : o + 1])

            emit_small_path(s, psum, bufs.aaa_w1, bufs.aaa_w2, bufs.xc, Bt,
                            64, None, aaaev, mid_sb=bufs.mid2_64)

            def mka2n(e):
                for j in range(NCC):
                    e.tensor_scalar(
                        bufs.a2n[:, j * Bt : (j + 1) * Bt],
                        bufs.u_s[:, j * Bt : (j + 1) * Bt],
                        1.0, -1.0, ALU.add, ALU.mult,
                    )

            co_pending = {1: [("vector", mka2n)]}
        elif sx == 5:
            def gev(e, o, region):
                e.tensor_copy(bufs.g_p[:, o * Bt : (o + 1) * Bt], region)

            emit_small_path(s, psum, bufs.gate_w1, bufs.gate_w2, bufs.xc, Bt,
                            128, None, gev, mid_sb=bufs.mid2_128,
                            evac_engine="vector")
            s.dma("sync", "spill", _blkslice(io["g_hbm"].ap(), T, bB, Bt),
                  bufs.g_p[:].rearrange("p (j t) -> p j t", j=NCC),
                  defer=True)
        elif sx == 0:
            def rev(e, o, region):
                e.tensor_copy(bufs.r_p[:, o * Bt : (o + 1) * Bt], region)

            _, evr = emit_proj_1024(s, psum, bufs.w_r, bufs.xc, Bt, rev)
            s.on("vector", evr)
            s.barrier()
        elif sx == 3:
            def vev(e, o, region):
                e.tensor_copy(bufs.v_p[:, o * Bt : (o + 1) * Bt], region)

            _, evv = emit_proj_1024(s, psum, bufs.w_v, bufs.xc, Bt, vev)
            s.on("vector", evv)
            s.barrier()
            s.dma("sync", "spill", _blkslice(io["v_hbm"].ap(), T, bB, Bt),
                  bufs.v_p[:].rearrange("p (j t) -> p j t", j=NCC),
                  defer=True)
        elif sx == 2:
            def kkmm1(e):
                for i in range(8):
                    e.matmul(psum[5][0:64, 0:Bt],
                             bufs.kkk_w1[:, i * 64 : (i + 1) * 64],
                             bufs.xc[:, i * Bt : (i + 1) * Bt],
                             start=(i == 0), stop=(i == 7))

            s.on("tensor", kkmm1)
            s.barrier(waiters=("scalar",))
            s.on("scalar", lambda e: e.activation(
                bufs.mid2_64[0:64, 0:Bt], psum[5][0:64, 0:Bt], AF.Tanh))

            kregions, _ = emit_proj_1024(s, psum, bufs.w_k, bufs.xc, Bt,
                                         None)

            def kev(e):
                for o in range(8):
                    e.tensor_mul(bufs.ktil[:, o * Bt : (o + 1) * Bt],
                                 kregions[o],
                                 bufs.kmul[:, o * Bt : (o + 1) * Bt])

            s.on("vector", kev)
            s.barrier(waiters=("tensor",))

            def kacc(e):
                for o in range(8):
                    e.matmul(kregions[o],
                             bufs.kkk_w2[0:64, o * 128 : o * 128 + 128],
                             bufs.mid2_64[0:64, 0:Bt],
                             start=False, stop=True, skip_group_check=True)

            s.on("tensor", kacc)
            s.barrier(waiters=("vector",))

            def kkev(e):
                for o in range(8):
                    e.tensor_copy(bufs.kk_f[:, o * Bt : (o + 1) * Bt],
                                  kregions[o])

            s.on("vector", kkev)
            s.barrier(waiters=("tensor", "scalar"))

            def kksq(e):
                for j in range(NCC):
                    e.activation(bufs.sq[:, j * Bt : (j + 1) * Bt],
                                 bufs.kk_f[:, j * Bt : (j + 1) * Bt],
                                 AF.Square)

            s.on("scalar", kksq)
            s.barrier(waiters=("tensor",))

            def kksum(e):
                for j in range(NCC):
                    e.matmul(psum[5][0:1, 0:Bt], consts.ones_col_b[:],
                             bufs.sq[:, j * Bt : (j + 1) * Bt],
                             start=(j == 0), stop=(j == NCC - 1))

            s.on("tensor", kksum)
            s.barrier(waiters=("scalar",))
            s.on("scalar", lambda e: e.activation(
                bufs.rows[0:1, 0:Bt], psum[5][0:1, 0:Bt], AF.Ln))
            s.barrier(waiters=("scalar",))
            s.on("scalar", lambda e: e.activation(
                bufs.rows2[0:1, 0:Bt], bufs.rows[0:1, 0:Bt], AF.Exp,
                scale=-0.5))
            s.barrier(waiters=("vector",))
            s.on("vector", lambda e: e.tensor_scalar_min(
                bufs.rows2[0:1, 0:Bt], bufs.rows2[0:1, 0:Bt], 1e12))
            s.barrier(waiters=("tensor",))
            s.on("tensor", lambda e: e.matmul(
                psum[2][:, 0:Bt], consts.negrow_r[:], bufs.rows2[0:1, 0:Bt],
                start=True, stop=True))
            s.barrier(waiters=("vector",))

            def mkkkn(e):
                for j in range(NCC):
                    e.tensor_mul(bufs.kkn[:, j * Bt : (j + 1) * Bt],
                                 bufs.kk_f[:, j * Bt : (j + 1) * Bt],
                                 psum[2][:, 0:Bt])

            s.on("vector", mkkkn)
            s.barrier(waiters=("vector", "gpsimd"))

            def mkbpre(e):
                for j in range(NCC):
                    e.tensor_mul(bufs.b_pre[:, j * Bt : (j + 1) * Bt],
                                 bufs.kkn[:, j * Bt : (j + 1) * Bt],
                                 bufs.a2n[:, j * Bt : (j + 1) * Bt])

            s.on("gpsimd", mkbpre)
            s.barrier(waiters=("vector",))

            def mkrkf(e):
                for j in range(NCC):
                    e.scalar_tensor_tensor(
                        bufs.rkf[:, j * Bt : (j + 1) * Bt],
                        bufs.r_p[:, j * Bt : (j + 1) * Bt],
                        bufs.v_faaaa[:, j : j + 1],
                        bufs.ktil[:, j * Bt : (j + 1) * Bt],
                        ALU.mult, ALU.mult,
                    )

            s.on("vector", mkrkf)
            s.barrier(waiters=("tensor",))

            CSLOT = ((4, 0), (5, 0), (6, 0), (7, 0))

            def coefmm(e):
                for q, (bk_, ba) in enumerate(CSLOT):
                    e.matmul(psum[bk_][ba : ba + 2, 0:512],
                             consts.ones_col2_r[:],
                             bufs.rkf[:, q * 512 : q * 512 + 512],
                             start=True, stop=True)

            s.on("tensor", coefmm)
            s.barrier(waiters=("vector",))

            def coefev(e):
                for q, (bk_, ba) in enumerate(CSLOT):
                    e.tensor_copy(bufs.coef_blk[0:2, q * 512 : q * 512 + 512],
                                  psum[bk_][ba : ba + 2, 0:512])

            s.on("vector", coefev)
            s.barrier()
            s.dma("sync", "spill", _blkslice(io["coef_hbm"].ap(), T, bB, Bt),
                  bufs.coef_blk[:].rearrange("p (j t) -> p j t", j=NCC),
                  defer=True)

    emit_scan_prep(s, bufs, psum, consts, Bt, bufs.src, -1.0,
                   bufs.r_p, bufs.ktil, bufs.vf, bufs.kkn, bufs.b_pre,
                   first_block=(b == 0), dl_t=bufs.dl_all, dl_cb=b * CB,
                   nch_tot=NCH, blk_par=b % 2)
    (emit_scan_block_v2_old if os.environ.get("SCAN_OLD") else
     emit_scan_block_v2)(s, bufs, psum, consts, Bt, bufs.y0_blk,
                    first_block=(b == 0), y_base=0, y_stride=Bt,
                    dl_t=bufs.dl_all, dl_cb=b * CB, nch_tot=NCH,
                    qf_t=bufs.qf_blk, ht_t=bufs.ht_blk)
    s.barrier()
    s.dma("sync", "spill", _blkslice(io["y0_hbm"].ap(), T, bB, Bt),
          bufs.y0_blk[:].rearrange("p (j t) -> p j t", j=NCC), defer=True)
    for nm, t in (("qf_hbm", bufs.qf_blk), ("ht_hbm", bufs.ht_blk)):
        s.dma("sync", "spill", io[nm].ap().rearrange(
            "p (j c) -> p j c", j=NCC)[:, :, b * CB * 64 : (b + 1) * CB * 64],
            t[:].rearrange("p (j c) -> p j c", j=NCC), defer=True)
    s.barrier()


def emit_exchange(s, nc, bufs, consts, io):
    psum = bufs.psum
    ncores = io["ncores"]
    s.dma_commit("spill")
    s.dma("sync", "ag", io["agin"].ap(), bufs.M[:])
    s.barrier()
    ccsem = nc.alloc_semaphore(name="ccsem")

    def cc(e):
        e.collective_compute(
            "AllGather", mybir.AluOpType.bypass,
            replica_groups=[list(range(ncores))],
            ins=[io["agin"].ap().opt()],
            outs=[io["agout"].ap().opt()],
        ).then_inc(ccsem, 1)

    s.q["gpsimd"].append(("op", cc))
    s.q["gpsimd"].append(("wait", ccsem, 1))
    s.dirty.add("gpsimd")
    s.barrier()
    s.dma("sync", "ag2",
          bufs.Mg[:, 0 : ncores * 1024].rearrange("p (k c) -> p k c", k=ncores),
          io["agout"].ap().rearrange("k p c -> p k c"))
    s.barrier()

    def zinit(e):
        e.memset(bufs.cum[:], 0.0)
        e.memset(bufs.ssel[:], 0.0)

    s.on("vector", zinit)
    s.barrier(waiters=("vector", "tensor"))

    for kp in range((ncores + 3) // 4):
        def ptrb(e, kp=kp):
            for dk in range(4):
                k = kp * 4 + dk
                if k >= ncores:
                    continue
                pb = psum[dk // 2]
                base = (dk % 2) * 512
                for j in range(NCC):
                    for hh in range(2):
                        e.transpose(
                            pb[64 * hh : 64 * hh + 64,
                               base + j * 64 : base + j * 64 + 64],
                            bufs.Mg[64 * hh : 64 * hh + 64,
                                    k * 1024 + j * 128 + 64 :
                                    k * 1024 + j * 128 + 128],
                            consts.i64b[64 * hh : 64 * hh + 64, :],
                            tile_position=(64 * hh, 64 * hh),
                        )

        s.on("tensor", ptrb)
        s.barrier(waiters=("vector",))

        def ptev(e, kp=kp):
            e.tensor_copy(bufs.pfx_all[:, kp * 2048 : kp * 2048 + 1024],
                          psum[0][:, 0:1024])
            e.tensor_copy(bufs.pfx_all[:, kp * 2048 + 1024 : kp * 2048 + 2048],
                          psum[1][:, 0:1024])

        s.on("vector", ptev)
        s.barrier(waiters=("tensor",))

    for k in range(ncores):
        def sel(e, k=k):
            e.scalar_tensor_tensor(
                bufs.ssel[:], bufs.cum[:], bufs.onehot[:, k : k + 1],
                bufs.ssel[:], ALU.mult, ALU.add,
            )

        s.on("vector", sel)
        s.barrier(waiters=("tensor",))

        def comp(e, k=k):
            for j in range(NCC):
                for hh in range(2):
                    hr = slice(64 * hh, 64 * hh + 64)
                    e.matmul(psum[4][hr, j * 64 : j * 64 + 64],
                             bufs.pfx_all[hr, k * 512 + j * 64 :
                                          k * 512 + j * 64 + 64],
                             bufs.cum[hr, j * 64 : j * 64 + 64],
                             start=True, stop=False,
                             tile_position=(64 * hh, 64 * hh))
                    e.matmul(psum[4][hr, j * 64 : j * 64 + 64],
                             consts.i64b[hr, :],
                             bufs.Mg[hr, k * 1024 + j * 128 : k * 1024 + j * 128 + 64],
                             start=False, stop=True,
                             tile_position=(64 * hh, 64 * hh))

        s.on("tensor", comp)
        s.barrier(waiters=("vector",))
        s.on("vector", lambda e: e.tensor_copy(bufs.cum[:], psum[4][:, 0:512]))
        s.barrier(waiters=("vector", "tensor"))

    s.on("vector", lambda e: e.tensor_copy(bufs.zi[:], bufs.ssel[:]))
    s.barrier()


def emit_pass2_block(s, nc, bufs, consts, io, T, Bt, b):
    psum = bufs.psum
    CB = Bt // L
    NCH = T // L
    bB = b * Bt

    s.barrier()
    s.dma("gpsimd", "xload",
          bufs.x_blk2[:].rearrange("p (j t) -> p j t", j=NCC),
          io["x"].rearrange("(j p) t -> p j t", p=128)[:, :, bB + 1 : bB + Bt + 1])
    s.dma("sync", "p2late",
          bufs.g_p[:].rearrange("p (j t) -> p j t", j=NCC),
          _blkslice(io["g_hbm"].ap(), T, bB, Bt), defer=True)
    s.dma("sync", "p2late",
          bufs.v_p[:].rearrange("p (j t) -> p j t", j=NCC),
          _blkslice(io["v_hbm"].ap(), T, bB, Bt), defer=True)
    s.dma("sync", "p2load",
          bufs.y0_blk[:].rearrange("p (j t) -> p j t", j=NCC),
          _blkslice(io["y0_hbm"].ap(), T, bB, Bt))
    s.dma("sync", "p2late",
          bufs.coef_p2[:].rearrange("p (j t) -> p j t", j=NCC),
          _blkslice(io["coef_hbm"].ap(), T, bB, Bt), defer=True)
    s.dma("sync", "p2load",
          bufs.qf_blk[:].rearrange("p (j c) -> p j c", j=NCC),
          io["qf_hbm"].ap().rearrange("p (j c) -> p j c", j=NCC)[
              :, :, b * CB * 64 : (b + 1) * CB * 64])
    s.dma("sync", "p2load",
          bufs.ht_blk[:].rearrange("p (j c) -> p j c", j=NCC),
          io["ht_hbm"].ap().rearrange("p (j c) -> p j c", j=NCC)[
              :, :, b * CB * 64 : (b + 1) * CB * 64])
    s.barrier()

    for c in range(CB):
        def corr(e, c=c):
            for j in range(NCC):
                col = (j * CB + c) * 64
                for hh in range(2):
                    hr = slice(64 * hh, 64 * hh + 64)
                    yb = psum[2] if hh == 0 else psum[6]
                    e.matmul(yb[hr, j * 64 : j * 64 + 64],
                             bufs.zi[hr, j * 64 : j * 64 + 64],
                             bufs.qf_blk[hr, col : col + 64],
                             start=True, stop=True,
                             tile_position=(64 * hh, 64 * hh))
                    e.matmul(psum[7][hr, j * 64 : j * 64 + 64],
                             bufs.ht_blk[hr, col : col + 64],
                             bufs.zi[hr, j * 64 : j * 64 + 64],
                             start=True, stop=False,
                             tile_position=(64 * hh, 64 * hh))
                    e.matmul(psum[7][hr, j * 64 : j * 64 + 64],
                             consts.i64b[hr, :],
                             bufs.zi[hr, j * 64 : j * 64 + 64],
                             start=False, stop=True,
                             tile_position=(64 * hh, 64 * hh))

        s.on("tensor", corr)
        s.barrier(waiters=("vector", "scalar"))

        def zev(e, c=c):
            for j in range(NCC):
                e.tensor_scalar(
                    bufs.zi[:, j * 64 : j * 64 + 64],
                    psum[7][:, j * 64 : j * 64 + 64],
                    bufs.dl_all[:, j * NCH + b * CB + c :
                                j * NCH + b * CB + c + 1],
                    None, ALU.mult,
                )

        def yaddc(e, c=c):
            yv = bufs.y_blk[:].rearrange("p (j t) -> p j t", t=Bt)
            y0 = bufs.y0_blk[:].rearrange("p (j t) -> p j t", t=Bt)
            e.tensor_add(yv[0:64, :, c * 64 : c * 64 + 64],
                         psum[2][0:64, 0:512].rearrange(
                             "p (j u) -> p j u", u=64),
                         y0[0:64, :, c * 64 : c * 64 + 64])
            e.tensor_add(yv[64:128, :, c * 64 : c * 64 + 64],
                         psum[6][64:128, 0:512].rearrange(
                             "p (j u) -> p j u", u=64),
                         y0[64:128, :, c * 64 : c * 64 + 64])

        s.on("vector", zev)
        s.on("vector", yaddc)
        s.barrier(waiters=("tensor",))

    s.dma_commit("p2late")
    s.barrier(waiters=("scalar",))

    def gnsq(e):
        for j in range(NCC):
            e.activation(bufs.sqy[:, j * Bt : (j + 1) * Bt],
                         bufs.y_blk[:, j * Bt : (j + 1) * Bt], AF.Square)

    s.on("scalar", gnsq)
    s.barrier(waiters=("tensor",))

    GSL_S = ((2, 0), (3, 0), (4, 0), (5, 0))
    GSL_Q = ((2, 32), (3, 32), (2, 64), (3, 64))

    def rowslot(j):
        if j < 4:
            return bufs.rjA, j * 512
        if j < 6:
            return bufs.rjB1, (j - 4) * 512
        return bufs.rjB2, (j - 6) * 512

    def gnsum(e):
        for q in range(4):
            bk_, ba = GSL_S[q]
            e.matmul(psum[bk_][ba : ba + 2, 0:512], consts.ones_col2_r[:],
                     bufs.y_blk[:, q * 512 : q * 512 + 512],
                     start=True, stop=True)
            bk_, ba = GSL_Q[q]
            e.matmul(psum[bk_][ba : ba + 2, 0:512], consts.ones_col2_b[:],
                     bufs.sqy[:, q * 512 : q * 512 + 512],
                     start=True, stop=True)

    s.on("tensor", gnsum)
    s.barrier(waiters=("vector",))

    def rjmx(j):
        return (bufs.rjm if j < 4 else bufs.rjm2), (j % 4) * 256

    def rmath(e):
        for j in range(NCC):
            t, o = rowslot(j)
            q, hh = j // 2, j % 2
            sb, sa = GSL_S[q]
            qb, qa = GSL_Q[q]
            e.tensor_scalar_mul(
                t[0:2, o : o + 256],
                psum[sb][sa : sa + 2, hh * 256 : hh * 256 + 256],
                1.0 / 64.0)
            e.tensor_scalar_mul(
                t[0:2, o + 256 : o + 512],
                psum[qb][qa : qa + 2, hh * 256 : hh * 256 + 256],
                1.0 / 64.0)

    s.on("vector", rmath)
    s.barrier(waiters=("vector",))

    def rm2(e):
        for j in range(NCC):
            t, o = rowslot(j)
            m, mo = rjmx(j)
            e.tensor_mul(m[0:2, mo : mo + 256],
                         t[0:2, o : o + 256], t[0:2, o : o + 256])

    s.on("vector", rm2)
    s.barrier(waiters=("vector",))

    def rvar(e):
        for j in range(NCC):
            t, o = rowslot(j)
            m, mo = rjmx(j)
            e.tensor_sub(t[0:2, o + 256 : o + 512],
                         t[0:2, o + 256 : o + 512], m[0:2, mo : mo + 256])

    s.on("vector", rvar)
    s.barrier(waiters=("scalar",))

    def rln(e):
        for j in range(NCC):
            t, o = rowslot(j)
            m, mo = rjmx(j)
            e.activation(m[0:2, mo : mo + 256], t[0:2, o + 256 : o + 512],
                         AF.Ln, bias=consts.eps64[0:2, 0:1])

    s.on("scalar", rln)
    s.barrier(waiters=("scalar",))

    def rexp(e):
        for j in range(NCC):
            t, o = rowslot(j)
            m, mo = rjmx(j)
            e.activation(t[0:2, o + 256 : o + 512], m[0:2, mo : mo + 256],
                         AF.Exp, scale=-0.5)

    s.on("scalar", rexp)
    s.barrier(waiters=("vector",))

    s.barrier(waiters=("tensor",))
    for half in range(2):
        def gbc(e, half=half):
            for dj in range(4):
                j = 4 * half + dj
                t, o = rowslot(j)
                bb, bo = 2 + dj // 2, (dj % 2) * 256
                e.matmul(psum[bb][:, bo : bo + 256], consts.ones2_128[:],
                         t[0:2, o : o + 256], start=True, stop=True)
                e.matmul(psum[bb + 2][:, bo : bo + 256], consts.ones2_128[:],
                         t[0:2, o + 256 : o + 512], start=True, stop=True)
                e.matmul(psum[bb + 4][:, bo : bo + 256], consts.ones2_128f[:],
                         bufs.coef_p2[0:2, j * Bt : (j + 1) * Bt],
                         start=True, stop=True)

        s.on("tensor", gbc)
        s.barrier(waiters=("vector",))

        def pj(dj, half):
            j = 4 * half + dj
            return (slice(j * Bt, (j + 1) * Bt),
                    2 + dj // 2, slice((dj % 2) * 256, (dj % 2) * 256 + 256))

        def at1(e, half=half):
            for dj in range(4):
                sl, bb, pc = pj(dj, half)
                e.tensor_sub(bufs.tmp1[:, sl], bufs.y_blk[:, sl],
                             psum[bb][:, pc])

        s.on("vector", at1)
        s.barrier(waiters=("vector",))

        def at2(e, half=half):
            for dj in range(4):
                sl, bb, pc = pj(dj, half)
                e.tensor_mul(bufs.tmp2[:, sl], psum[bb + 4][:, pc],
                             bufs.v_p[:, sl])
                e.tensor_mul(bufs.tmp1[:, sl], bufs.tmp1[:, sl],
                             psum[bb + 2][:, pc])

        s.on("vector", at2)
        s.barrier(waiters=("vector",))

        def at3(e, half=half):
            for dj in range(4):
                sl, bb, pc = pj(dj, half)
                j = 4 * half + dj
                e.scalar_tensor_tensor(
                    bufs.tmp1[:, sl], bufs.tmp1[:, sl].bitcast(F32),
                    bufs.v_ln_w[:, j : j + 1], bufs.tmp2[:, sl].bitcast(F32),
                    ALU.mult, ALU.add)

        s.on("vector", at3)
        s.barrier(waiters=("vector",))

        def at4(e, half=half):
            for dj in range(4):
                sl, bb, pc = pj(dj, half)
                j = 4 * half + dj
                e.scalar_tensor_tensor(
                    bufs.yg[:, sl], bufs.tmp1[:, sl].bitcast(F32),
                    bufs.v_ln_b[:, j : j + 1], bufs.g_p[:, sl],
                    ALU.add, ALU.mult)

        s.on("vector", at4)
        s.barrier(waiters=("tensor",))

    def attev(e, o, region):
        e.activation(bufs.att_s[:, o * Bt : (o + 1) * Bt], region, AF.Copy)

    _, eva = emit_proj_1024(s, psum, bufs.w_o, bufs.yg, Bt, attev)
    s.on("scalar", eva)
    s.barrier()

    emit_ln_stats(s, bufs, psum, consts, bufs.att_s[:], bufs.sqy[:], Bt, consts.eps1)

    def l2a1(e):
        for j in range(NCC):
            e.tensor_sub(bufs.tmp1[:, j * Bt : (j + 1) * Bt],
                         bufs.att_s[:, j * Bt : (j + 1) * Bt],
                         psum[6][:, 0:Bt])

    s.on("vector", l2a1)
    s.barrier(waiters=("vector",))

    def l2a2(e):
        for j in range(NCC):
            e.tensor_mul(bufs.tmp2[:, j * Bt : (j + 1) * Bt],
                         bufs.tmp1[:, j * Bt : (j + 1) * Bt], psum[2][:, 0:Bt])

    s.on("vector", l2a2)
    s.barrier(waiters=("gpsimd",))

    def l2a3(e):
        for j in range(NCC):
            e.tensor_add(bufs.x2[:, j * Bt : (j + 1) * Bt],
                         bufs.tmp2[:, j * Bt : (j + 1) * Bt].bitcast(F32),
                         bufs.x_blk2[:, j * Bt : (j + 1) * Bt].bitcast(F32))

    s.on("gpsimd", l2a3)
    s.barrier()

    emit_ln_stats(s, bufs, psum, consts, bufs.x2[:], bufs.sqy[:], Bt, consts.eps1)
    emit_ln_apply(s, bufs.x2[:].bitcast(F32), bufs.ln3[:], psum, Bt)
    s.barrier(waiters=("tensor",))

    def ftr(e):
        for tt in range(2):
            for i in range(8):
                bank = psum[((4, 5) if tt == 0 else (2, 3))[i // 4]]
                e.transpose(
                    bank[:, (i % 4) * 128 : (i % 4) * 128 + 128].bitcast(F32R),
                    bufs.ln3[:, i * Bt + tt * 128 : i * Bt + tt * 128 + 128],
                    consts.i128_r[:],
                )

    s.on("tensor", ftr)
    s.barrier(waiters=("vector",))

    def fam(e):
        e.reduce_max(bufs.amax[:, 0:1], psum[4][:, 0:512],
                     mybir.AxisListType.X, apply_absolute_value=True)
        e.reduce_max(bufs.amax2[:, 0:1], psum[5][:, 0:512],
                     mybir.AxisListType.X, apply_absolute_value=True)
        e.reduce_max(bufs.amax[:, 1:2], psum[2][:, 0:512],
                     mybir.AxisListType.X, apply_absolute_value=True)
        e.reduce_max(bufs.amax2[:, 1:2], psum[3][:, 0:512],
                     mybir.AxisListType.X, apply_absolute_value=True)

    s.on("vector", fam)
    s.barrier(waiters=("vector",))

    def fam2(e):
        for tt in range(2):
            e.tensor_max(bufs.amax[:, tt : tt + 1],
                         bufs.amax2[:, tt : tt + 1],
                         bufs.amax[:, tt : tt + 1])

    s.on("vector", fam2)
    s.barrier(waiters=("tensor",))

    def atr(e):
        for tt in range(2):
            e.transpose(psum[4][0:1, tt * 128 : tt * 128 + 128].bitcast(F32R),
                        bufs.amax[:, tt : tt + 1], consts.i128_r[:])

    s.on("tensor", atr)
    s.barrier(waiters=("vector",))

    def arec2(e):
        e.tensor_copy(bufs.sclrow[0:1, 0:Bt], psum[4][0:1, 0:Bt])
        with nc.allow_low_precision(reason="f32r row scratch, full 32-bit"):
            e.reciprocal(bufs.invrow[0:1, 0:Bt],
                         psum[4][0:1, 0:Bt].bitcast(F32))

    s.on("vector", arec2)
    s.barrier(waiters=("vector",))

    def arow3(e):
        e.tensor_scalar_mul(bufs.invrow[0:1, 0:Bt], bufs.invrow[0:1, 0:Bt],
                            127.0)
        e.tensor_scalar_mul(bufs.sclrow[0:1, 0:Bt], bufs.sclrow[0:1, 0:Bt],
                            1.0 / 127.0)

    s.on("vector", arow3)
    s.barrier(waiters=("tensor",))

    def qbc(e):
        e.matmul(psum[2][:, 0:Bt], consts.onesrow_r[:], bufs.invrow[0:1, 0:Bt],
                 start=True, stop=True)
        e.matmul(psum[6][:, 0:Bt], consts.onesrow_r[:], bufs.sclrow[0:1, 0:Bt],
                 start=True, stop=True)

    s.on("tensor", qbc)
    s.barrier(waiters=("vector",))

    def xq1(e):
        for j in range(NCC):
            e.tensor_mul(bufs.tmp1[:, j * Bt : (j + 1) * Bt],
                         bufs.ln3[:, j * Bt : (j + 1) * Bt].bitcast(F32),
                         psum[2][:, 0:Bt])

    s.on("vector", xq1)
    s.barrier(waiters=("vector",))

    def xq2(e):
        for j in range(NCC):
            e.tensor_scalar(
                bufs.xq[:, j * Bt : (j + 1) * Bt],
                bufs.tmp1[:, j * Bt : (j + 1) * Bt],
                MAGIC, -MAGIC, ALU.add, ALU.add,
            )

    s.on("vector", xq2)
    s.barrier(waiters=("tensor",))

    def ffev(e, o, region):
        e.activation(bufs.tmp1[:, o * Bt : (o + 1) * Bt], region, AF.Copy,
                     scale=bufs.v_wq_scale[:, o : o + 1])

    _, evf = emit_proj_1024(s, psum, bufs.wq_t, bufs.xq, Bt, ffev,
                            banks=(7, 3, 4, 5))
    s.on("scalar", evf)
    s.barrier()

    def ffs(e):
        for j in range(NCC):
            e.tensor_mul(bufs.tmp2[:, j * Bt : (j + 1) * Bt],
                         bufs.tmp1[:, j * Bt : (j + 1) * Bt], psum[6][:, 0:Bt])

    s.on("vector", ffs)
    s.barrier(waiters=("vector",))

    def ffb(e):
        for j in range(NCC):
            e.tensor_scalar(
                bufs.ffo[:, j * Bt : (j + 1) * Bt],
                bufs.tmp2[:, j * Bt : (j + 1) * Bt],
                bufs.v_ffn_bias[:, j : j + 1], None, ALU.add,
            )

    s.on("vector", ffb)
    s.barrier()

    emit_ln_stats(s, bufs, psum, consts, bufs.ffo[:], bufs.sqy[:], Bt, consts.eps1)

    def l4a1(e):
        for j in range(NCC):
            e.tensor_sub(bufs.tmp1[:, j * Bt : (j + 1) * Bt],
                         bufs.ffo[:, j * Bt : (j + 1) * Bt], psum[6][:, 0:Bt])

    s.on("vector", l4a1)
    s.barrier(waiters=("vector",))

    def l4a2(e):
        for j in range(NCC):
            e.tensor_mul(bufs.tmp2[:, j * Bt : (j + 1) * Bt],
                         bufs.tmp1[:, j * Bt : (j + 1) * Bt], psum[2][:, 0:Bt])

    s.on("vector", l4a2)
    s.barrier(waiters=("gpsimd",))

    def l4a3(e):
        for j in range(NCC):
            e.tensor_add(bufs.x3[:, j * Bt : (j + 1) * Bt],
                         bufs.tmp2[:, j * Bt : (j + 1) * Bt].bitcast(F32),
                         bufs.x2[:, j * Bt : (j + 1) * Bt].bitcast(F32))

    s.on("gpsimd", l4a3)
    s.barrier()

    s.dma("sync", "ostore",
          io["out"].rearrange("(j p) t -> p j t", p=128)[:, :, bB : bB + Bt],
          bufs.x3[:].rearrange("p (j t) -> p j t", j=NCC))


def build_core_program(T, Bt, ncores=8):
    nc = bass.Bass("TRN2", target_bir_lowering=False)
    io = declare_io(nc, T, ncores)
    s = Sched(nc)
    consts = FullConsts(nc, s)
    consts.declare_inputs(nc)
    bufs = FullBufs(nc, Bt, T)
    s.dummy = bufs.dummy
    NBLK = T // Bt

    with nc.Block() as block:
        emit_weight_load(s, nc, bufs, consts, io)
        for b in range(NBLK):
            emit_pass1_block(s, nc, bufs, consts, io, T, Bt, b)
        emit_exchange(s, nc, bufs, consts, io)
        emit_prep2(s, nc, bufs, consts, io)
        for b in range(NBLK):
            emit_pass2_block(s, nc, bufs, consts, io, T, Bt, b)
        s.barrier()
        s.finalize(block)
    return nc, io


from concourse.bass_utils import run_bass_kernel_spmd

_T_FULL = 16384
_NCORES = 8
_T_CORE = _T_FULL // _NCORES
_BT = 256
_CACHE = {}


def _fmvec(v):
    return np.ascontiguousarray(np.asarray(v, np.float32).reshape(8, 128).T)


def _get_program():
    if "nc" not in _CACHE:
        _CACHE["nc"] = build_core_program(_T_CORE, _BT, _NCORES)
    return _CACHE["nc"]


def kernel(**inputs):
    nc, io = _get_program()
    x = np.asarray(inputs["L0_attn_x"], np.float32)[0]
    xT = np.ascontiguousarray(x.T)
    carr = full_const_arrays()
    w2 = np.asarray(inputs["L0_attn_time_maa_w2"], np.float32)

    shared = {
        "W_r": np.asarray(inputs["L0_attn_W_r"], np.float32),
        "W_k": np.asarray(inputs["L0_attn_W_k"], np.float32),
        "W_v": np.asarray(inputs["L0_attn_W_v"], np.float32),
        "W_o": np.asarray(inputs["L0_attn_W_o"], np.float32),
        "W_ffn": np.asarray(inputs["L0_ffn_weight"], np.float32),
        "maa_w1_d": np.asarray(inputs["L0_attn_time_maa_w1"], np.float32),
        "maa_w2a_d": np.ascontiguousarray(w2[0:4].reshape(128, 1024)),
        "maa_w2b_d": np.ascontiguousarray(w2[4:6].reshape(64, 1024)),
        "dec_w1_d": np.asarray(inputs["L0_attn_time_decay_w1"], np.float32),
        "dec_w2_d": np.asarray(inputs["L0_attn_time_decay_w2"], np.float32),
        "aaa_w1_d": np.asarray(inputs["L0_attn_time_aaa_w1"], np.float32),
        "aaa_w2_d": np.asarray(inputs["L0_attn_time_aaa_w2"], np.float32),
        "kkk_w1_d": np.asarray(inputs["L0_attn_time_kkk_w1"], np.float32),
        "kkk_w2_d": np.asarray(inputs["L0_attn_time_kkk_w2"], np.float32),
        "gate_w1_d": np.asarray(inputs["L0_attn_gate_w1"], np.float32),
        "gate_w2_d": np.asarray(inputs["L0_attn_gate_w2"], np.float32),
        "v_maa_x_d": _fmvec(np.asarray(inputs["L0_attn_time_maa_x"]).ravel()),
        "v_maa_r_d": _fmvec(np.asarray(inputs["L0_attn_time_maa_r"]).ravel()),
        "v_maa_w_d": _fmvec(np.asarray(inputs["L0_attn_time_maa_w"]).ravel()),
        "v_maa_k_d": _fmvec(np.asarray(inputs["L0_attn_time_maa_k"]).ravel()),
        "v_maa_v_d": _fmvec(np.asarray(inputs["L0_attn_time_maa_v"]).ravel()),
        "v_maa_a_d": _fmvec(np.asarray(inputs["L0_attn_time_maa_a"]).ravel()),
        "v_maa_g_d": _fmvec(np.asarray(inputs["L0_attn_time_maa_g"]).ravel()),
        "v_td_d": _fmvec(np.asarray(inputs["L0_attn_time_decay"]).ravel()),
        "v_aaaaa_d": _fmvec(np.asarray(inputs["L0_attn_time_aaaaa"]).ravel()),
        "v_faaaa_d": _fmvec(np.asarray(inputs["L0_attn_time_faaaa"]).ravel()),
        "v_ln_w_d": _fmvec(np.asarray(inputs["L0_attn_ln_w"]).ravel()),
        "v_ln_b_d": _fmvec(np.asarray(inputs["L0_attn_ln_b"]).ravel()),
        "v_ffn_bias_d": _fmvec(np.asarray(inputs["L0_ffn_bias"]).ravel()),
        "c_mask_g1": carr["mask_g1"], "c_mask_sl": carr["mask_sl"],
        "c_i64b_f": carr["i64b"], "c_zeros1": carr["zeros1"],
        "c_ones_col": carr["ones_col"], "c_ones_col2": carr["ones_col2"],
        "c_onesrow": carr["onesrow"], "c_negrow": carr["negrow"],
        "c_ones2_128": carr["ones2_128"], "c_i128": carr["i128"],
        "c_eps1": carr["eps1"], "c_eps64": carr["eps64"],
        "c_mask_g2": carr["mask_g2"], "c_mask_sl8": carr["mask_sl8"],
    }

    maps = []
    for c in range(_NCORES):
        t0 = c * _T_CORE
        xh = np.zeros((1024, _T_CORE + 1), np.float32)
        if t0 > 0:
            xh[:, 0] = xT[:, t0 - 1]
        xh[:, 1:] = xT[:, t0 : t0 + _T_CORE]
        onehot = np.zeros((128, 8), np.float32)
        onehot[:, c] = 1.0
        m = dict(shared)
        m["x_in"] = xh
        m["onehot_d"] = onehot
        maps.append(m)

    res = run_bass_kernel_spmd(nc, maps, core_ids=list(range(_NCORES)))
    outs = [np.asarray(res.results[c]["x_out"]) for c in range(_NCORES)]
    got = np.concatenate([o.T for o in outs], axis=0)
    return np.ascontiguousarray(got, dtype=np.float32)[None]
